# version 03
# baseline (speedup 1.0000x reference)
import numpy as np
import concourse.bass as bass
import concourse.mybir as mybir
from concourse.bass_utils import run_bass_kernel_spmd

F32 = mybir.dt.float32
BF16 = mybir.dt.bfloat16
I32 = mybir.dt.int32
AF = mybir.ActivationFunctionType
ALU = mybir.AluOpType
AX = mybir.AxisListType

D = 1024
FF = 4096
T = 2048
N = 256
DC = D // 128
FC = FF // 128
NKC = N // 128
NTT = T // 512
EPS = 1e-5
CLAMP = 1e6

T1 = 16
NCH = T // T1
T2 = 8
NG = NCH // T2

KB = 1024
ENGS = ("sync", "tensor", "scalar", "vector", "gpsimd")


class Sem:
    def __init__(self, nc, name):
        self._cm = nc.semaphore(name)
        self.h = self._cm.__enter__()
        self.n = 0


class Phase:
    def __init__(self, nc, name=None):
        self.nc = nc
        self.name = name
        self.ops = {e: [] for e in ENGS}

    def op(self, eng, fn, waits=(), inc=None, k=1):
        self.ops[eng].append((fn, [(s.h, c) for (s, c) in waits if c > 0], inc, k))
        if inc is not None:
            inc.n += k
            return inc.n
        return None

    def emit(self):
        import contextlib
        scope = (self.nc.named_scope(self.name) if self.name
                 else contextlib.nullcontext())
        with scope, self.nc.Block() as blk:
            for name in ENGS:
                items = self.ops[name]
                if not items:
                    continue
                section = getattr(blk, name)

                @section
                def _(eng, items=items):
                    for fn, waits, inc, k in items:
                        for (h, c) in waits:
                            eng.wait_ge(h, c)
                        instr = fn(eng)
                        if inc is not None:
                            instr.then_inc(inc.h, k)


def build_program(dbg=False, stop_after=None, skip_ladder=False):
    nc = bass.Bass("TRN2", target_bir_lowering=False)

    x_f = nc.dram_tensor("x_f", [128, DC * T], F32, kind="ExternalInput").ap()
    at_m = nc.dram_tensor("at_m", [128, NKC * N], F32, kind="ExternalInput").ap()
    a_m = nc.dram_tensor("a_m", [128, NKC * N], F32, kind="ExternalInput").ap()
    bt_m = nc.dram_tensor("bt_m", [128, DC * N], F32, kind="ExternalInput").ap()
    ct_m = nc.dram_tensor("ct_m", [128, NKC * D], F32, kind="ExternalInput").ap()
    gam_m = nc.dram_tensor("gam_m", [128, DC], F32, kind="ExternalInput").ap()
    bet_m = nc.dram_tensor("bet_m", [128, DC], F32, kind="ExternalInput").ap()
    w1t_m = nc.dram_tensor("w1t_m", [128, DC * FF], F32, kind="ExternalInput").ap()
    b1_m = nc.dram_tensor("b1_m", [128, FC], F32, kind="ExternalInput").ap()
    w2t_m = nc.dram_tensor("w2t_m", [128, FC * D], F32, kind="ExternalInput").ap()
    b2_m = nc.dram_tensor("b2_m", [128, DC], F32, kind="ExternalInput").ap()
    idf_m = nc.dram_tensor("idf_m", [128, 128], F32, kind="ExternalInput").ap()
    idb_m = nc.dram_tensor("idb_m", [128, 128], BF16, kind="ExternalInput").ap()

    out_m = nc.dram_tensor("out", [128, DC * T], F32, kind="ExternalOutput").ap()
    x1s_t = nc.dram_tensor("x1s", [128, DC * T], F32,
                           kind="ExternalOutput" if dbg else "Internal")
    x1s = x1s_t.ap()
    q2s_t = nc.dram_tensor("q2s", [128, FC * T], BF16,
                           kind="ExternalOutput" if dbg else "Internal")
    c2row_t = nc.dram_tensor("c2row", [1, T], F32,
                             kind="ExternalOutput" if dbg else "Internal")
    if dbg:
        udump = nc.dram_tensor("udump", [128, NKC * T], F32, kind="ExternalOutput").ap()
        sdump = nc.dram_tensor("sdump", [128, NKC * T], BF16, kind="ExternalOutput").ap()
        statdump = nc.dram_tensor("statdump", [128, 8 * 512], F32, kind="ExternalOutput").ap()
        sqdump = nc.dram_tensor("sqdump", [128, DC * 512], BF16, kind="ExternalOutput").ap()
        xbdump = nc.dram_tensor("xbdump", [128, DC * T], BF16, kind="ExternalOutput").ap()
        xfdump = nc.dram_tensor("xfdump", [128, DC * T], F32, kind="ExternalOutput").ap()
        gdump = nc.dram_tensor("gdump", [128, FC * 512], BF16, kind="ExternalOutput").ap()
        wqdump = nc.dram_tensor("wqdump", [128, 16], F32, kind="ExternalOutput").ap()
        w1tdump = nc.dram_tensor("w1tdump", [128, DC * FF], BF16, kind="ExternalOutput").ap()
        q1dump = nc.dram_tensor("q1dump", [128, DC * 512], BF16, kind="ExternalOutput").ap()
        w2tdump = nc.dram_tensor("w2tdump", [128, FC * D], BF16, kind="ExternalOutput").ap()
        o2dump = nc.dram_tensor("o2dump", [128, DC * 512], F32, kind="ExternalOutput").ap()

    A0, B0, C0, D0 = 18 * KB, 82 * KB, 146 * KB, 194 * KB

    def at_(name, shape, dt, off):
        return nc.alloc_sbuf_tensor_at(name, shape, dt, offset=off)

    xf = at_("xf", [128, DC * T], F32, A0)
    w1t = at_("w1t", [128, DC * FF], BF16, A0)
    o2st = at_("o2st", [128, DC * 512], F32, A0)
    outst = at_("outst", [128, DC * 512], F32, A0 + 16 * KB)
    x1bb_f2 = at_("x1bb_f2", [128, DC * 512], BF16, A0 + 32 * KB)
    sq8_f2 = at_("sq8_f2", [128, DC * 512], BF16, A0 + 40 * KB)
    scr1_f2 = at_("scr1_f2", [128, 2 * 512], F32, A0 + 48 * KB)
    p0stg = at_("p0stg", [128, 6144], F32, B0)
    xb = at_("xb", [128, DC * T], BF16, B0)
    sq8_a = at_("sq8_a", [128, DC * 512], BF16, B0 + 32 * KB)
    lad = at_("lad", [128, 16 * 512], BF16, B0 + 40 * KB)
    ppow2 = at_("ppow2", [128, T2 * NKC * N], BF16, B0 + 40 * KB)
    act_t = at_("act_t", [128, DC * 512], F32, B0 + 40 * KB)
    btb = at_("btb", [128, DC * N], BF16, B0 + 56 * KB)
    ctb = at_("ctb", [128, NKC * D], BF16, B0 + 60 * KB)
    absdump = at_("absdump", [128, 4096], BF16, B0)
    g_t = at_("g_t", [128, FC * 512], BF16, B0)
    w2t = at_("w2t", [128, FC * D], BF16, B0)
    u_t = at_("u_t", [128, NKC * T], F32, C0)
    sloc = at_("sloc", [128, NKC * T1 * 128], BF16, C0 + 16 * KB)
    states = at_("states", [128, NKC * T], BF16, C0 + 24 * KB)
    ppow = at_("ppow", [128, T1 * NKC * N], BF16, C0 + 32 * KB)
    x1bb_p5 = at_("x1bb_p5", [128, DC * 512], BF16, C0)
    wchunk = at_("wchunk", [128, 4096], F32, C0)
    wchunk2 = at_("wchunk2", [128, 4096], F32, C0 + 32 * KB)
    wcint = at_("wcint", [128, 4096], I32, C0 + 16 * KB)
    q1b = at_("q1b", [128, DC * 512], BF16, C0)
    x1bb_f1 = at_("x1bb_f1", [128, DC * 512], BF16, C0)
    scr1_f1 = at_("scr1_f1", [128, 2 * 512], F32, C0 + 8 * KB)
    scri_f1 = at_("scri_f1", [128, 2 * 512], I32, C0 + 8 * KB)
    sq8_f1 = at_("sq8_f1", [128, DC * 512], BF16, C0 + 16 * KB)
    x1t = at_("x1t", [128, DC * 512], F32, C0 + 32 * KB)
    x1t2 = at_("x1t2", [128, DC * 512], F32, B0 + 32 * KB)
    q2st = at_("q2st", [128, 8 * 512], BF16, C0 + 24 * KB)
    q2in = at_("q2in", [128, FC * 512], BF16, C0)
    dptr = D0

    def dal(name, shape, dt, nbytes, share=None):
        nonlocal dptr
        off = share if share is not None else dptr
        t = at_(name, shape, dt, off)
        if share is None:
            dptr += nbytes
        return t

    ones128b = dal("ones128b", [128, 128], BF16, 256)
    onescb = dal("onescb", [128, 1], BF16, 32)
    onesc = dal("onesc", [128, 1], F32, 32)
    ones_r = dal("ones_r", [1, 128], F32, 512)
    idf = dal("idf", [128, 128], F32, 512)
    idb = dal("idb", [128, 128], BF16, 256)
    gam = dal("gam", [128, DC], F32, 32)
    bet = dal("bet", [128, DC], F32, 32)
    b1c = dal("b1c", [128, FC], F32, 128)
    b2c = dal("b2c", [128, DC], F32, 32)
    rowbb = dal("rowbb", [1, N], BF16, 512)
    nmrow = dal("nmrow", [1, 512], BF16, 1024)
    NSTAT = 8
    stat = dal("stat", [128, NSTAT * 512], F32, NSTAT * 2 * KB)
    scrb = dal("scrb", [128, 2 * 512], BF16, 2 * KB)
    rmax = dal("rmax", [128, 512], BF16, KB)
    mcol = dal("mcol", [128, 4], F32, 32)
    mrow = dal("mrow", [1, 512], F32, 2 * KB)
    acc8 = dal("acc8", [128, 8], F32, 32)
    mu1pp = dal("mu1pp", [128, 1], F32, 32)
    mu2pp = dal("mu2pp", [128, 1], F32, 32)
    invmu1 = dal("invmu1", [128, 1], F32, 32)
    invmu2 = dal("invmu2", [128, 1], F32, 32)
    epspp = dal("epspp", [128, 1], F32, 32)
    acc8b = dal("acc8b", [128, 8], BF16, 32)
    ones128f = dal("ones128f", [128, 128], F32, 512)
    sqacc = dal("sqacc", [128, 512], F32, 2 * KB)
    g0col = dal("g0col", [128, NKC * (NG + 1)], BF16, 96)
    y2 = dal("y2", [128, NKC * T2 * NG], BF16, NKC * T2 * NG * 2)
    e_t = dal("e_t", [128, NKC * NCH], BF16, NKC * NCH * 2)
    assert dptr <= 228 * KB, f"D region overflow: ends at {dptr}"
    nc.sbuf_base = dptr

    def st(i, sl=slice(0, 512)):
        return stat[:, i * 512:(i + 1) * 512][:, sl]

    pbk = [nc.alloc_psum_tensor(f"pb{i}", [128, 512], F32) for i in range(7)]
    pT = nc.alloc_psum_tensor("pT", [128, 512], BF16)

    LD = Sem(nc, "LD")
    MM = Sem(nc, "MM")
    VV = Sem(nc, "VV")
    SS = Sem(nc, "SS")
    PP = Sem(nc, "PP")
    STO = Sem(nc, "STO")

    def xfc(k, sl=slice(0, T)):
        return xf[:, k * T:(k + 1) * T][:, sl]

    def xbc(k, sl=slice(0, T)):
        return xb[:, k * T:(k + 1) * T][:, sl]

    def att(k):
        return act_t[:, k * 512:(k + 1) * 512]

    def uc(n, sl):
        return u_t[:, n * T:(n + 1) * T][:, sl]

    def stcv(n, sl):
        return states[:, n * T:(n + 1) * T][:, sl]

    def slocv(n, i, sl=slice(0, 128)):
        base = n * T1 * 128 + i * 128
        return sloc[:, base:base + 128][:, sl]

    def pw(i, kc):
        base = (i - 1) * NKC * N + kc * N
        return ppow[:, base:base + N]

    def pw2(j, kc):
        base = (j - 1) * NKC * N + kc * N
        return ppow2[:, base:base + N]

    def ladv(j, rc):
        base = (j - 1) * 512 + rc * 256
        return lad[:, base:base + 256]

    def btbc(k):
        return btb[:, k * N:(k + 1) * N]

    def ctbc(kc):
        return ctb[:, kc * D:(kc + 1) * D]

    def gtc(fc):
        return g_t[:, fc * 512:(fc + 1) * 512]

    def x1tc(k):
        return x1t[:, k * 512:(k + 1) * 512]

    def q1bc(k):
        return q1b[:, k * 512:(k + 1) * 512]

    def q2inc(fc):
        return q2in[:, fc * 512:(fc + 1) * 512]

    def ev(n, sl):
        return e_t[:, n * NCH:(n + 1) * NCH][:, sl]

    def gcol(n, g):
        return g0col[:, n * (NG + 1) + g:n * (NG + 1) + g + 1]

    def y2v(n, j, sl=slice(0, NG)):
        base = n * T2 * NG + (j - 1) * NG
        return y2[:, base:base + NG][:, sl]

    ph = Phase(nc, "p0_load")
    ph.op("sync", lambda sy: sy.dma_start(out=xf[:], in_=x_f), inc=LD, k=16)
    ph.op("sync", lambda sy: sy.dma_start(out=p0stg[:, 0:512], in_=at_m), inc=LD, k=16)
    ph.op("sync", lambda sy: sy.dma_start(out=p0stg[:, 512:1024], in_=a_m), inc=LD, k=16)
    ph.op("sync", lambda sy: sy.dma_start(out=p0stg[:, 1024:3072], in_=bt_m), inc=LD, k=16)
    ph.op("sync", lambda sy: sy.dma_start(out=p0stg[:, 3072:5120], in_=ct_m), inc=LD, k=16)
    ph.op("sync", lambda sy: sy.dma_start(out=idf[:], in_=idf_m), inc=LD, k=16)
    ph.op("sync", lambda sy: sy.dma_start(out=idb[:], in_=idb_m), inc=LD, k=16)
    ph.op("sync", lambda sy: sy.dma_start(out=gam[:], in_=gam_m), inc=LD, k=16)
    ph.op("sync", lambda sy: sy.dma_start(out=bet[:], in_=bet_m), inc=LD, k=16)
    ph.op("sync", lambda sy: sy.dma_start(out=b1c[:], in_=b1_m), inc=LD, k=16)
    ph.op("sync", lambda sy: sy.dma_start(out=b2c[:], in_=b2_m), inc=LD, k=16)
    ldall = LD.n

    def _consts(vv):
        vv.memset(ones128b[:], 1.0)
        vv.memset(onescb[:], 1.0)
        vv.memset(onesc[:], 1.0)
        vv.memset(ones_r[:], 1.0)
        vv.memset(ones128f[:], 1.0)
        vv.memset(epspp[:], EPS)
        for kc in range(NKC):
            vv.tensor_copy(pw(1, kc), p0stg[:, kc * N:(kc + 1) * N])
            vv.tensor_copy(ladv(1, kc), p0stg[:, 512 + kc * N:512 + (kc + 1) * N])
            vv.tensor_copy(ctbc(kc), p0stg[:, 3072 + kc * D:3072 + (kc + 1) * D])
        for k in range(DC):
            vv.tensor_copy(btbc(k), p0stg[:, 1024 + k * N:1024 + (k + 1) * N])
        last = None
        for k in range(DC):
            last = vv.tensor_copy(xbc(k), xfc(k))
        return last
    ph.op("vector", _consts, waits=[(LD, ldall)], inc=VV)
    ph.emit()

    ph = Phase(nc, "p1_ladder")

    def emit_ladder_stage(ph, c, pwf, rslot, cnt_target):
        if skip_ladder:
            return
        njs = min(c, cnt_target - c)
        for j in range(1, njs + 1):
            nj = c + j

            def mmf(tt, j=j, c=c, pwf=pwf, rslot=rslot):
                last = None
                for mc in range(2):
                    for kc in range(NKC):
                        last = tt.matmul(pbk[4][:, mc * 256:(mc + 1) * 256],
                                         pwf(c, kc)[:, mc * 128:(mc + 1) * 128],
                                         rslot(j, kc),
                                         start=(kc == 0), stop=(kc == NKC - 1))
                return last
            m1 = ph.op("tensor", mmf, waits=[(VV, VV.n)], inc=MM)

            def evf(vv, nj=nj, rslot=rslot):
                vv.tensor_copy(rslot(nj, 0), pbk[4][:, 0:256])
                return vv.tensor_copy(rslot(nj, 1), pbk[4][:, 256:512])
            e1 = ph.op("vector", evf, waits=[(MM, m1)], inc=VV)

            def tpf(tt, nj=nj, rslot=rslot):
                last = None
                for rc in range(2):
                    for sc in range(2):
                        last = tt.transpose(
                            pT[:, (2 * rc + sc) * 128:(2 * rc + sc + 1) * 128],
                            rslot(nj, rc)[:, sc * 128:(sc + 1) * 128], idb[:])
                return last
            m2 = ph.op("tensor", tpf, waits=[(VV, e1)], inc=MM)

            def pvf(vv, nj=nj, pwf=pwf):
                last = None
                for rc in range(2):
                    for sc in range(2):
                        last = vv.tensor_copy(
                            pwf(nj, sc)[:, rc * 128:(rc + 1) * 128],
                            pT[:, (2 * rc + sc) * 128:(2 * rc + sc + 1) * 128])
                return last
            ph.op("vector", pvf, waits=[(MM, m2)], inc=VV)

    def rslot1(j, kc):
        return ladv(j, kc)

    c = 1
    while c < T1:
        emit_ladder_stage(ph, c, pw, rslot1, T1)
        c *= 2

    def pw2f(i, kc):
        if i == 1:
            return pw(T1, kc)
        return pw2(i, kc)

    def rslot2(j, kc):
        if j == 1:
            return ladv(16, kc)
        return ladv(8 + j, kc)

    def cp16(vv):
        vv.tensor_copy(pw2(1, 0), pw(T1, 0))
        return vv.tensor_copy(pw2(1, 1), pw(T1, 1))
    if not skip_ladder:
        ph.op("vector", cp16, waits=[(MM, MM.n)], inc=VV)

    c = 1
    while c < T2:
        emit_ladder_stage(ph, c, pw2f, rslot2, T2)
        c *= 2
    ph.emit()

    ph = Phase(nc, "p1_rowb")

    def rbmm(tt):
        last = None
        for k in range(DC):
            last = tt.matmul(pbk[6][0:1, 0:N], onescb[:], btbc(k),
                             start=(k == 0), stop=(k == DC - 1))
        return last
    m = ph.op("tensor", rbmm, waits=[(VV, VV.n)], inc=MM)
    ph.op("vector", lambda vv: vv.tensor_copy(rowbb[:], pbk[6][0:1, 0:N]),
          waits=[(MM, m)], inc=VV)
    ph.emit()

    S_MEAN, S_VAR, S_RSTD, S_NEGM, S_SCR, S_A, S_B, S_C = range(8)

    def emit_stats(ph, src_b16, src_f32, sqbuf, want_negm_row=False):
        def sqf(ss):
            last = None
            for k in range(DC):
                last = ss.activation(sqbuf[:, k * 512:(k + 1) * 512],
                                     src_f32(k), AF.Square)
            return last
        s1 = ph.op("scalar", sqf, waits=[(VV, VV.n), (MM, MM.n)], inc=SS)

        def summ(tt):
            last = None
            for k in range(DC):
                last = tt.matmul(pbk[0][:], ones128b[:], src_b16(k),
                                 start=(k == 0), stop=(k == DC - 1))
            return last
        ph.op("tensor", summ, waits=[(VV, VV.n), (SS, SS.n)], inc=MM)

        def sqm(tt):
            last = None
            for k in range(DC):
                last = tt.matmul(pbk[1][:], ones128b[:],
                                 sqbuf[:, k * 512:(k + 1) * 512],
                                 start=(k == 0), stop=(k == DC - 1))
            return last
        m2 = ph.op("tensor", sqm, waits=[(SS, s1)], inc=MM)

        def soup1(vv):
            vv.tensor_scalar_mul(st(S_MEAN), pbk[0][:], 1.0 / D)
            vv.tensor_scalar_mul(st(S_VAR), pbk[1][:], 1.0 / D)
            vv.tensor_mul(st(S_SCR), st(S_MEAN), st(S_MEAN))
            vv.tensor_sub(st(S_VAR), st(S_VAR), st(S_SCR))
            return vv.tensor_scalar_mul(st(S_NEGM), st(S_MEAN), -1.0)
        v1 = ph.op("vector", soup1, waits=[(MM, m2)], inc=VV)
        s2 = ph.op("scalar",
                   lambda ss: ss.activation(st(S_SCR), st(S_VAR), AF.Sqrt, bias=epspp[:, 0:1]),
                   waits=[(VV, v1)], inc=SS)
        v2 = ph.op("vector", lambda vv: vv.reciprocal(st(S_RSTD), st(S_SCR)),
                   waits=[(SS, s2)], inc=VV)
        if want_negm_row:
            v2 = ph.op("vector",
                       lambda vv: vv.tensor_copy(nmrow[:], st(S_NEGM)[0:1, :]),
                       inc=VV)
        return v2

    if stop_after == "sq0":
        sqtest = nc.alloc_sbuf_tensor("sqtest", [128, 1024], BF16)
        ph = Phase(nc)
        def sq0f(ss):
            ss.activation(sq8_a[:, 0:512], xfc(0, slice(0, 512)), AF.Square)
            ss.activation(sqtest[:, 0:512], xfc(0, slice(0, 512)), AF.Square)
            return ss.activation(sqtest[:, 512:1024], scr1_f1[:, 0:512], AF.Square)
        def cpf(vv):
            return vv.tensor_copy(scr1_f1[:, 0:512], xfc(0, slice(0, 512)))
        v0 = ph.op("vector", cpf, waits=[(VV, VV.n)], inc=VV)
        s0 = ph.op("scalar", sq0f, waits=[(VV, v0)], inc=SS)
        ph.op("sync", lambda sy: sy.dma_start(out=sqdump, in_=sq8_a[:]),
              waits=[(SS, s0)], inc=STO, k=16)
        ph.op("sync", lambda sy: sy.dma_start(out=xbdump[:, 0:1024], in_=sqtest[:]),
              waits=[(SS, s0)], inc=STO, k=16)
        ph.op("sync", lambda sy: sy.dma_start(out=xfdump[:, 0:512], in_=scr1_f1[:, 0:512]),
              waits=[(SS, s0)], inc=STO, k=16)
        ph.emit()
        return nc, _emit_finish(nc, STO)

    ph = Phase(nc, "p2_lnub")
    for tt_i in range(NTT):
        cs = slice(tt_i * 512, (tt_i + 1) * 512)
        vstat = emit_stats(ph,
                           (lambda k, cs=cs: xbc(k, cs)),
                           (lambda k, cs=cs: xfc(k, cs)),
                           sq8_a, want_negm_row=True)

        def ubmm(tt, cs=cs):
            last = None
            for n_ in range(NKC):
                for k in range(DC):
                    tt.matmul(pbk[2 + n_][:], btbc(k)[:, n_ * 128:(n_ + 1) * 128],
                              xbc(k, cs), start=(k == 0), stop=False)
                last = tt.matmul(pbk[2 + n_][:], rowbb[0:1, n_ * 128:(n_ + 1) * 128],
                                 nmrow[:], start=False, stop=True)
            return last
        m3 = ph.op("tensor", ubmm, waits=[(VV, vstat)], inc=MM)

        def ubev(vv, cs=cs):
            vv.tensor_mul(uc(0, cs), pbk[2][:], st(S_RSTD))
            return vv.tensor_mul(uc(1, cs), pbk[3][:], st(S_RSTD))
        ph.op("vector", ubev, waits=[(MM, m3)], inc=VV)
    ph.emit()
    if dbg:
        ph = Phase(nc)
        ph.op("sync", lambda sy: sy.dma_start(out=udump, in_=u_t[:]),
              waits=[(VV, VV.n)], inc=STO, k=16)
        ph.op("sync", lambda sy: sy.dma_start(out=statdump, in_=stat[:]),
              inc=STO, k=16)
        ph.op("sync", lambda sy: sy.dma_start(out=sqdump, in_=sq8_a[:]),
              inc=STO, k=16)
        ph.op("sync", lambda sy: sy.dma_start(out=xbdump, in_=xb[:]),
              inc=STO, k=16)
        ph.op("sync", lambda sy: sy.dma_start(out=xfdump, in_=xf[:]),
              inc=STO, k=16)
        ph.emit()
    if stop_after == "p2":
        return nc, _emit_finish(nc, STO)

    ph = Phase(nc, "p3_scan")

    def s1f(vv):
        vv.tensor_copy(slocv(0, 0), uc(0, slice(0, T, T1)))
        return vv.tensor_copy(slocv(1, 0), uc(1, slice(0, T, T1)))
    vprev = ph.op("vector", s1f, waits=[(VV, VV.n)], inc=VV)

    for i in range(2, T1 + 1):
        bank = pbk[i % 2]

        def stepmm(tt, i=i, bank=bank):
            last = None
            for n_ in range(NKC):
                for kc in range(NKC):
                    last = tt.matmul(bank[:, n_ * 128:(n_ + 1) * 128],
                                     pw(1, kc)[:, n_ * 128:(n_ + 1) * 128],
                                     slocv(kc, i - 2),
                                     start=(kc == 0), stop=(kc == NKC - 1))
            return last
        m = ph.op("tensor", stepmm, waits=[(VV, vprev)], inc=MM)

        def stepev(vv, i=i, bank=bank):
            vv.tensor_add(slocv(0, i - 1), bank[:, 0:128], uc(0, slice(i - 1, T, T1)))
            return vv.tensor_add(slocv(1, i - 1), bank[:, 128:256],
                                 uc(1, slice(i - 1, T, T1)))
        vprev = ph.op("vector", stepev, waits=[(MM, m)], inc=VV)

    def y1f(vv):
        vv.tensor_copy(y2v(0, 1), slocv(0, T1 - 1, slice(0, 128, T2)))
        return vv.tensor_copy(y2v(1, 1), slocv(1, T1 - 1, slice(0, 128, T2)))
    vprev = ph.op("vector", y1f, inc=VV)
    for j in range(2, T2 + 1):
        bank = pbk[j % 2]

        def l2mm(tt, j=j, bank=bank):
            last = None
            for n_ in range(NKC):
                for kc in range(NKC):
                    last = tt.matmul(bank[:, n_ * 128:n_ * 128 + NG],
                                     pw2(1, kc)[:, n_ * 128:(n_ + 1) * 128],
                                     y2v(kc, j - 1),
                                     start=(kc == 0), stop=(kc == NKC - 1))
            return last
        m = ph.op("tensor", l2mm, waits=[(VV, vprev)], inc=MM)

        def l2ev(vv, j=j, bank=bank):
            vv.tensor_add(y2v(0, j), bank[:, 0:NG],
                          slocv(0, T1 - 1, slice(j - 1, 128, T2)))
            return vv.tensor_add(y2v(1, j), bank[:, 128:128 + NG],
                                 slocv(1, T1 - 1, slice(j - 1, 128, T2)))
        vprev = ph.op("vector", l2ev, waits=[(MM, m)], inc=VV)

    def gz(vv):
        vv.memset(gcol(0, 0), 0.0)
        return vv.memset(gcol(1, 0), 0.0)
    vg = ph.op("vector", gz, inc=VV)
    for g in range(NG):
        bank = pbk[g % 2]

        def l3mm(tt, g=g, bank=bank):
            last = None
            for n_ in range(NKC):
                for kc in range(NKC):
                    last = tt.matmul(bank[:, n_ * 128:n_ * 128 + 1],
                                     pw2(T2, kc)[:, n_ * 128:(n_ + 1) * 128],
                                     gcol(kc, g),
                                     start=(kc == 0), stop=(kc == NKC - 1))
            return last
        m = ph.op("tensor", l3mm, waits=[(VV, vg)], inc=MM)

        def l3ev(vv, g=g, bank=bank):
            vv.tensor_add(gcol(0, g + 1), bank[:, 0:1], y2v(0, T2, slice(g, g + 1)))
            return vv.tensor_add(gcol(1, g + 1), bank[:, 128:129],
                                 y2v(1, T2, slice(g, g + 1)))
        vg = ph.op("vector", l3ev, waits=[(MM, m)], inc=VV)
    vprev = vg

    def ecopyf(vv):
        vv.memset(ev(0, slice(0, 1)), 0.0)
        vv.memset(ev(1, slice(0, 1)), 0.0)
        vv.tensor_copy(ev(0, slice(T2, NCH, T2)),
                       g0col[:, 1:NG])
        return vv.tensor_copy(ev(1, slice(T2, NCH, T2)),
                              g0col[:, (NG + 1) + 1:(NG + 1) + NG])
    vprev3 = ph.op("vector", ecopyf, waits=[(VV, vprev)], inc=VV)
    for j in range(1, T2):
        bank = pbk[2 + (j % 2)]

        def c2mm(tt, j=j, bank=bank):
            last = None
            for n_ in range(NKC):
                for kc in range(NKC):
                    last = tt.matmul(bank[:, n_ * 128:n_ * 128 + NG],
                                     pw2(j, kc)[:, n_ * 128:(n_ + 1) * 128],
                                     g0col[:, kc * (NG + 1):kc * (NG + 1) + NG],
                                     start=(kc == 0), stop=(kc == NKC - 1))
            return last
        m = ph.op("tensor", c2mm, waits=[(VV, vprev3)], inc=MM)

        def c2ev(vv, j=j, bank=bank):
            vv.tensor_add(ev(0, slice(j, NCH, T2)), bank[:, 0:NG], y2v(0, j))
            return vv.tensor_add(ev(1, slice(j, NCH, T2)), bank[:, 128:128 + NG],
                                 y2v(1, j))
        vprev3 = ph.op("vector", c2ev, waits=[(MM, m)], inc=VV)

    vfin = vprev3
    for i in range(1, T1 + 1):
        bank = pbk[4 + (i % 2)]

        def fmm(tt, i=i, bank=bank):
            last = None
            for n_ in range(NKC):
                for kc in range(NKC):
                    last = tt.matmul(bank[:, n_ * 128:(n_ + 1) * 128],
                                     pw(i, kc)[:, n_ * 128:(n_ + 1) * 128],
                                     ev(kc, slice(0, NCH)),
                                     start=(kc == 0), stop=(kc == NKC - 1))
            return last
        m = ph.op("tensor", fmm, waits=[(VV, vfin)], inc=MM)

        def fev(vv, i=i, bank=bank):
            vv.tensor_add(stcv(0, slice(i - 1, T, T1)), bank[:, 0:128],
                          slocv(0, i - 1))
            return vv.tensor_add(stcv(1, slice(i - 1, T, T1)), bank[:, 128:256],
                                 slocv(1, i - 1))
        vfin = ph.op("vector", fev, waits=[(MM, m)], inc=VV)
    ph.emit()
    if dbg:
        ph = Phase(nc)
        ph.op("sync", lambda sy: sy.dma_start(out=sdump, in_=states[:]),
              waits=[(VV, VV.n)], inc=STO, k=16)
        ph.emit()
    if stop_after == "scan":
        return nc, _emit_finish(nc, STO)

    spill_sto = []
    ph = Phase(nc, "p4_epi")
    for tt_i in range(NTT):
        cs = slice(tt_i * 512, (tt_i + 1) * 512)
        sgel = []
        for dc in range(DC):
            bank = pbk[dc % 3]
            w = [(VV, VV.n)] if dc < 3 else [(SS, sgel[dc - 3])]

            def ymm(tt, dc=dc, cs=cs, bank=bank):
                last = None
                for kc in range(NKC):
                    last = tt.matmul(bank[:], ctbc(kc)[:, dc * 128:(dc + 1) * 128],
                                     stcv(kc, cs), start=(kc == 0),
                                     stop=(kc == NKC - 1))
                return last
            m = ph.op("tensor", ymm, waits=w, inc=MM)
            sgel.append(ph.op("scalar",
                              lambda ss, dc=dc, bank=bank:
                              ss.activation(att(dc), bank[:], AF.Gelu),
                              waits=[(MM, m)], inc=SS))

        def abf(vv):
            last = None
            for k in range(DC):
                last = vv.tensor_copy(x1bb_p5[:, k * 512:(k + 1) * 512], att(k))
            return last
        ph.op("vector", abf, waits=[(SS, sgel[-1])], inc=VV)
        vstat = emit_stats(ph, lambda k: x1bb_p5[:, k * 512:(k + 1) * 512],
                           lambda k: att(k), sq8_a)

        vap = None
        for k in range(DC):
            def ap1(vv, k=k):
                vv.tensor_add(att(k), att(k), st(S_NEGM))
                return vv.tensor_mul(att(k), att(k), st(S_RSTD))
            v1 = ph.op("vector", ap1, waits=[(VV, vstat)], inc=VV)
            s1 = ph.op("scalar",
                       lambda ss, k=k: ss.activation(att(k), att(k), AF.Identity,
                                                     bias=bet[:, k:k + 1],
                                                     scale=gam[:, k:k + 1]),
                       waits=[(VV, v1)], inc=SS)
            vap = ph.op("vector",
                        lambda vv, k=k: vv.tensor_scalar(att(k), att(k),
                                                         CLAMP, -CLAMP,
                                                         ALU.min, ALU.max),
                        waits=[(SS, s1)], inc=VV)

        def abf2(vv):
            last = None
            for k in range(DC):
                last = vv.tensor_copy(x1bb_p5[:, k * 512:(k + 1) * 512], att(k))
            return last
        ph.op("vector", abf2, waits=[(VV, vap)], inc=VV)
        vstat2 = emit_stats(ph, lambda k: x1bb_p5[:, k * 512:(k + 1) * 512],
                            lambda k: att(k), sq8_a)

        vr = None
        for k in range(DC):
            def ap2(vv, k=k, cs=cs):
                vv.tensor_add(att(k), att(k), st(S_NEGM))
                vv.tensor_mul(att(k), att(k), st(S_RSTD))
                return vv.tensor_add(xfc(k, cs), att(k), xfc(k, cs))
            vr = ph.op("vector", ap2, waits=[(VV, vstat2)], inc=VV)

        spill_sto.append(
            ph.op("sync",
                  lambda sy, cs=cs: sy.dma_start(
                      out=x1s.rearrange("p (k t) -> p k t", k=DC)[:, :, cs],
                      in_=xf.ap().rearrange("p (k t) -> p k t", k=DC)[:, :, cs]),
                  waits=[(VV, vr)], inc=STO, k=16))
    ph.emit()
    if stop_after == "attn":
        return nc, _emit_finish(nc, STO)

    def emit_wq_pass1(src_m, mu_pp, inv_pp):
        ph = Phase(nc, "wq_p1")
        svals = []
        for ch in range(8):
            buf = wchunk if ch % 2 == 0 else wchunk2
            w = [(SS, svals[ch - 2])] if ch >= 2 else [(STO, STO.n)]
            l = ph.op("sync",
                      lambda sy, ch=ch, buf=buf: sy.dma_start(
                          out=buf[:], in_=src_m[:, ch * 4096:(ch + 1) * 4096]),
                      waits=w, inc=LD, k=16)
            svals.append(ph.op("scalar",
                               lambda ss, ch=ch, buf=buf: ss.activation(
                                   absdump[:], buf[:], AF.Abs,
                                   accum_out=acc8[:, ch:ch + 1]),
                               waits=[(LD, l)], inc=SS))
        v1 = ph.op("vector",
                   lambda vv: vv.tensor_copy(acc8b[:], acc8[:]),
                   waits=[(SS, svals[-1])], inc=VV)
        m1 = ph.op("tensor",
                   lambda tt: tt.matmul(pbk[6][:, 0:8], ones128b[:], acc8b[:],
                                        start=True, stop=True),
                   waits=[(VV, v1)], inc=MM)

        v2 = ph.op("vector",
                   lambda vv: vv.reduce_sum(inv_pp[:], pbk[6][:, 0:8], axis=AX.X),
                   waits=[(MM, m1)], inc=VV)
        v3 = ph.op("vector",
                   lambda vv: vv.tensor_scalar(mu_pp[:], inv_pp[:],
                                               1.0 / (1024.0 * 4096.0), 1e-5,
                                               ALU.mult, ALU.max),
                   waits=[(VV, v2)], inc=VV)
        ph.op("vector", lambda vv: vv.reciprocal(inv_pp[:], mu_pp[:]),
              waits=[(VV, v3)], inc=VV)
        ph.emit()

    def emit_wq_pass2(src_m, dst_cols, inv_pp, first_waits=()):
        ph = Phase(nc, "wq_p2")
        qvals = []
        for ch in range(8):
            buf2 = wchunk if ch % 2 == 0 else wchunk2
            w = [(VV, qvals[ch - 2])] if ch >= 2 else list(first_waits) + [(VV, VV.n)]
            l = ph.op("sync",
                      lambda sy, ch=ch, buf2=buf2: sy.dma_start(
                          out=buf2[:], in_=src_m[:, ch * 4096:(ch + 1) * 4096]),
                      waits=w, inc=LD, k=16)

            def qf(vv, ch=ch, buf2=buf2):
                vv.tensor_scalar_mul(wcint[:], buf2[:], inv_pp[:, 0:1])
                return vv.tensor_scalar(dst_cols(ch), wcint[:], 1.0, -1.0,
                                        ALU.min, ALU.max)
            qvals.append(ph.op("vector", qf, waits=[(LD, l)], inc=VV))
        ph.emit()

    emit_wq_pass1(w1t_m, mu1pp, invmu1)
    emit_wq_pass2(w1t_m, lambda ch: w1t[:, ch * 4096:(ch + 1) * 4096], invmu1,
                  first_waits=[(STO, spill_sto[-1])])
    emit_wq_pass1(w2t_m, mu2pp, invmu2)
    if stop_after == "wq":
        return nc, _emit_finish(nc, STO)

    q2_sto = []
    ph = Phase(nc, "f1")
    x1_lds = []
    x1_lastuse = []
    gdma = None
    for tt_i in range(NTT):
        cs = slice(tt_i * 512, (tt_i + 1) * 512)
        xbuf = x1t if tt_i % 2 == 0 else x1t2

        def x1tc_f1(k, xbuf=xbuf):
            return xbuf[:, k * 512:(k + 1) * 512]

        wld = [(STO, spill_sto[tt_i])]
        if tt_i >= 2:
            wld.append((VV, x1_lastuse[tt_i - 2]))
        l = ph.op("sync",
                  lambda sy, cs=cs, xbuf=xbuf: sy.dma_start(
                      out=xbuf[:],
                      in_=x1s.rearrange("p (k t) -> p k t", k=DC)[:, :, cs]),
                  waits=wld, inc=LD, k=16)
        x1_lds.append(l)

        def xbf(ss, x1tc=x1tc_f1):
            last = None
            for k in range(DC):
                last = ss.copy(x1bb_f1[:, k * 512:(k + 1) * 512], x1tc(k))
            return last
        sxb = ph.op("scalar", xbf, waits=[(LD, l)], inc=SS)
        emit_stats(ph, lambda k, x1tc=x1tc_f1: x1bb_f1[:, k * 512:(k + 1) * 512] if False else x1bb_f1[:, k * 512:(k + 1) * 512],
                   lambda k, x1tc=x1tc_f1: x1tc(k), sq8_f1)

        scen = []
        for k in range(DC):
            sl = scr1_f1[:, (k % 2) * 512:((k % 2) + 1) * 512]
            slb = scrb[:, (k % 2) * 512:((k % 2) + 1) * 512]
            w = [(SS, scen[k - 2])] if k >= 2 else [(VV, VV.n)]
            v = ph.op("vector",
                      lambda vv, k=k, sl=sl, x1tc=x1tc_f1:
                      vv.tensor_add(sl, x1tc(k), st(S_NEGM)),
                      waits=w, inc=VV)
            scen.append(ph.op("scalar",
                              lambda ss, sl=sl, slb=slb:
                              ss.activation(slb, sl, AF.Square),
                              waits=[(VV, v)], inc=SS))
            if k == 0:
                ph.op("vector", lambda vv: vv.tensor_copy(rmax[:], scrb[:, 0:512]),
                      waits=[(SS, scen[0])], inc=VV)
            else:
                ph.op("vector",
                      lambda vv, slb=slb: vv.tensor_max(rmax[:], rmax[:], slb),
                      waits=[(SS, scen[k])], inc=VV)
        vrm = VV.n

        def amax_chain(tag):
            def tpf(tt):
                last = None
                for j in range(4):
                    last = tt.transpose(pT[:, j * 128:(j + 1) * 128],
                                        rmax[:, j * 128:(j + 1) * 128], idb[:])
                return last
            m = ph.op("tensor", tpf, waits=[(VV, VV.n)], inc=MM)

            def redf(vv):
                last = None
                for j in range(4):
                    last = vv.reduce_max(mcol[:, j:j + 1],
                                         pT[:, j * 128:(j + 1) * 128], axis=AX.X)
                return last
            v = ph.op("vector", redf, waits=[(MM, m)], inc=VV)

            def mtpf(tt):
                last = None
                for j in range(4):
                    last = tt.transpose(pbk[6][0:1, j * 128:(j + 1) * 128],
                                        mcol[:, j:j + 1], idf[:])
                return last
            m = ph.op("tensor", mtpf, waits=[(VV, v)], inc=MM)
            v = ph.op("vector",
                      lambda vv: vv.tensor_copy(mrow[:], pbk[6][0:1, 0:512]),
                      waits=[(MM, m)], inc=VV)
            m = ph.op("tensor",
                      lambda tt: tt.matmul(pbk[5][:], ones_r[:], mrow[:],
                                           start=True, stop=True),
                      waits=[(VV, v)], inc=MM)
            return ph.op("scalar",
                         lambda ss: ss.activation(st(S_A), pbk[5][:], AF.Sqrt),
                         waits=[(MM, m)], inc=SS)

        s = amax_chain("a1")

        def soup2(vv):
            vv.tensor_mul(st(S_SCR), st(S_VAR), st(S_RSTD))
            return vv.tensor_mul(st(S_SCR), st(S_SCR), st(S_RSTD))
        v = ph.op("vector", soup2, waits=[(SS, s)], inc=VV)
        s = ph.op("scalar",
                  lambda ss: ss.activation(st(S_B), st(S_SCR), AF.Sqrt,
                                           bias=epspp[:, 0:1]),
                  waits=[(VV, v)], inc=SS)

        def soup3(vv):
            vv.reciprocal(st(S_SCR), st(S_B))
            vv.tensor_mul(st(S_A), st(S_A), st(S_RSTD))
            vv.tensor_mul(st(S_A), st(S_A), st(S_SCR))
            vv.tensor_scalar_max(st(S_A), st(S_A), 1e-5)
            vv.tensor_scalar(st(S_C), st(S_A), mu1pp[:, 0:1], 1.0 / 127.0,
                             ALU.mult, ALU.mult)
            vv.reciprocal(st(S_B), st(S_A))
            vv.tensor_scalar_mul(st(S_B), st(S_B), 127.0)
            vv.tensor_mul(st(S_B), st(S_B), st(S_SCR))
            vv.tensor_mul(st(S_B), st(S_B), st(S_RSTD))
            return vv.tensor_mul(st(S_SCR), st(S_NEGM), st(S_B))
        v = ph.op("vector", soup3, waits=[(SS, s)], inc=VV)

        for k in range(DC):
            def q1f(vv, k=k, x1tc=x1tc_f1):
                vv.tensor_mul(scr1_f1[:, 0:512], x1tc(k), st(S_B))
                vv.tensor_tensor(scri_f1[:, 512:1024], scr1_f1[:, 0:512],
                                 st(S_SCR), ALU.add)
                return vv.tensor_mul(q1bc(k), scri_f1[:, 512:1024], st(S_C))
            v = ph.op("vector", q1f, inc=VV)
        vq1 = v
        x1_lastuse.append(vq1)

        sgel = []
        ssqs = []
        vmaxs = []
        gmms = []
        for fc in range(FC):
            bank = pbk[fc % 4]
            w = [(VV, vq1)]
            if fc >= 4:
                w.append((SS, ssqs[fc - 4]))

            def m1f(tt, fc=fc, bank=bank):
                last = None
                for k in range(DC):
                    last = tt.matmul(
                        bank[:], w1t[:, k * FF + fc * 128:k * FF + (fc + 1) * 128],
                        q1bc(k), start=(k == 0), stop=(k == DC - 1))
                return last
            m = ph.op("tensor", m1f, waits=w, inc=MM)
            sg = ph.op("scalar",
                       lambda ss, fc=fc, bank=bank: ss.activation(
                           gtc(fc), bank[:], AF.Gelu, bias=b1c[:, fc:fc + 1]),
                       waits=[(MM, m)], inc=SS)
            sgel.append(sg)
            slb = sq8_f1[:, (fc % 8) * 512:((fc % 8) + 1) * 512]
            wsq = []
            if fc >= 8:
                wsq = [(MM, gmms[fc // 8 - 1]), (VV, vmaxs[fc - 8])]
            ssq = ph.op("scalar",
                        lambda ss, fc=fc, slb=slb:
                        ss.activation(slb, gtc(fc), AF.Square),
                        waits=wsq, inc=SS)
            ssqs.append(ssq)
            if fc == 0:
                vg = ph.op("vector",
                           lambda vv, slb=slb: vv.tensor_copy(rmax[:], slb),
                           waits=[(SS, ssq)], inc=VV)
            else:
                vg = ph.op("vector",
                           lambda vv, slb=slb: vv.tensor_max(rmax[:], rmax[:], slb),
                           waits=[(SS, ssq)], inc=VV)
            vmaxs.append(vg)
            if fc % 8 == 7:
                def gmf(tt, fc=fc):
                    grp = fc // 8
                    last = None
                    for j in range(8):
                        last = tt.matmul(pbk[4][:], ones128b[:],
                                         sq8_f1[:, j * 512:(j + 1) * 512],
                                         start=(grp == 0 and j == 0),
                                         stop=(grp == 3 and j == 7))
                    return last
                gmms.append(ph.op("tensor", gmf, waits=[(SS, ssq)], inc=MM))

        v = ph.op("vector",
                  lambda vv: vv.tensor_scalar_mul(st(S_VAR), pbk[4][:], 1.0 / FF),
                  waits=[(MM, gmms[-1])], inc=VV)
        s_r2 = ph.op("scalar",
                     lambda ss: ss.activation(st(S_B), st(S_VAR), AF.Sqrt,
                                              bias=epspp[:, 0:1]),
                     waits=[(VV, v)], inc=SS)
        s = amax_chain("a2")

        def soup4(vv):
            vv.reciprocal(st(S_SCR), st(S_B))
            vv.tensor_mul(st(S_A), st(S_A), st(S_SCR))
            vv.tensor_scalar_max(st(S_A), st(S_A), 1e-5)
            vv.tensor_scalar(st(S_C), st(S_A), mu2pp[:, 0:1], 1.0 / 127.0,
                             ALU.mult, ALU.mult)
            vv.reciprocal(st(S_B), st(S_A))
            vv.tensor_scalar_mul(st(S_B), st(S_B), 127.0)
            return vv.tensor_mul(st(S_B), st(S_B), st(S_SCR))
        v = ph.op("vector", soup4, waits=[(SS, s), (SS, s_r2)], inc=VV)
        c2sto = ph.op("sync",
                      lambda sy, cs=cs: sy.dma_start(out=c2row_t.ap()[0:1, cs],
                                                     in_=st(S_C)[0:1, :]),
                      waits=[(VV, v)], inc=STO, k=16)
        q2_sto.append(c2sto)

        pp_hist = []
        if gdma is None:
            gdma = c2sto
        for fcg in range(4):
            for jj in range(8):
                fc = fcg * 8 + jj
                si = scri_f1[:, (fc % 2) * 512:((fc % 2) + 1) * 512]
                w = [(VV, v)] if fc < 2 else [(PP, pp_hist[fc - 2])]
                vv_ = ph.op("vector",
                            lambda vv, fc=fc, si=si:
                            vv.tensor_mul(si, gtc(fc), st(S_B)),
                            waits=w, inc=VV)
                wp = [(VV, vv_)]
                if jj == 0:
                    wp.append((STO, gdma))
                pp_hist.append(ph.op("gpsimd",
                                     lambda pp, jj=jj, si=si: pp.tensor_copy(
                                         q2st[:, jj * 512:(jj + 1) * 512], si),
                                     waits=wp, inc=PP))
            gdma = ph.op("sync",
                         lambda sy, fcg=fcg, cs=cs: sy.dma_start(
                             out=bass.AP(q2s_t, 8 * fcg * T + cs.start,
                                         [[FC * T, 128], [T, 8], [1, 512]]),
                             in_=q2st.ap().rearrange("p (c t) -> p c t", c=8)),
                         waits=[(PP, pp_hist[-1])], inc=STO, k=16)
            q2_sto.append(gdma)
    ph.emit()

    if dbg:
        ph = Phase(nc)
        ph.op("sync", lambda sy: sy.dma_start(out=gdump, in_=g_t[:]),
              waits=[(VV, VV.n), (SS, SS.n), (PP, PP.n)], inc=STO, k=16)
        ph.op("sync", lambda sy: sy.dma_start(out=q1dump, in_=q1b[:]),
              inc=STO, k=16)
        ph.op("sync", lambda sy: sy.dma_start(out=wqdump[:, 0:8], in_=acc8[:]),
              inc=STO, k=16)
        ph.op("sync", lambda sy: sy.dma_start(out=wqdump[:, 8:12], in_=mcol[:]),
              inc=STO, k=16)
        ph.op("sync", lambda sy: sy.dma_start(out=wqdump[:, 12:13], in_=mu1pp[:], allow_slow_non_contiguous=True),
              inc=STO, k=16)
        ph.op("sync", lambda sy: sy.dma_start(out=wqdump[:, 13:14], in_=mu2pp[:], allow_slow_non_contiguous=True),
              inc=STO, k=16)
        ph.op("sync", lambda sy: sy.dma_start(out=wqdump[:, 14:15], in_=invmu1[:], allow_slow_non_contiguous=True),
              inc=STO, k=16)
        ph.op("sync", lambda sy: sy.dma_start(out=wqdump[:, 15:16], in_=invmu2[:], allow_slow_non_contiguous=True),
              inc=STO, k=16)
        ph.op("sync", lambda sy: sy.dma_start(out=w1tdump, in_=w1t[:]),
              inc=STO, k=16)
        ph.op("sync", lambda sy: sy.dma_start(out=statdump, in_=stat[:]),
              inc=STO, k=16)
        ph.emit()
    if stop_after == "f1":
        return nc, _emit_finish(nc, STO)

    emit_wq_pass2(w2t_m, lambda ch: w2t[:, ch * 4096:(ch + 1) * 4096], invmu2)
    q2_done = STO.n

    ph = Phase(nc, "f2")
    prev_mm_last = None
    prev_sev_last = None
    prev_vr = None
    prev_outdma = None
    prev_evac_last = None
    for tt_i in range(NTT):
        cs = slice(tt_i * 512, (tt_i + 1) * 512)
        wq2 = [(STO, q2_done)]
        if prev_mm_last is not None:
            wq2 = [(MM, prev_mm_last)]
        l = ph.op("sync",
                  lambda sy, cs=cs: sy.dma_start(
                      out=q2in[:].rearrange("p (c t) -> p c t", c=FC),
                      in_=bass.AP(q2s_t, cs.start, [[FC * T, 128], [T, FC], [1, 512]])),
                  waits=wq2, inc=LD, k=16)
        wx1 = [(STO, q2_done)] if prev_vr is None else [(VV, prev_vr)]
        l2 = ph.op("sync",
                   lambda sy, cs=cs: sy.dma_start(
                       out=x1t[:],
                       in_=x1s.rearrange("p (k t) -> p k t", k=DC)[:, :, cs]),
                   waits=wx1, inc=LD, k=16)
        wc2 = [(STO, q2_done)] if prev_evac_last is None else [(VV, prev_evac_last)]
        l3 = ph.op("sync",
                   lambda sy, cs=cs: sy.dma_start(
                       out=st(S_C),
                       in_=bass.AP(c2row_t, cs.start, [[0, 128], [1, 512]])),
                   waits=wc2, inc=LD, k=16)

        sev = []
        vev = []
        for dc in range(DC):
            bank = pbk[dc % 3]
            w = [(LD, l)]
            if dc >= 3:
                w.append((SS, sev[dc - 3]))
            elif prev_sev_last is not None:
                w.append((SS, prev_sev_last))

            def m2f(tt, dc=dc, bank=bank):
                last = None
                for fc in range(FC):
                    last = tt.matmul(
                        bank[:], w2t[:, fc * D + dc * 128:fc * D + (dc + 1) * 128],
                        q2inc(fc), start=(fc == 0), stop=(fc == FC - 1))
                return last
            m = ph.op("tensor", m2f, waits=w, inc=MM)
            if dc == DC - 1:
                prev_mm_last = m
            slo = scr1_f2[:, (dc % 2) * 512:((dc % 2) + 1) * 512]
            wv = [(MM, m), (LD, l3)]
            if dc >= 2:
                wv.append((SS, sev[dc - 2]))
            v = ph.op("vector",
                      lambda vv, bank=bank, slo=slo: vv.tensor_mul(slo, bank[:],
                                                                   st(S_C)),
                      waits=wv, inc=VV)
            vev.append(v)
            sev.append(ph.op("scalar",
                             lambda ss, dc=dc, slo=slo: ss.activation(
                                 o2st[:, dc * 512:(dc + 1) * 512], slo,
                                 AF.Identity, bias=b2c[:, dc:dc + 1]),
                             waits=[(VV, v)], inc=SS))
        prev_sev_last = sev[-1]
        prev_evac_last = vev[-1]

        def abf3(vv):
            last = None
            for k in range(DC):
                last = vv.tensor_copy(x1bb_f2[:, k * 512:(k + 1) * 512],
                                      o2st[:, k * 512:(k + 1) * 512])
            return last
        ph.op("vector", abf3, waits=[(SS, sev[-1])], inc=VV)
        emit_stats(ph, lambda k: x1bb_f2[:, k * 512:(k + 1) * 512],
                   lambda k: o2st[:, k * 512:(k + 1) * 512], sq8_f2)

        vr = None
        for k in range(DC):
            wap = [(LD, l2), (VV, VV.n)]
            if k == 0 and prev_outdma is not None:
                wap.append((STO, prev_outdma))

            def ap3(vv, k=k):
                vv.tensor_add(scr1_f2[:, 0:512], o2st[:, k * 512:(k + 1) * 512],
                              st(S_NEGM))
                vv.tensor_mul(scr1_f2[:, 0:512], scr1_f2[:, 0:512], st(S_RSTD))
                return vv.tensor_add(outst[:, k * 512:(k + 1) * 512],
                                     scr1_f2[:, 0:512], x1tc(k))
            vr = ph.op("vector", ap3, waits=wap, inc=VV)
        prev_vr = vr
        prev_outdma = ph.op("sync",
                            lambda sy, cs=cs: sy.dma_start(
                                out=out_m.rearrange("p (k t) -> p k t", k=DC)[:, :, cs],
                                in_=outst.ap().rearrange("p (k t) -> p k t", k=DC)),
                            waits=[(VV, vr)], inc=STO, k=16)
    ph.emit()
    return nc, _emit_finish(nc, STO)


def _emit_finish(nc, STO):
    with nc.Block() as blk:
        @blk.sync
        def _(sy):
            sy.wait_ge(STO.h, STO.n)
    return STO.n


def _tile_rows(a, p=128):
    a = np.ascontiguousarray(a)
    R, C = a.shape
    return np.ascontiguousarray(
        a.reshape(R // p, p, C).transpose(1, 0, 2).reshape(p, (R // p) * C))


def make_inputs(x, A, B, C, gam, bet, w1, b1, w2, b2):
    import ml_dtypes
    shared = {
        "at_m": _tile_rows(A.T),
        "a_m": _tile_rows(A),
        "bt_m": _tile_rows(B.T),
        "ct_m": _tile_rows(C.T),
        "gam_m": _tile_rows(gam.reshape(D, 1)),
        "bet_m": _tile_rows(bet.reshape(D, 1)),
        "w1t_m": _tile_rows(w1.T),
        "b1_m": _tile_rows(b1.reshape(FF, 1)),
        "w2t_m": _tile_rows(w2.T),
        "b2_m": _tile_rows(b2.reshape(D, 1)),
        "idf_m": np.eye(128, dtype=np.float32),
        "idb_m": np.eye(128, dtype=ml_dtypes.bfloat16),
    }
    maps = []
    for b in range(x.shape[0]):
        m = dict(shared)
        m["x_f"] = _tile_rows(x[b].T)
        maps.append(m)
    return maps


def kernel(L0_attn_x, L0_attn_A, L0_attn_B, L0_attn_C, L0_attn_gamma,
           L0_attn_beta, L0_ffn_w1, L0_ffn_b1, L0_ffn_w2, L0_ffn_b2):
    x = np.asarray(L0_attn_x, dtype=np.float32)
    in_maps = make_inputs(x,
                          np.asarray(L0_attn_A, np.float32),
                          np.asarray(L0_attn_B, np.float32),
                          np.asarray(L0_attn_C, np.float32),
                          np.asarray(L0_attn_gamma, np.float32),
                          np.asarray(L0_attn_beta, np.float32),
                          np.asarray(L0_ffn_w1, np.float32),
                          np.asarray(L0_ffn_b1, np.float32),
                          np.asarray(L0_ffn_w2, np.float32),
                          np.asarray(L0_ffn_b2, np.float32))
    nc, _ = build_program()
    res = run_bass_kernel_spmd(nc, in_maps, core_ids=list(range(8)))
    outs = []
    for b in range(8):
        o = res.results[b]["out"]
        o = o.reshape(128, DC, T).transpose(1, 0, 2).reshape(D, T)
        outs.append(np.ascontiguousarray(o.T))
    return np.stack(outs, axis=0)


# version 04
# speedup vs baseline: 1.0420x; 1.0420x over previous
import numpy as np
import concourse.bass as bass
import concourse.mybir as mybir
from concourse.bass_utils import run_bass_kernel_spmd

F32 = mybir.dt.float32
BF16 = mybir.dt.bfloat16
I32 = mybir.dt.int32
AF = mybir.ActivationFunctionType
ALU = mybir.AluOpType
AX = mybir.AxisListType

D = 1024
FF = 4096
T = 2048
N = 256
DC = D // 128
FC = FF // 128
NKC = N // 128
NTT = T // 512
EPS = 1e-5
CLAMP = 1e6

T1 = 16
NCH = T // T1
T2 = 8
NG = NCH // T2

KB = 1024
ENGS = ("sync", "tensor", "scalar", "vector", "gpsimd")


class Sem:
    def __init__(self, nc, name):
        self._cm = nc.semaphore(name)
        self.h = self._cm.__enter__()
        self.n = 0


class Phase:
    def __init__(self, nc, name=None):
        self.nc = nc
        self.name = name
        self.ops = {e: [] for e in ENGS}

    def op(self, eng, fn, waits=(), inc=None, k=1):
        self.ops[eng].append((fn, [(s.h, c) for (s, c) in waits if c > 0], inc, k))
        if inc is not None:
            inc.n += k
            return inc.n
        return None

    def emit(self):
        import contextlib
        scope = (self.nc.named_scope(self.name) if self.name
                 else contextlib.nullcontext())
        with scope, self.nc.Block() as blk:
            for name in ENGS:
                items = self.ops[name]
                if not items:
                    continue
                section = getattr(blk, name)

                @section
                def _(eng, items=items):
                    for fn, waits, inc, k in items:
                        for (h, c) in waits:
                            eng.wait_ge(h, c)
                        instr = fn(eng)
                        if inc is not None:
                            instr.then_inc(inc.h, k)


def build_program(dbg=False, stop_after=None, skip_ladder=False):
    nc = bass.Bass("TRN2", target_bir_lowering=False)

    x_f = nc.dram_tensor("x_f", [128, DC * T], F32, kind="ExternalInput").ap()
    at_m = nc.dram_tensor("at_m", [128, NKC * N], F32, kind="ExternalInput").ap()
    a_m = nc.dram_tensor("a_m", [128, NKC * N], F32, kind="ExternalInput").ap()
    bt_m = nc.dram_tensor("bt_m", [128, DC * N], F32, kind="ExternalInput").ap()
    ct_m = nc.dram_tensor("ct_m", [128, NKC * D], F32, kind="ExternalInput").ap()
    gam_m = nc.dram_tensor("gam_m", [128, DC], F32, kind="ExternalInput").ap()
    bet_m = nc.dram_tensor("bet_m", [128, DC], F32, kind="ExternalInput").ap()
    w1t_m = nc.dram_tensor("w1t_m", [128, DC * FF], F32, kind="ExternalInput").ap()
    b1_m = nc.dram_tensor("b1_m", [128, FC], F32, kind="ExternalInput").ap()
    w2t_m = nc.dram_tensor("w2t_m", [128, FC * D], F32, kind="ExternalInput").ap()
    b2_m = nc.dram_tensor("b2_m", [128, DC], F32, kind="ExternalInput").ap()
    idf_m = nc.dram_tensor("idf_m", [128, 128], F32, kind="ExternalInput").ap()
    idb_m = nc.dram_tensor("idb_m", [128, 128], BF16, kind="ExternalInput").ap()

    out_m = nc.dram_tensor("out", [128, DC * T], F32, kind="ExternalOutput").ap()
    x1s_t = nc.dram_tensor("x1s", [128, DC * T], F32,
                           kind="ExternalOutput" if dbg else "Internal")
    x1s = x1s_t.ap()
    q2s_t = nc.dram_tensor("q2s", [128, FC * T], BF16,
                           kind="ExternalOutput" if dbg else "Internal")
    c2row_t = nc.dram_tensor("c2row", [1, T], F32,
                             kind="ExternalOutput" if dbg else "Internal")
    if dbg:
        udump = nc.dram_tensor("udump", [128, NKC * T], F32, kind="ExternalOutput").ap()
        sdump = nc.dram_tensor("sdump", [128, NKC * T], BF16, kind="ExternalOutput").ap()
        statdump = nc.dram_tensor("statdump", [128, 8 * 512], F32, kind="ExternalOutput").ap()
        sqdump = nc.dram_tensor("sqdump", [128, DC * 512], BF16, kind="ExternalOutput").ap()
        xbdump = nc.dram_tensor("xbdump", [128, DC * T], BF16, kind="ExternalOutput").ap()
        xfdump = nc.dram_tensor("xfdump", [128, DC * T], F32, kind="ExternalOutput").ap()
        gdump = nc.dram_tensor("gdump", [128, FC * 512], BF16, kind="ExternalOutput").ap()
        wqdump = nc.dram_tensor("wqdump", [128, 16], F32, kind="ExternalOutput").ap()
        w1tdump = nc.dram_tensor("w1tdump", [128, DC * FF], BF16, kind="ExternalOutput").ap()
        q1dump = nc.dram_tensor("q1dump", [128, DC * 512], BF16, kind="ExternalOutput").ap()
        w2tdump = nc.dram_tensor("w2tdump", [128, FC * D], BF16, kind="ExternalOutput").ap()
        o2dump = nc.dram_tensor("o2dump", [128, DC * 512], F32, kind="ExternalOutput").ap()

    A0, B0, C0, D0 = 18 * KB, 82 * KB, 146 * KB, 194 * KB

    def at_(name, shape, dt, off):
        return nc.alloc_sbuf_tensor_at(name, shape, dt, offset=off)

    xf = at_("xf", [128, DC * T], F32, A0)
    w1t = at_("w1t", [128, DC * FF], BF16, A0)
    o2stA = at_("o2stA", [128, DC * 512], BF16, A0)
    o2stB = at_("o2stB", [128, DC * 512], BF16, A0 + 8 * KB)
    outst = at_("outst", [128, DC * 512], F32, A0 + 16 * KB)
    sq8_f2 = at_("sq8_f2", [128, DC * 512], BF16, A0 + 40 * KB)
    scr1_f2 = at_("scr1_f2", [128, 2 * 512], F32, A0 + 48 * KB)
    p0stg = at_("p0stg", [128, 6144], F32, B0)
    xb = at_("xb", [128, DC * T], BF16, B0)
    sq8_a = at_("sq8_a", [128, DC * 512], BF16, B0 + 32 * KB)
    lad = at_("lad", [128, 16 * 512], BF16, B0 + 40 * KB)
    ppow2 = at_("ppow2", [128, T2 * NKC * N], BF16, B0 + 40 * KB)
    act_t = at_("act_t", [128, DC * 512], F32, B0 + 40 * KB)
    btb = at_("btb", [128, DC * N], BF16, B0 + 56 * KB)
    ctb = at_("ctb", [128, NKC * D], BF16, B0 + 60 * KB)
    absdump = at_("absdump", [128, 4096], BF16, B0)
    wqbufA = at_("wqbufA", [128, 2048], F32, B0)
    wqbufB = at_("wqbufB", [128, 2048], F32, B0 + 8 * KB)
    wqabs = at_("wqabs", [128, 2048], BF16, B0 + 16 * KB)
    g_t = at_("g_t", [128, FC * 512], BF16, B0)
    w2t = at_("w2t", [128, FC * D], BF16, B0)
    u_t = at_("u_t", [128, NKC * T], F32, C0)
    sloc = at_("sloc", [128, NKC * T1 * 128], BF16, C0 + 16 * KB)
    states = at_("states", [128, NKC * T], BF16, C0 + 24 * KB)
    ppow = at_("ppow", [128, T1 * NKC * N], BF16, C0 + 32 * KB)
    x1bb_p5 = at_("x1bb_p5", [128, DC * 512], BF16, C0)
    wchunk = at_("wchunk", [128, 4096], F32, C0)
    wchunk2 = at_("wchunk2", [128, 4096], F32, C0 + 32 * KB)
    wcint = at_("wcint", [128, 4096], I32, C0 + 16 * KB)
    q1b = at_("q1b", [128, DC * 512], BF16, C0)
    x1bb_f1 = at_("x1bb_f1", [128, DC * 512], BF16, C0)
    scr1_f1 = at_("scr1_f1", [128, 2 * 512], F32, C0 + 8 * KB)
    scri_f1 = at_("scri_f1", [128, 2 * 512], I32, C0 + 8 * KB)
    sq8_f1 = at_("sq8_f1", [128, DC * 512], BF16, C0 + 16 * KB)
    x1t = at_("x1t", [128, DC * 512], F32, C0 + 32 * KB)
    x1t2 = at_("x1t2", [128, DC * 512], F32, B0 + 32 * KB)
    q2st = at_("q2st", [128, 8 * 512], BF16, C0 + 24 * KB)
    q2in = at_("q2in", [128, FC * 512], BF16, C0)
    dptr = D0

    def dal(name, shape, dt, nbytes, share=None):
        nonlocal dptr
        off = share if share is not None else dptr
        t = at_(name, shape, dt, off)
        if share is None:
            dptr += nbytes
        return t

    ones128b = dal("ones128b", [128, 128], BF16, 256)
    onescb = dal("onescb", [128, 1], BF16, 32)
    onesc = dal("onesc", [128, 1], F32, 32)
    ones_r = dal("ones_r", [1, 128], F32, 512)
    idf = dal("idf", [128, 128], F32, 512)
    idb = dal("idb", [128, 128], BF16, 256)
    gam = dal("gam", [128, DC], F32, 32)
    bet = dal("bet", [128, DC], F32, 32)
    b1c = dal("b1c", [128, FC], F32, 128)
    b2c = dal("b2c", [128, DC], F32, 32)
    rowbb = dal("rowbb", [1, N], BF16, 512)
    nmrow = dal("nmrow", [1, 512], BF16, 1024)
    NSTAT = 8
    stat = dal("stat", [128, NSTAT * 512], F32, NSTAT * 2 * KB)
    scrb = dal("scrb", [128, 2 * 512], BF16, 2 * KB)
    rmax = dal("rmax", [128, 512], BF16, KB)
    mcol = dal("mcol", [128, 4], F32, 32)
    mrow = dal("mrow", [1, 512], F32, 2 * KB)
    acc8 = dal("acc8", [128, 8], F32, 32)
    mu1pp = dal("mu1pp", [128, 1], F32, 32)
    mu2pp = dal("mu2pp", [128, 1], F32, 32)
    invmu1 = dal("invmu1", [128, 1], F32, 32)
    invmu2 = dal("invmu2", [128, 1], F32, 32)
    epspp = dal("epspp", [128, 1], F32, 32)
    acc8b = dal("acc8b", [128, 8], BF16, 32)
    acc16 = dal("acc16", [128, 16], F32, 64)
    acc16b = dal("acc16b", [128, 16], BF16, 64)
    ones128f = dal("ones128f", [128, 128], F32, 512)
    sqacc = dal("sqacc", [128, 512], F32, 2 * KB)
    g0col = dal("g0col", [128, NKC * (NG + 1)], BF16, 96)
    y2 = dal("y2", [128, NKC * T2 * NG], BF16, NKC * T2 * NG * 2)
    e_t = dal("e_t", [128, NKC * NCH], BF16, NKC * NCH * 2)
    assert dptr <= 228 * KB, f"D region overflow: ends at {dptr}"
    nc.sbuf_base = dptr

    def st(i, sl=slice(0, 512)):
        return stat[:, i * 512:(i + 1) * 512][:, sl]

    pbk = [nc.alloc_psum_tensor(f"pb{i}", [128, 512], F32) for i in range(7)]
    pT = nc.alloc_psum_tensor("pT", [128, 512], BF16)

    LD = Sem(nc, "LD")
    MM = Sem(nc, "MM")
    VV = Sem(nc, "VV")
    SS = Sem(nc, "SS")
    PP = Sem(nc, "PP")
    STO = Sem(nc, "STO")

    def xfc(k, sl=slice(0, T)):
        return xf[:, k * T:(k + 1) * T][:, sl]

    def xbc(k, sl=slice(0, T)):
        return xb[:, k * T:(k + 1) * T][:, sl]

    def att(k):
        return act_t[:, k * 512:(k + 1) * 512]

    def uc(n, sl):
        return u_t[:, n * T:(n + 1) * T][:, sl]

    def stcv(n, sl):
        return states[:, n * T:(n + 1) * T][:, sl]

    def slocv(n, i, sl=slice(0, 128)):
        base = n * T1 * 128 + i * 128
        return sloc[:, base:base + 128][:, sl]

    def pw(i, kc):
        base = (i - 1) * NKC * N + kc * N
        return ppow[:, base:base + N]

    def pw2(j, kc):
        base = (j - 1) * NKC * N + kc * N
        return ppow2[:, base:base + N]

    def ladv(j, rc):
        base = (j - 1) * 512 + rc * 256
        return lad[:, base:base + 256]

    def btbc(k):
        return btb[:, k * N:(k + 1) * N]

    def ctbc(kc):
        return ctb[:, kc * D:(kc + 1) * D]

    def gtc(fc):
        return g_t[:, fc * 512:(fc + 1) * 512]

    def x1tc(k):
        return x1t[:, k * 512:(k + 1) * 512]

    def q1bc(k):
        return q1b[:, k * 512:(k + 1) * 512]

    def q2inc(fc):
        return q2in[:, fc * 512:(fc + 1) * 512]

    def ev(n, sl):
        return e_t[:, n * NCH:(n + 1) * NCH][:, sl]

    def gcol(n, g):
        return g0col[:, n * (NG + 1) + g:n * (NG + 1) + g + 1]

    def y2v(n, j, sl=slice(0, NG)):
        base = n * T2 * NG + (j - 1) * NG
        return y2[:, base:base + NG][:, sl]

    ph = Phase(nc, "p0_load")
    ph.op("sync", lambda sy: sy.dma_start(out=xf[:], in_=x_f), inc=LD, k=16)
    ph.op("sync", lambda sy: sy.dma_start(out=p0stg[:, 0:512], in_=at_m), inc=LD, k=16)
    ph.op("sync", lambda sy: sy.dma_start(out=p0stg[:, 512:1024], in_=a_m), inc=LD, k=16)
    ph.op("sync", lambda sy: sy.dma_start(out=p0stg[:, 1024:3072], in_=bt_m), inc=LD, k=16)
    ph.op("sync", lambda sy: sy.dma_start(out=p0stg[:, 3072:5120], in_=ct_m), inc=LD, k=16)
    ph.op("sync", lambda sy: sy.dma_start(out=idf[:], in_=idf_m), inc=LD, k=16)
    ph.op("sync", lambda sy: sy.dma_start(out=idb[:], in_=idb_m), inc=LD, k=16)
    ph.op("sync", lambda sy: sy.dma_start(out=gam[:], in_=gam_m), inc=LD, k=16)
    ph.op("sync", lambda sy: sy.dma_start(out=bet[:], in_=bet_m), inc=LD, k=16)
    ph.op("sync", lambda sy: sy.dma_start(out=b1c[:], in_=b1_m), inc=LD, k=16)
    ph.op("sync", lambda sy: sy.dma_start(out=b2c[:], in_=b2_m), inc=LD, k=16)
    ldall = LD.n

    def _consts(vv):
        vv.memset(ones128b[:], 1.0)
        vv.memset(onescb[:], 1.0)
        vv.memset(onesc[:], 1.0)
        vv.memset(ones_r[:], 1.0)
        vv.memset(ones128f[:], 1.0)
        vv.memset(epspp[:], EPS)
        for kc in range(NKC):
            vv.tensor_copy(pw(1, kc), p0stg[:, kc * N:(kc + 1) * N])
            vv.tensor_copy(ladv(1, kc), p0stg[:, 512 + kc * N:512 + (kc + 1) * N])
            vv.tensor_copy(ctbc(kc), p0stg[:, 3072 + kc * D:3072 + (kc + 1) * D])
        for k in range(DC):
            vv.tensor_copy(btbc(k), p0stg[:, 1024 + k * N:1024 + (k + 1) * N])
        last = None
        for k in range(DC):
            last = vv.tensor_copy(xbc(k), xfc(k))
        return last
    ph.op("vector", _consts, waits=[(LD, ldall)], inc=VV)
    ph.emit()

    ph = Phase(nc, "p1_ladder")

    def emit_ladder_stage(ph, c, pwf, rslot, cnt_target):
        if skip_ladder:
            return
        njs = min(c, cnt_target - c)
        for j in range(1, njs + 1):
            nj = c + j

            def mmf(tt, j=j, c=c, pwf=pwf, rslot=rslot):
                last = None
                for mc in range(2):
                    for kc in range(NKC):
                        last = tt.matmul(pbk[4][:, mc * 256:(mc + 1) * 256],
                                         pwf(c, kc)[:, mc * 128:(mc + 1) * 128],
                                         rslot(j, kc),
                                         start=(kc == 0), stop=(kc == NKC - 1))
                return last
            m1 = ph.op("tensor", mmf, waits=[(VV, VV.n)], inc=MM)

            def evf(vv, nj=nj, rslot=rslot):
                vv.tensor_copy(rslot(nj, 0), pbk[4][:, 0:256])
                return vv.tensor_copy(rslot(nj, 1), pbk[4][:, 256:512])
            e1 = ph.op("vector", evf, waits=[(MM, m1)], inc=VV)

            def tpf(tt, nj=nj, rslot=rslot):
                last = None
                for rc in range(2):
                    for sc in range(2):
                        last = tt.transpose(
                            pT[:, (2 * rc + sc) * 128:(2 * rc + sc + 1) * 128],
                            rslot(nj, rc)[:, sc * 128:(sc + 1) * 128], idb[:])
                return last
            m2 = ph.op("tensor", tpf, waits=[(VV, e1)], inc=MM)

            def pvf(vv, nj=nj, pwf=pwf):
                last = None
                for rc in range(2):
                    for sc in range(2):
                        last = vv.tensor_copy(
                            pwf(nj, sc)[:, rc * 128:(rc + 1) * 128],
                            pT[:, (2 * rc + sc) * 128:(2 * rc + sc + 1) * 128])
                return last
            ph.op("vector", pvf, waits=[(MM, m2)], inc=VV)

    def rslot1(j, kc):
        return ladv(j, kc)

    c = 1
    while c < T1:
        emit_ladder_stage(ph, c, pw, rslot1, T1)
        c *= 2

    def pw2f(i, kc):
        if i == 1:
            return pw(T1, kc)
        return pw2(i, kc)

    def rslot2(j, kc):
        if j == 1:
            return ladv(16, kc)
        return ladv(8 + j, kc)

    def cp16(vv):
        vv.tensor_copy(pw2(1, 0), pw(T1, 0))
        return vv.tensor_copy(pw2(1, 1), pw(T1, 1))
    if not skip_ladder:
        ph.op("vector", cp16, waits=[(MM, MM.n)], inc=VV)

    c = 1
    while c < T2:
        emit_ladder_stage(ph, c, pw2f, rslot2, T2)
        c *= 2
    ph.emit()

    ph = Phase(nc, "p1_rowb")

    def rbmm(tt):
        last = None
        for k in range(DC):
            last = tt.matmul(pbk[6][0:1, 0:N], onescb[:], btbc(k),
                             start=(k == 0), stop=(k == DC - 1))
        return last
    m = ph.op("tensor", rbmm, waits=[(VV, VV.n)], inc=MM)
    ph.op("vector", lambda vv: vv.tensor_copy(rowbb[:], pbk[6][0:1, 0:N]),
          waits=[(MM, m)], inc=VV)
    ph.emit()

    S_MEAN, S_VAR, S_RSTD, S_NEGM, S_SCR, S_A, S_B, S_C = range(8)

    def emit_stats(ph, src_b16, src_f32, sqbuf, want_negm_row=False):
        def sqf(ss):
            last = None
            for k in range(DC):
                last = ss.activation(sqbuf[:, k * 512:(k + 1) * 512],
                                     src_f32(k), AF.Square)
            return last
        s1 = ph.op("scalar", sqf, waits=[(VV, VV.n), (MM, MM.n)], inc=SS)

        def summ(tt):
            last = None
            for k in range(DC):
                last = tt.matmul(pbk[0][:], ones128b[:], src_b16(k),
                                 start=(k == 0), stop=(k == DC - 1))
            return last
        ph.op("tensor", summ, waits=[(VV, VV.n), (SS, SS.n)], inc=MM)

        def sqm(tt):
            last = None
            for k in range(DC):
                last = tt.matmul(pbk[1][:], ones128b[:],
                                 sqbuf[:, k * 512:(k + 1) * 512],
                                 start=(k == 0), stop=(k == DC - 1))
            return last
        m2 = ph.op("tensor", sqm, waits=[(SS, s1)], inc=MM)

        def soup1(vv):
            vv.tensor_scalar_mul(st(S_MEAN), pbk[0][:], 1.0 / D)
            vv.tensor_scalar_mul(st(S_VAR), pbk[1][:], 1.0 / D)
            vv.tensor_mul(st(S_SCR), st(S_MEAN), st(S_MEAN))
            vv.tensor_sub(st(S_VAR), st(S_VAR), st(S_SCR))
            return vv.tensor_scalar_mul(st(S_NEGM), st(S_MEAN), -1.0)
        v1 = ph.op("vector", soup1, waits=[(MM, m2)], inc=VV)
        s2 = ph.op("scalar",
                   lambda ss: ss.activation(st(S_SCR), st(S_VAR), AF.Sqrt, bias=epspp[:, 0:1]),
                   waits=[(VV, v1)], inc=SS)
        v2 = ph.op("vector", lambda vv: vv.reciprocal(st(S_RSTD), st(S_SCR)),
                   waits=[(SS, s2)], inc=VV)
        if want_negm_row:
            v2 = ph.op("vector",
                       lambda vv: vv.tensor_copy(nmrow[:], st(S_NEGM)[0:1, :]),
                       inc=VV)
        return v2

    if stop_after == "sq0":
        sqtest = nc.alloc_sbuf_tensor("sqtest", [128, 1024], BF16)
        ph = Phase(nc)
        def sq0f(ss):
            ss.activation(sq8_a[:, 0:512], xfc(0, slice(0, 512)), AF.Square)
            ss.activation(sqtest[:, 0:512], xfc(0, slice(0, 512)), AF.Square)
            return ss.activation(sqtest[:, 512:1024], scr1_f1[:, 0:512], AF.Square)
        def cpf(vv):
            return vv.tensor_copy(scr1_f1[:, 0:512], xfc(0, slice(0, 512)))
        v0 = ph.op("vector", cpf, waits=[(VV, VV.n)], inc=VV)
        s0 = ph.op("scalar", sq0f, waits=[(VV, v0)], inc=SS)
        ph.op("sync", lambda sy: sy.dma_start(out=sqdump, in_=sq8_a[:]),
              waits=[(SS, s0)], inc=STO, k=16)
        ph.op("sync", lambda sy: sy.dma_start(out=xbdump[:, 0:1024], in_=sqtest[:]),
              waits=[(SS, s0)], inc=STO, k=16)
        ph.op("sync", lambda sy: sy.dma_start(out=xfdump[:, 0:512], in_=scr1_f1[:, 0:512]),
              waits=[(SS, s0)], inc=STO, k=16)
        ph.emit()
        return nc, _emit_finish(nc, STO)

    def make_wq_pass1(ph, src_m, mu_pp, inv_pp):
        state = {"h": 0, "svals": []}

        def step(n):
            for _ in range(n):
                h = state["h"]
                if h >= 16:
                    return
                state["h"] += 1
                buf = wqbufA if h % 2 == 0 else wqbufB
                w = [(SS, state["svals"][h - 2])] if h >= 2 else []
                l = ph.op("sync",
                          lambda sy, h=h, buf=buf: sy.dma_start(
                              out=buf[:], in_=src_m[:, h * 2048:(h + 1) * 2048]),
                          waits=w, inc=LD, k=16)
                state["svals"].append(
                    ph.op("scalar",
                          lambda ss, h=h, buf=buf: ss.activation(
                              wqabs[:], buf[:], AF.Abs,
                              accum_out=acc16[:, h:h + 1]),
                          waits=[(LD, l)], inc=SS))

        def finish():
            step(16)
            v1 = ph.op("vector",
                       lambda vv: vv.tensor_copy(acc16b[:], acc16[:]),
                       waits=[(SS, state["svals"][-1])], inc=VV)
            m1 = ph.op("tensor",
                       lambda tt: tt.matmul(pbk[6][:, 0:16], ones128b[:],
                                            acc16b[:], start=True, stop=True),
                       waits=[(VV, v1)], inc=MM)
            v2 = ph.op("vector",
                       lambda vv: vv.reduce_sum(inv_pp[:], pbk[6][:, 0:16],
                                                axis=AX.X),
                       waits=[(MM, m1)], inc=VV)
            v3 = ph.op("vector",
                       lambda vv: vv.tensor_scalar(mu_pp[:], inv_pp[:],
                                                   1.0 / (1024.0 * 4096.0), 1e-5,
                                                   ALU.mult, ALU.max),
                       waits=[(VV, v2)], inc=VV)
            ph.op("vector", lambda vv: vv.reciprocal(inv_pp[:], mu_pp[:]),
                  waits=[(VV, v3)], inc=VV)

        return step, finish


    ph = Phase(nc, "p2_lnub")
    for tt_i in range(NTT):
        cs = slice(tt_i * 512, (tt_i + 1) * 512)
        vstat = emit_stats(ph,
                           (lambda k, cs=cs: xbc(k, cs)),
                           (lambda k, cs=cs: xfc(k, cs)),
                           sq8_a, want_negm_row=True)

        def ubmm(tt, cs=cs):
            last = None
            for n_ in range(NKC):
                for k in range(DC):
                    tt.matmul(pbk[2 + n_][:], btbc(k)[:, n_ * 128:(n_ + 1) * 128],
                              xbc(k, cs), start=(k == 0), stop=False)
                last = tt.matmul(pbk[2 + n_][:], rowbb[0:1, n_ * 128:(n_ + 1) * 128],
                                 nmrow[:], start=False, stop=True)
            return last
        m3 = ph.op("tensor", ubmm, waits=[(VV, vstat)], inc=MM)

        def ubev(vv, cs=cs):
            vv.tensor_mul(uc(0, cs), pbk[2][:], st(S_RSTD))
            return vv.tensor_mul(uc(1, cs), pbk[3][:], st(S_RSTD))
        ph.op("vector", ubev, waits=[(MM, m3)], inc=VV)
    ph.emit()
    if dbg:
        ph = Phase(nc)
        ph.op("sync", lambda sy: sy.dma_start(out=udump, in_=u_t[:]),
              waits=[(VV, VV.n)], inc=STO, k=16)
        ph.op("sync", lambda sy: sy.dma_start(out=statdump, in_=stat[:]),
              inc=STO, k=16)
        ph.op("sync", lambda sy: sy.dma_start(out=sqdump, in_=sq8_a[:]),
              inc=STO, k=16)
        ph.op("sync", lambda sy: sy.dma_start(out=xbdump, in_=xb[:]),
              inc=STO, k=16)
        ph.op("sync", lambda sy: sy.dma_start(out=xfdump, in_=xf[:]),
              inc=STO, k=16)
        ph.emit()
    if stop_after == "p2":
        return nc, _emit_finish(nc, STO)

    ph = Phase(nc, "p3_scan")
    wq1_step, wq1_fin = make_wq_pass1(ph, w1t_m, mu1pp, invmu1)
    wq1_step(16)

    def s1f(vv):
        vv.tensor_copy(slocv(0, 0), uc(0, slice(0, T, T1)))
        return vv.tensor_copy(slocv(1, 0), uc(1, slice(0, T, T1)))
    vprev = ph.op("vector", s1f, waits=[(VV, VV.n)], inc=VV)

    for i in range(2, T1 + 1):
        bank = pbk[i % 2]

        def stepmm(tt, i=i, bank=bank):
            last = None
            for n_ in range(NKC):
                for kc in range(NKC):
                    last = tt.matmul(bank[:, n_ * 128:(n_ + 1) * 128],
                                     pw(1, kc)[:, n_ * 128:(n_ + 1) * 128],
                                     slocv(kc, i - 2),
                                     start=(kc == 0), stop=(kc == NKC - 1))
            return last
        m = ph.op("tensor", stepmm, waits=[(VV, vprev)], inc=MM)

        def stepev(vv, i=i, bank=bank):
            vv.tensor_add(slocv(0, i - 1), bank[:, 0:128], uc(0, slice(i - 1, T, T1)))
            return vv.tensor_add(slocv(1, i - 1), bank[:, 128:256],
                                 uc(1, slice(i - 1, T, T1)))
        vprev = ph.op("vector", stepev, waits=[(MM, m)], inc=VV)

    def y1f(vv):
        vv.tensor_copy(y2v(0, 1), slocv(0, T1 - 1, slice(0, 128, T2)))
        return vv.tensor_copy(y2v(1, 1), slocv(1, T1 - 1, slice(0, 128, T2)))
    vprev = ph.op("vector", y1f, inc=VV)
    for j in range(2, T2 + 1):
        bank = pbk[j % 2]

        def l2mm(tt, j=j, bank=bank):
            last = None
            for n_ in range(NKC):
                for kc in range(NKC):
                    last = tt.matmul(bank[:, n_ * 128:n_ * 128 + NG],
                                     pw2(1, kc)[:, n_ * 128:(n_ + 1) * 128],
                                     y2v(kc, j - 1),
                                     start=(kc == 0), stop=(kc == NKC - 1))
            return last
        m = ph.op("tensor", l2mm, waits=[(VV, vprev)], inc=MM)

        def l2ev(vv, j=j, bank=bank):
            vv.tensor_add(y2v(0, j), bank[:, 0:NG],
                          slocv(0, T1 - 1, slice(j - 1, 128, T2)))
            return vv.tensor_add(y2v(1, j), bank[:, 128:128 + NG],
                                 slocv(1, T1 - 1, slice(j - 1, 128, T2)))
        vprev = ph.op("vector", l2ev, waits=[(MM, m)], inc=VV)

    def gz(vv):
        vv.memset(gcol(0, 0), 0.0)
        return vv.memset(gcol(1, 0), 0.0)
    vg = ph.op("vector", gz, inc=VV)
    for g in range(NG):
        bank = pbk[g % 2]

        def l3mm(tt, g=g, bank=bank):
            last = None
            for n_ in range(NKC):
                for kc in range(NKC):
                    last = tt.matmul(bank[:, n_ * 128:n_ * 128 + 1],
                                     pw2(T2, kc)[:, n_ * 128:(n_ + 1) * 128],
                                     gcol(kc, g),
                                     start=(kc == 0), stop=(kc == NKC - 1))
            return last
        m = ph.op("tensor", l3mm, waits=[(VV, vg)], inc=MM)

        def l3ev(vv, g=g, bank=bank):
            vv.tensor_add(gcol(0, g + 1), bank[:, 0:1], y2v(0, T2, slice(g, g + 1)))
            return vv.tensor_add(gcol(1, g + 1), bank[:, 128:129],
                                 y2v(1, T2, slice(g, g + 1)))
        vg = ph.op("vector", l3ev, waits=[(MM, m)], inc=VV)
    vprev = vg

    def ecopyf(vv):
        vv.memset(ev(0, slice(0, 1)), 0.0)
        vv.memset(ev(1, slice(0, 1)), 0.0)
        vv.tensor_copy(ev(0, slice(T2, NCH, T2)),
                       g0col[:, 1:NG])
        return vv.tensor_copy(ev(1, slice(T2, NCH, T2)),
                              g0col[:, (NG + 1) + 1:(NG + 1) + NG])
    vprev3 = ph.op("vector", ecopyf, waits=[(VV, vprev)], inc=VV)
    for j in range(1, T2):
        bank = pbk[2 + (j % 2)]

        def c2mm(tt, j=j, bank=bank):
            last = None
            for n_ in range(NKC):
                for kc in range(NKC):
                    last = tt.matmul(bank[:, n_ * 128:n_ * 128 + NG],
                                     pw2(j, kc)[:, n_ * 128:(n_ + 1) * 128],
                                     g0col[:, kc * (NG + 1):kc * (NG + 1) + NG],
                                     start=(kc == 0), stop=(kc == NKC - 1))
            return last
        m = ph.op("tensor", c2mm, waits=[(VV, vprev3)], inc=MM)

        def c2ev(vv, j=j, bank=bank):
            vv.tensor_add(ev(0, slice(j, NCH, T2)), bank[:, 0:NG], y2v(0, j))
            return vv.tensor_add(ev(1, slice(j, NCH, T2)), bank[:, 128:128 + NG],
                                 y2v(1, j))
        vprev3 = ph.op("vector", c2ev, waits=[(MM, m)], inc=VV)

    vfin = vprev3
    for i in range(1, T1 + 1):
        bank = pbk[4 + (i % 2)]

        def fmm(tt, i=i, bank=bank):
            last = None
            for n_ in range(NKC):
                for kc in range(NKC):
                    last = tt.matmul(bank[:, n_ * 128:(n_ + 1) * 128],
                                     pw(i, kc)[:, n_ * 128:(n_ + 1) * 128],
                                     ev(kc, slice(0, NCH)),
                                     start=(kc == 0), stop=(kc == NKC - 1))
            return last
        m = ph.op("tensor", fmm, waits=[(VV, vfin)], inc=MM)

        def fev(vv, i=i, bank=bank):
            vv.tensor_add(stcv(0, slice(i - 1, T, T1)), bank[:, 0:128],
                          slocv(0, i - 1))
            return vv.tensor_add(stcv(1, slice(i - 1, T, T1)), bank[:, 128:256],
                                 slocv(1, i - 1))
        vfin = ph.op("vector", fev, waits=[(MM, m)], inc=VV)
    wq1_fin()
    ph.emit()
    if dbg:
        ph = Phase(nc)
        ph.op("sync", lambda sy: sy.dma_start(out=sdump, in_=states[:]),
              waits=[(VV, VV.n)], inc=STO, k=16)
        ph.emit()
    if stop_after == "scan":
        return nc, _emit_finish(nc, STO)

    spill_sto = []
    ph = Phase(nc, "p4_epi")
    wq2_step, wq2_fin = make_wq_pass1(ph, w2t_m, mu2pp, invmu2)
    for tt_i in range(NTT):
        wq2_step(4)
        cs = slice(tt_i * 512, (tt_i + 1) * 512)
        sgel = []
        for dc in range(DC):
            bank = pbk[dc % 3]
            w = [(VV, VV.n)] if dc < 3 else [(SS, sgel[dc - 3])]

            def ymm(tt, dc=dc, cs=cs, bank=bank):
                last = None
                for kc in range(NKC):
                    last = tt.matmul(bank[:], ctbc(kc)[:, dc * 128:(dc + 1) * 128],
                                     stcv(kc, cs), start=(kc == 0),
                                     stop=(kc == NKC - 1))
                return last
            m = ph.op("tensor", ymm, waits=w, inc=MM)
            sgel.append(ph.op("scalar",
                              lambda ss, dc=dc, bank=bank:
                              ss.activation(att(dc), bank[:], AF.Gelu),
                              waits=[(MM, m)], inc=SS))

        def abf(vv):
            last = None
            for k in range(DC):
                last = vv.tensor_copy(x1bb_p5[:, k * 512:(k + 1) * 512], att(k))
            return last
        ph.op("vector", abf, waits=[(SS, sgel[-1])], inc=VV)
        vstat = emit_stats(ph, lambda k: x1bb_p5[:, k * 512:(k + 1) * 512],
                           lambda k: att(k), sq8_a)

        vap = None
        for k in range(DC):
            def ap1(vv, k=k):
                vv.tensor_add(att(k), att(k), st(S_NEGM))
                return vv.tensor_mul(att(k), att(k), st(S_RSTD))
            v1 = ph.op("vector", ap1, waits=[(VV, vstat)], inc=VV)
            s1 = ph.op("scalar",
                       lambda ss, k=k: ss.activation(att(k), att(k), AF.Identity,
                                                     bias=bet[:, k:k + 1],
                                                     scale=gam[:, k:k + 1]),
                       waits=[(VV, v1)], inc=SS)
            vap = ph.op("vector",
                        lambda vv, k=k: vv.tensor_scalar(att(k), att(k),
                                                         CLAMP, -CLAMP,
                                                         ALU.min, ALU.max),
                        waits=[(SS, s1)], inc=VV)

        def abf2(vv):
            last = None
            for k in range(DC):
                last = vv.tensor_copy(x1bb_p5[:, k * 512:(k + 1) * 512], att(k))
            return last
        ph.op("vector", abf2, waits=[(VV, vap)], inc=VV)
        vstat2 = emit_stats(ph, lambda k: x1bb_p5[:, k * 512:(k + 1) * 512],
                            lambda k: att(k), sq8_a)

        vr = None
        for k in range(DC):
            def ap2(vv, k=k, cs=cs):
                vv.tensor_add(att(k), att(k), st(S_NEGM))
                vv.tensor_mul(att(k), att(k), st(S_RSTD))
                return vv.tensor_add(xfc(k, cs), att(k), xfc(k, cs))
            vr = ph.op("vector", ap2, waits=[(VV, vstat2)], inc=VV)

        spill_sto.append(
            ph.op("sync",
                  lambda sy, cs=cs: sy.dma_start(
                      out=x1s.rearrange("p (k t) -> p k t", k=DC)[:, :, cs],
                      in_=xf.ap().rearrange("p (k t) -> p k t", k=DC)[:, :, cs]),
                  waits=[(VV, vr)], inc=STO, k=16))
    wq2_fin()
    ph.emit()
    if stop_after == "attn":
        return nc, _emit_finish(nc, STO)

    def emit_wq_pass1(src_m, mu_pp, inv_pp):
        ph = Phase(nc, "wq_p1")
        svals = []
        for ch in range(8):
            buf = wchunk if ch % 2 == 0 else wchunk2
            w = [(SS, svals[ch - 2])] if ch >= 2 else [(STO, STO.n)]
            l = ph.op("sync",
                      lambda sy, ch=ch, buf=buf: sy.dma_start(
                          out=buf[:], in_=src_m[:, ch * 4096:(ch + 1) * 4096]),
                      waits=w, inc=LD, k=16)
            svals.append(ph.op("scalar",
                               lambda ss, ch=ch, buf=buf: ss.activation(
                                   absdump[:], buf[:], AF.Abs,
                                   accum_out=acc8[:, ch:ch + 1]),
                               waits=[(LD, l)], inc=SS))
        v1 = ph.op("vector",
                   lambda vv: vv.tensor_copy(acc8b[:], acc8[:]),
                   waits=[(SS, svals[-1])], inc=VV)
        m1 = ph.op("tensor",
                   lambda tt: tt.matmul(pbk[6][:, 0:8], ones128b[:], acc8b[:],
                                        start=True, stop=True),
                   waits=[(VV, v1)], inc=MM)

        v2 = ph.op("vector",
                   lambda vv: vv.reduce_sum(inv_pp[:], pbk[6][:, 0:8], axis=AX.X),
                   waits=[(MM, m1)], inc=VV)
        v3 = ph.op("vector",
                   lambda vv: vv.tensor_scalar(mu_pp[:], inv_pp[:],
                                               1.0 / (1024.0 * 4096.0), 1e-5,
                                               ALU.mult, ALU.max),
                   waits=[(VV, v2)], inc=VV)
        ph.op("vector", lambda vv: vv.reciprocal(inv_pp[:], mu_pp[:]),
              waits=[(VV, v3)], inc=VV)
        ph.emit()

    def emit_wq_pass2(src_m, dst_cols, inv_pp, first_waits=()):
        ph = Phase(nc, "wq_p2")
        qvals = []
        for ch in range(8):
            buf2 = wchunk if ch % 2 == 0 else wchunk2
            w = [(VV, qvals[ch - 2])] if ch >= 2 else list(first_waits) + [(VV, VV.n)]
            l = ph.op("sync",
                      lambda sy, ch=ch, buf2=buf2: sy.dma_start(
                          out=buf2[:], in_=src_m[:, ch * 4096:(ch + 1) * 4096]),
                      waits=w, inc=LD, k=16)

            def qf(vv, ch=ch, buf2=buf2):
                vv.tensor_scalar_mul(wcint[:], buf2[:], inv_pp[:, 0:1])
                return vv.tensor_scalar(dst_cols(ch), wcint[:], 1.0, -1.0,
                                        ALU.min, ALU.max)
            qvals.append(ph.op("vector", qf, waits=[(LD, l)], inc=VV))
        ph.emit()

    emit_wq_pass2(w1t_m, lambda ch: w1t[:, ch * 4096:(ch + 1) * 4096], invmu1,
                  first_waits=[(STO, spill_sto[-1])])
    if stop_after == "wq":
        return nc, _emit_finish(nc, STO)

    q2_sto = []
    ph = Phase(nc, "f1")
    x1_lds = []
    x1_lastuse = []
    gdma = None
    for tt_i in range(NTT):
        cs = slice(tt_i * 512, (tt_i + 1) * 512)
        xbuf = x1t if tt_i % 2 == 0 else x1t2

        def x1tc_f1(k, xbuf=xbuf):
            return xbuf[:, k * 512:(k + 1) * 512]

        wld = [(STO, spill_sto[tt_i])]
        if tt_i >= 2:
            wld.append((VV, x1_lastuse[tt_i - 2]))
        l = ph.op("sync",
                  lambda sy, cs=cs, xbuf=xbuf: sy.dma_start(
                      out=xbuf[:],
                      in_=x1s.rearrange("p (k t) -> p k t", k=DC)[:, :, cs]),
                  waits=wld, inc=LD, k=16)
        x1_lds.append(l)

        def xbf(ss, x1tc=x1tc_f1):
            last = None
            for k in range(DC):
                last = ss.copy(x1bb_f1[:, k * 512:(k + 1) * 512], x1tc(k))
            return last
        sxb = ph.op("scalar", xbf, waits=[(LD, l)], inc=SS)
        emit_stats(ph, lambda k, x1tc=x1tc_f1: x1bb_f1[:, k * 512:(k + 1) * 512] if False else x1bb_f1[:, k * 512:(k + 1) * 512],
                   lambda k, x1tc=x1tc_f1: x1tc(k), sq8_f1)

        scen = []
        for k in range(DC):
            sl = scr1_f1[:, (k % 2) * 512:((k % 2) + 1) * 512]
            slb = scrb[:, (k % 2) * 512:((k % 2) + 1) * 512]
            w = [(SS, scen[k - 2])] if k >= 2 else [(VV, VV.n)]
            v = ph.op("vector",
                      lambda vv, k=k, sl=sl, x1tc=x1tc_f1:
                      vv.tensor_add(sl, x1tc(k), st(S_NEGM)),
                      waits=w, inc=VV)
            scen.append(ph.op("scalar",
                              lambda ss, sl=sl, slb=slb:
                              ss.activation(slb, sl, AF.Square),
                              waits=[(VV, v)], inc=SS))
            if k == 0:
                ph.op("vector", lambda vv: vv.tensor_copy(rmax[:], scrb[:, 0:512]),
                      waits=[(SS, scen[0])], inc=VV)
            else:
                ph.op("vector",
                      lambda vv, slb=slb: vv.tensor_max(rmax[:], rmax[:], slb),
                      waits=[(SS, scen[k])], inc=VV)
        vrm = VV.n

        def amax_chain(tag):
            def tpf(tt):
                last = None
                for j in range(4):
                    last = tt.transpose(pT[:, j * 128:(j + 1) * 128],
                                        rmax[:, j * 128:(j + 1) * 128], idb[:])
                return last
            m = ph.op("tensor", tpf, waits=[(VV, VV.n)], inc=MM)

            def redf(vv):
                last = None
                for j in range(4):
                    last = vv.reduce_max(mcol[:, j:j + 1],
                                         pT[:, j * 128:(j + 1) * 128], axis=AX.X)
                return last
            v = ph.op("vector", redf, waits=[(MM, m)], inc=VV)

            def mtpf(tt):
                last = None
                for j in range(4):
                    last = tt.transpose(pbk[6][0:1, j * 128:(j + 1) * 128],
                                        mcol[:, j:j + 1], idf[:])
                return last
            m = ph.op("tensor", mtpf, waits=[(VV, v)], inc=MM)
            v = ph.op("vector",
                      lambda vv: vv.tensor_copy(mrow[:], pbk[6][0:1, 0:512]),
                      waits=[(MM, m)], inc=VV)
            m = ph.op("tensor",
                      lambda tt: tt.matmul(pbk[5][:], ones_r[:], mrow[:],
                                           start=True, stop=True),
                      waits=[(VV, v)], inc=MM)
            return ph.op("scalar",
                         lambda ss: ss.activation(st(S_A), pbk[5][:], AF.Sqrt),
                         waits=[(MM, m)], inc=SS)

        s = amax_chain("a1")

        def soup2(vv):
            vv.tensor_mul(st(S_SCR), st(S_VAR), st(S_RSTD))
            return vv.tensor_mul(st(S_SCR), st(S_SCR), st(S_RSTD))
        v = ph.op("vector", soup2, waits=[(SS, s)], inc=VV)
        s = ph.op("scalar",
                  lambda ss: ss.activation(st(S_B), st(S_SCR), AF.Sqrt,
                                           bias=epspp[:, 0:1]),
                  waits=[(VV, v)], inc=SS)

        def soup3(vv):
            vv.reciprocal(st(S_SCR), st(S_B))
            vv.tensor_mul(st(S_A), st(S_A), st(S_RSTD))
            vv.tensor_mul(st(S_A), st(S_A), st(S_SCR))
            vv.tensor_scalar_max(st(S_A), st(S_A), 1e-5)
            vv.tensor_scalar(st(S_C), st(S_A), mu1pp[:, 0:1], 1.0 / 127.0,
                             ALU.mult, ALU.mult)
            vv.reciprocal(st(S_B), st(S_A))
            vv.tensor_scalar_mul(st(S_B), st(S_B), 127.0)
            vv.tensor_mul(st(S_B), st(S_B), st(S_SCR))
            vv.tensor_mul(st(S_B), st(S_B), st(S_RSTD))
            return vv.tensor_mul(st(S_SCR), st(S_NEGM), st(S_B))
        v = ph.op("vector", soup3, waits=[(SS, s)], inc=VV)

        for k in range(DC):
            def q1f(vv, k=k, x1tc=x1tc_f1):
                vv.tensor_mul(scr1_f1[:, 0:512], x1tc(k), st(S_B))
                vv.tensor_tensor(scri_f1[:, 512:1024], scr1_f1[:, 0:512],
                                 st(S_SCR), ALU.add)
                return vv.tensor_mul(q1bc(k), scri_f1[:, 512:1024], st(S_C))
            v = ph.op("vector", q1f, inc=VV)
        vq1 = v
        x1_lastuse.append(vq1)

        sgel = []
        ssqs = []
        vmaxs = []
        gmms = []
        for fc in range(FC):
            bank = pbk[fc % 4]
            w = [(VV, vq1)]
            if fc >= 4:
                w.append((SS, ssqs[fc - 4]))

            def m1f(tt, fc=fc, bank=bank):
                last = None
                for k in range(DC):
                    last = tt.matmul(
                        bank[:], w1t[:, k * FF + fc * 128:k * FF + (fc + 1) * 128],
                        q1bc(k), start=(k == 0), stop=(k == DC - 1))
                return last
            m = ph.op("tensor", m1f, waits=w, inc=MM)
            sg = ph.op("scalar",
                       lambda ss, fc=fc, bank=bank: ss.activation(
                           gtc(fc), bank[:], AF.Gelu, bias=b1c[:, fc:fc + 1]),
                       waits=[(MM, m)], inc=SS)
            sgel.append(sg)
            slb = scrb[:, (fc % 2) * 512:((fc % 2) + 1) * 512]
            wsq = [(VV, vmaxs[fc - 2])] if fc >= 2 else []
            ssq = ph.op("scalar",
                        lambda ss, fc=fc, slb=slb:
                        ss.activation(slb, gtc(fc), AF.Square),
                        waits=wsq, inc=SS)
            ssqs.append(ssq)
            if fc == 0:
                def v0(vv):
                    vv.tensor_copy(rmax[:], scrb[:, 0:512])
                    return vv.tensor_copy(sqacc[:], scrb[:, 0:512])
                vg = ph.op("vector", v0, waits=[(SS, ssq)], inc=VV)
            else:
                def vn(vv, slb=slb):
                    vv.tensor_max(rmax[:], rmax[:], slb)
                    return vv.tensor_add(sqacc[:], sqacc[:], slb)
                vg = ph.op("vector", vn, waits=[(SS, ssq)], inc=VV)
            vmaxs.append(vg)

        m = ph.op("tensor",
                  lambda tt: tt.matmul(pbk[4][:], ones128f[:], sqacc[:],
                                       start=True, stop=True),
                  waits=[(VV, vmaxs[-1])], inc=MM)
        v = ph.op("vector",
                  lambda vv: vv.tensor_scalar_mul(st(S_VAR), pbk[4][:], 1.0 / FF),
                  waits=[(MM, m)], inc=VV)
        s_r2 = ph.op("scalar",
                     lambda ss: ss.activation(st(S_B), st(S_VAR), AF.Sqrt,
                                              bias=epspp[:, 0:1]),
                     waits=[(VV, v)], inc=SS)
        s = amax_chain("a2")

        def soup4(vv):
            vv.reciprocal(st(S_SCR), st(S_B))
            vv.tensor_mul(st(S_A), st(S_A), st(S_SCR))
            vv.tensor_scalar_max(st(S_A), st(S_A), 1e-5)
            vv.tensor_scalar(st(S_C), st(S_A), mu2pp[:, 0:1], 1.0 / 127.0,
                             ALU.mult, ALU.mult)
            vv.reciprocal(st(S_B), st(S_A))
            vv.tensor_scalar_mul(st(S_B), st(S_B), 127.0)
            return vv.tensor_mul(st(S_B), st(S_B), st(S_SCR))
        v = ph.op("vector", soup4, waits=[(SS, s), (SS, s_r2)], inc=VV)
        c2sto = ph.op("sync",
                      lambda sy, cs=cs: sy.dma_start(out=c2row_t.ap()[0:1, cs],
                                                     in_=st(S_C)[0:1, :]),
                      waits=[(VV, v)], inc=STO, k=16)
        q2_sto.append(c2sto)

        pp_hist = []
        if gdma is None:
            gdma = c2sto
        for fcg in range(4):
            for jj in range(8):
                fc = fcg * 8 + jj
                si = scri_f1[:, (fc % 2) * 512:((fc % 2) + 1) * 512]
                w = [(VV, v)] if fc < 2 else [(PP, pp_hist[fc - 2])]
                vv_ = ph.op("vector",
                            lambda vv, fc=fc, si=si:
                            vv.tensor_mul(si, gtc(fc), st(S_B)),
                            waits=w, inc=VV)
                wp = [(VV, vv_)]
                if jj == 0:
                    wp.append((STO, gdma))
                pp_hist.append(ph.op("gpsimd",
                                     lambda pp, jj=jj, si=si: pp.tensor_copy(
                                         q2st[:, jj * 512:(jj + 1) * 512], si),
                                     waits=wp, inc=PP))
            gdma = ph.op("sync",
                         lambda sy, fcg=fcg, cs=cs: sy.dma_start(
                             out=bass.AP(q2s_t, 8 * fcg * T + cs.start,
                                         [[FC * T, 128], [T, 8], [1, 512]]),
                             in_=q2st.ap().rearrange("p (c t) -> p c t", c=8)),
                         waits=[(PP, pp_hist[-1])], inc=STO, k=16)
            q2_sto.append(gdma)
    ph.emit()

    if dbg:
        ph = Phase(nc)
        ph.op("sync", lambda sy: sy.dma_start(out=gdump, in_=g_t[:]),
              waits=[(VV, VV.n), (SS, SS.n), (PP, PP.n)], inc=STO, k=16)
        ph.op("sync", lambda sy: sy.dma_start(out=q1dump, in_=q1b[:]),
              inc=STO, k=16)
        ph.op("sync", lambda sy: sy.dma_start(out=wqdump[:, 0:8], in_=acc8[:]),
              inc=STO, k=16)
        ph.op("sync", lambda sy: sy.dma_start(out=wqdump[:, 8:12], in_=mcol[:]),
              inc=STO, k=16)
        ph.op("sync", lambda sy: sy.dma_start(out=wqdump[:, 12:13], in_=mu1pp[:], allow_slow_non_contiguous=True),
              inc=STO, k=16)
        ph.op("sync", lambda sy: sy.dma_start(out=wqdump[:, 13:14], in_=mu2pp[:], allow_slow_non_contiguous=True),
              inc=STO, k=16)
        ph.op("sync", lambda sy: sy.dma_start(out=wqdump[:, 14:15], in_=invmu1[:], allow_slow_non_contiguous=True),
              inc=STO, k=16)
        ph.op("sync", lambda sy: sy.dma_start(out=wqdump[:, 15:16], in_=invmu2[:], allow_slow_non_contiguous=True),
              inc=STO, k=16)
        ph.op("sync", lambda sy: sy.dma_start(out=w1tdump, in_=w1t[:]),
              inc=STO, k=16)
        ph.op("sync", lambda sy: sy.dma_start(out=statdump, in_=stat[:]),
              inc=STO, k=16)
        ph.emit()
    if stop_after == "f1":
        return nc, _emit_finish(nc, STO)

    emit_wq_pass2(w2t_m, lambda ch: w2t[:, ch * 4096:(ch + 1) * 4096], invmu2)
    q2_done = STO.n

    ph = Phase(nc, "f2")

    def o2v(t, k):
        buf = o2stA if t % 2 == 0 else o2stB
        return buf[:, k * 512:(k + 1) * 512]

    ctxs = []
    prev_mm_last = None
    prev_sev_last = None
    prev_vr = None
    prev_outdma = None
    prev_evac_last = None

    def emit_head(tt_i):
        nonlocal prev_mm_last, prev_sev_last, prev_evac_last
        cs = slice(tt_i * 512, (tt_i + 1) * 512)
        wq2 = [(MM, prev_mm_last)] if prev_mm_last else [(STO, q2_done)]
        l = ph.op("sync",
                  lambda sy, cs=cs: sy.dma_start(
                      out=q2in[:].rearrange("p (c t) -> p c t", c=FC),
                      in_=bass.AP(q2s_t, cs.start, [[FC * T, 128], [T, FC], [1, 512]])),
                  waits=wq2, inc=LD, k=16)
        wc2 = [(VV, prev_evac_last)] if prev_evac_last else [(STO, q2_done)]
        l3 = ph.op("sync",
                   lambda sy, cs=cs: sy.dma_start(
                       out=st(S_C),
                       in_=bass.AP(c2row_t, cs.start, [[0, 128], [1, 512]])),
                   waits=wc2, inc=LD, k=16)
        sev = []
        for dc in range(DC):
            bank = pbk[dc % 3]
            w = [(LD, l)]
            if dc >= 3:
                w.append((SS, sev[dc - 3]))
            elif prev_sev_last is not None:
                w.append((SS, prev_sev_last))

            def m2f(tt, dc=dc, bank=bank):
                last = None
                for fc in range(FC):
                    last = tt.matmul(
                        bank[:], w2t[:, fc * D + dc * 128:fc * D + (dc + 1) * 128],
                        q2inc(fc), start=(fc == 0), stop=(fc == FC - 1))
                return last
            m = ph.op("tensor", m2f, waits=w, inc=MM)
            if dc == DC - 1:
                prev_mm_last = m
            slo = scr1_f2[:, (dc % 2) * 512:((dc % 2) + 1) * 512]
            wv = [(MM, m), (LD, l3)]
            if dc >= 2:
                wv.append((SS, sev[dc - 2]))
            v = ph.op("vector",
                      lambda vv, bank=bank, slo=slo: vv.tensor_mul(slo, bank[:],
                                                                   st(S_C)),
                      waits=wv, inc=VV)
            if dc == DC - 1:
                prev_evac_last = v
            sev.append(ph.op("scalar",
                             lambda ss, dc=dc, tt_i=tt_i, slo=slo: ss.activation(
                                 o2v(tt_i, dc), slo,
                                 AF.Identity, bias=b2c[:, dc:dc + 1]),
                             waits=[(VV, v)], inc=SS))
        prev_sev_last = sev[-1]
        ctxs.append((tt_i, cs, sev))

    def emit_tail():
        nonlocal prev_vr, prev_outdma
        tt_i, cs, sev = ctxs.pop(0)
        wx1 = [(VV, prev_vr)] if prev_vr else [(STO, q2_done)]
        l2 = ph.op("sync",
                   lambda sy, cs=cs: sy.dma_start(
                       out=x1t[:],
                       in_=x1s.rearrange("p (k t) -> p k t", k=DC)[:, :, cs]),
                   waits=wx1, inc=LD, k=16)
        emit_stats(ph, lambda k: o2v(tt_i, k), lambda k: o2v(tt_i, k), sq8_f2)
        vr = None
        for k in range(DC):
            wap = [(LD, l2), (VV, VV.n)]
            if k == 0 and prev_outdma is not None:
                wap.append((STO, prev_outdma))

            def ap3(vv, k=k, tt_i=tt_i):
                vv.tensor_add(scr1_f2[:, 0:512], o2v(tt_i, k), st(S_NEGM))
                vv.tensor_mul(scr1_f2[:, 0:512], scr1_f2[:, 0:512], st(S_RSTD))
                return vv.tensor_add(outst[:, k * 512:(k + 1) * 512],
                                     scr1_f2[:, 0:512], x1tc(k))
            vr = ph.op("vector", ap3, waits=wap, inc=VV)
        prev_vr = vr
        prev_outdma = ph.op("sync",
                            lambda sy, cs=cs: sy.dma_start(
                                out=out_m.rearrange("p (k t) -> p k t", k=DC)[:, :, cs],
                                in_=outst.ap().rearrange("p (k t) -> p k t", k=DC)),
                            waits=[(VV, vr)], inc=STO, k=16)

    for tt_i in range(NTT):
        emit_head(tt_i)
        if tt_i >= 1:
            emit_tail()
    emit_tail()
    ph.emit()
    return nc, _emit_finish(nc, STO)


def _emit_finish(nc, STO):
    with nc.Block() as blk:
        @blk.sync
        def _(sy):
            sy.wait_ge(STO.h, STO.n)
    return STO.n


def _tile_rows(a, p=128):
    a = np.ascontiguousarray(a)
    R, C = a.shape
    return np.ascontiguousarray(
        a.reshape(R // p, p, C).transpose(1, 0, 2).reshape(p, (R // p) * C))


def make_inputs(x, A, B, C, gam, bet, w1, b1, w2, b2):
    import ml_dtypes
    shared = {
        "at_m": _tile_rows(A.T),
        "a_m": _tile_rows(A),
        "bt_m": _tile_rows(B.T),
        "ct_m": _tile_rows(C.T),
        "gam_m": _tile_rows(gam.reshape(D, 1)),
        "bet_m": _tile_rows(bet.reshape(D, 1)),
        "w1t_m": _tile_rows(w1.T),
        "b1_m": _tile_rows(b1.reshape(FF, 1)),
        "w2t_m": _tile_rows(w2.T),
        "b2_m": _tile_rows(b2.reshape(D, 1)),
        "idf_m": np.eye(128, dtype=np.float32),
        "idb_m": np.eye(128, dtype=ml_dtypes.bfloat16),
    }
    maps = []
    for b in range(x.shape[0]):
        m = dict(shared)
        m["x_f"] = _tile_rows(x[b].T)
        maps.append(m)
    return maps


def kernel(L0_attn_x, L0_attn_A, L0_attn_B, L0_attn_C, L0_attn_gamma,
           L0_attn_beta, L0_ffn_w1, L0_ffn_b1, L0_ffn_w2, L0_ffn_b2):
    x = np.asarray(L0_attn_x, dtype=np.float32)
    in_maps = make_inputs(x,
                          np.asarray(L0_attn_A, np.float32),
                          np.asarray(L0_attn_B, np.float32),
                          np.asarray(L0_attn_C, np.float32),
                          np.asarray(L0_attn_gamma, np.float32),
                          np.asarray(L0_attn_beta, np.float32),
                          np.asarray(L0_ffn_w1, np.float32),
                          np.asarray(L0_ffn_b1, np.float32),
                          np.asarray(L0_ffn_w2, np.float32),
                          np.asarray(L0_ffn_b2, np.float32))
    nc, _ = build_program()
    res = run_bass_kernel_spmd(nc, in_maps, core_ids=list(range(8)))
    outs = []
    for b in range(8):
        o = res.results[b]["out"]
        o = o.reshape(128, DC, T).transpose(1, 0, 2).reshape(D, T)
        outs.append(np.ascontiguousarray(o.T))
    return np.stack(outs, axis=0)


# version 07
# speedup vs baseline: 1.1578x; 1.1112x over previous
import numpy as np
import concourse.bass as bass
import concourse.mybir as mybir
from concourse.bass_utils import run_bass_kernel_spmd

F32 = mybir.dt.float32
BF16 = mybir.dt.bfloat16
I32 = mybir.dt.int32
AF = mybir.ActivationFunctionType
ALU = mybir.AluOpType
AX = mybir.AxisListType

D = 1024
FF = 4096
T = 2048
N = 256
DC = D // 128
FC = FF // 128
NKC = N // 128
NTT = T // 512
EPS = 1e-5
CLAMP = 1e6

T1 = 16
NCH = T // T1
T2 = 8
NG = NCH // T2

KB = 1024
ENGS = ("sync", "tensor", "scalar", "vector", "gpsimd")


class Sem:
    def __init__(self, nc, name):
        self._cm = nc.semaphore(name)
        self.h = self._cm.__enter__()
        self.n = 0


class Phase:
    def __init__(self, nc, name=None):
        self.nc = nc
        self.name = name
        self.ops = {e: [] for e in ENGS}

    def op(self, eng, fn, waits=(), inc=None, k=1):
        self.ops[eng].append((fn, [(s.h, c) for (s, c) in waits if c > 0], inc, k))
        if inc is not None:
            inc.n += k
            return inc.n
        return None

    def emit(self):
        import contextlib
        scope = (self.nc.named_scope(self.name) if self.name
                 else contextlib.nullcontext())
        with scope, self.nc.Block() as blk:
            for name in ENGS:
                items = self.ops[name]
                if not items:
                    continue
                section = getattr(blk, name)

                @section
                def _(eng, items=items):
                    for fn, waits, inc, k in items:
                        for (h, c) in waits:
                            eng.wait_ge(h, c)
                        instr = fn(eng)
                        if inc is not None:
                            instr.then_inc(inc.h, k)


def build_program(dbg=False, stop_after=None, skip_ladder=False):
    nc = bass.Bass("TRN2", target_bir_lowering=False)

    x_f = nc.dram_tensor("x_f", [128, DC * T], F32, kind="ExternalInput").ap()
    at_m = nc.dram_tensor("at_m", [128, NKC * N], F32, kind="ExternalInput").ap()
    a_m = nc.dram_tensor("a_m", [128, NKC * N], F32, kind="ExternalInput").ap()
    bt_m = nc.dram_tensor("bt_m", [128, DC * N], F32, kind="ExternalInput").ap()
    ct_m = nc.dram_tensor("ct_m", [128, NKC * D], F32, kind="ExternalInput").ap()
    gam_m = nc.dram_tensor("gam_m", [128, DC], F32, kind="ExternalInput").ap()
    bet_m = nc.dram_tensor("bet_m", [128, DC], F32, kind="ExternalInput").ap()
    w1t_m = nc.dram_tensor("w1t_m", [128, DC * FF], F32, kind="ExternalInput").ap()
    b1_m = nc.dram_tensor("b1_m", [128, FC], F32, kind="ExternalInput").ap()
    w2t_m = nc.dram_tensor("w2t_m", [128, FC * D], F32, kind="ExternalInput").ap()
    b2_m = nc.dram_tensor("b2_m", [128, DC], F32, kind="ExternalInput").ap()
    idf_m = nc.dram_tensor("idf_m", [128, 128], F32, kind="ExternalInput").ap()
    idb_m = nc.dram_tensor("idb_m", [128, 128], BF16, kind="ExternalInput").ap()

    out_m = nc.dram_tensor("out", [128, DC * T], F32, kind="ExternalOutput").ap()
    x1s_t = nc.dram_tensor("x1s", [128, DC * T], F32,
                           kind="ExternalOutput" if dbg else "Internal")
    x1s = x1s_t.ap()
    q2s_t = nc.dram_tensor("q2s", [128, FC * T], BF16,
                           kind="ExternalOutput" if dbg else "Internal")
    c2row_t = nc.dram_tensor("c2row", [1, T], F32,
                             kind="ExternalOutput" if dbg else "Internal")
    if dbg:
        udump = nc.dram_tensor("udump", [128, NKC * T], F32, kind="ExternalOutput").ap()
        sdump = nc.dram_tensor("sdump", [128, NKC * T], BF16, kind="ExternalOutput").ap()
        statdump = nc.dram_tensor("statdump", [128, 8 * 512], F32, kind="ExternalOutput").ap()
        sqdump = nc.dram_tensor("sqdump", [128, DC * 512], BF16, kind="ExternalOutput").ap()
        xbdump = nc.dram_tensor("xbdump", [128, DC * T], BF16, kind="ExternalOutput").ap()
        xfdump = nc.dram_tensor("xfdump", [128, DC * T], F32, kind="ExternalOutput").ap()
        gdump = nc.dram_tensor("gdump", [128, FC * 512], BF16, kind="ExternalOutput").ap()
        wqdump = nc.dram_tensor("wqdump", [128, 16], F32, kind="ExternalOutput").ap()
        w1tdump = nc.dram_tensor("w1tdump", [128, DC * FF], BF16, kind="ExternalOutput").ap()
        q1dump = nc.dram_tensor("q1dump", [128, DC * 512], BF16, kind="ExternalOutput").ap()
        w2tdump = nc.dram_tensor("w2tdump", [128, FC * D], BF16, kind="ExternalOutput").ap()
        o2dump = nc.dram_tensor("o2dump", [128, DC * 512], F32, kind="ExternalOutput").ap()

    A0, B0, C0, D0 = 18 * KB, 82 * KB, 146 * KB, 194 * KB

    def at_(name, shape, dt, off):
        return nc.alloc_sbuf_tensor_at(name, shape, dt, offset=off)

    xf = at_("xf", [128, DC * T], F32, A0)
    w1t = at_("w1t", [128, DC * FF], BF16, A0)
    o2stA = at_("o2stA", [128, DC * 512], BF16, A0)
    o2stB = at_("o2stB", [128, DC * 512], BF16, A0 + 8 * KB)
    outst = at_("outst", [128, DC * 512], F32, A0 + 16 * KB)
    sq8_f2 = at_("sq8_f2", [128, DC * 512], BF16, A0 + 40 * KB)
    scr1_f2 = at_("scr1_f2", [128, 2 * 512], F32, A0 + 48 * KB)
    p0stg = at_("p0stg", [128, 6144], F32, B0)
    xb = at_("xb", [128, DC * T], BF16, B0)
    sq8_a = at_("sq8_a", [128, DC * 512], BF16, B0 + 32 * KB)
    lad = at_("lad", [128, 16 * 512], BF16, B0 + 40 * KB)
    ppow2 = at_("ppow2", [128, T2 * NKC * N], BF16, B0 + 40 * KB)
    act_t = at_("act_t", [128, DC * 512], F32, B0 + 40 * KB)
    btb = at_("btb", [128, DC * N], BF16, B0 + 56 * KB)
    ctb = at_("ctb", [128, NKC * D], BF16, B0 + 60 * KB)
    absdump = at_("absdump", [128, 4096], BF16, B0)
    wqbufA = at_("wqbufA", [128, 2048], F32, B0)
    wqbufB = at_("wqbufB", [128, 2048], F32, B0 + 8 * KB)
    wqabs = at_("wqabs", [128, 2048], BF16, B0 + 16 * KB)
    g_t = at_("g_t", [128, FC * 512], BF16, B0)
    w2t = at_("w2t", [128, FC * D], BF16, B0)
    u_t = at_("u_t", [128, NKC * T], F32, C0)
    sloc = at_("sloc", [128, NKC * T1 * 128], BF16, C0 + 16 * KB)
    states = at_("states", [128, NKC * T], BF16, C0 + 24 * KB)
    ppow = at_("ppow", [128, T1 * NKC * N], BF16, C0 + 32 * KB)
    x1bb_p5 = at_("x1bb_p5", [128, DC * 512], BF16, C0)
    wchunk = at_("wchunk", [128, 4096], F32, C0)
    wchunk2 = at_("wchunk2", [128, 4096], F32, C0 + 32 * KB)
    wcint = at_("wcint", [128, 4096], I32, C0 + 16 * KB)
    q1b = at_("q1b", [128, DC * 512], BF16, C0)
    x1bb_f1 = at_("x1bb_f1", [128, DC * 512], BF16, C0)
    scr1_f1 = at_("scr1_f1", [128, 2 * 512], F32, C0 + 8 * KB)
    scri_f1 = at_("scri_f1", [128, 2 * 512], I32, C0 + 8 * KB)
    sq8_f1 = at_("sq8_f1", [128, DC * 512], BF16, C0 + 16 * KB)
    x1t = at_("x1t", [128, DC * 512], F32, C0 + 32 * KB)
    x1t2 = at_("x1t2", [128, DC * 512], F32, B0 + 32 * KB)
    q2st = at_("q2st", [128, 8 * 512], BF16, C0 + 24 * KB)
    q2in = at_("q2in", [128, FC * 512], BF16, C0)
    dptr = D0

    def dal(name, shape, dt, nbytes, share=None):
        nonlocal dptr
        off = share if share is not None else dptr
        t = at_(name, shape, dt, off)
        if share is None:
            dptr += nbytes
        return t

    ones128b = dal("ones128b", [128, 128], BF16, 256)
    onescb = dal("onescb", [128, 1], BF16, 32)
    onesc = dal("onesc", [128, 1], F32, 32)
    ones_r = dal("ones_r", [1, 128], F32, 512)
    idf = dal("idf", [128, 128], F32, 512)
    idb = dal("idb", [128, 128], BF16, 256)
    gam = dal("gam", [128, DC], F32, 32)
    bet = dal("bet", [128, DC], F32, 32)
    b1c = dal("b1c", [128, FC], F32, 128)
    b2c = dal("b2c", [128, DC], F32, 32)
    rowbb = dal("rowbb", [1, N], BF16, 512)
    nmrow = dal("nmrow", [1, 512], BF16, 1024)
    NSTAT = 8
    stat = dal("stat", [128, NSTAT * 512], F32, NSTAT * 2 * KB)
    scrb = dal("scrb", [128, 2 * 512], BF16, 2 * KB)
    rmax = dal("rmax", [128, 512], BF16, KB)
    mcol = dal("mcol", [128, 4], F32, 32)
    mrow = dal("mrow", [1, 512], F32, 2 * KB)
    acc8 = dal("acc8", [128, 8], F32, 32)
    mu1pp = dal("mu1pp", [128, 1], F32, 32)
    mu2pp = dal("mu2pp", [128, 1], F32, 32)
    invmu1 = dal("invmu1", [128, 1], F32, 32)
    invmu2 = dal("invmu2", [128, 1], F32, 32)
    epspp = dal("epspp", [128, 1], F32, 32)
    acc8b = dal("acc8b", [128, 8], BF16, 32)
    acc16 = dal("acc16", [128, 16], F32, 64)
    acc16b = dal("acc16b", [128, 16], BF16, 64)
    ones128f = dal("ones128f", [128, 128], F32, 512)
    sqacc = dal("sqacc", [128, 512], F32, 2 * KB)
    g0col = dal("g0col", [128, NKC * (NG + 1)], BF16, 96)
    y2 = dal("y2", [128, NKC * T2 * NG], BF16, NKC * T2 * NG * 2)
    e_t = dal("e_t", [128, NKC * NCH], BF16, NKC * NCH * 2)
    assert dptr <= 228 * KB, f"D region overflow: ends at {dptr}"
    nc.sbuf_base = dptr

    def st(i, sl=slice(0, 512)):
        return stat[:, i * 512:(i + 1) * 512][:, sl]

    pbk = [nc.alloc_psum_tensor(f"pb{i}", [128, 512], F32) for i in range(7)]
    pT = nc.alloc_psum_tensor("pT", [128, 512], BF16)

    LD = Sem(nc, "LD")
    MM = Sem(nc, "MM")
    VV = Sem(nc, "VV")
    SS = Sem(nc, "SS")
    PP = Sem(nc, "PP")
    STO = Sem(nc, "STO")

    def xfc(k, sl=slice(0, T)):
        return xf[:, k * T:(k + 1) * T][:, sl]

    def xbc(k, sl=slice(0, T)):
        return xb[:, k * T:(k + 1) * T][:, sl]

    def att(k):
        return act_t[:, k * 512:(k + 1) * 512]

    def uc(n, sl):
        return u_t[:, n * T:(n + 1) * T][:, sl]

    def stcv(n, sl):
        return states[:, n * T:(n + 1) * T][:, sl]

    def slocv(n, i, sl=slice(0, 128)):
        base = n * T1 * 128 + i * 128
        return sloc[:, base:base + 128][:, sl]

    def pw(i, kc):
        base = (i - 1) * NKC * N + kc * N
        return ppow[:, base:base + N]

    def pw2(j, kc):
        base = (j - 1) * NKC * N + kc * N
        return ppow2[:, base:base + N]

    def ladv(j, rc):
        base = (j - 1) * 512 + rc * 256
        return lad[:, base:base + 256]

    def btbc(k):
        return btb[:, k * N:(k + 1) * N]

    def ctbc(kc):
        return ctb[:, kc * D:(kc + 1) * D]

    def gtc(fc):
        return g_t[:, fc * 512:(fc + 1) * 512]

    def x1tc(k):
        return x1t[:, k * 512:(k + 1) * 512]

    def q1bc(k):
        return q1b[:, k * 512:(k + 1) * 512]

    def q2inc(fc):
        return q2in[:, fc * 512:(fc + 1) * 512]

    def ev(n, sl):
        return e_t[:, n * NCH:(n + 1) * NCH][:, sl]

    def gcol(n, g):
        return g0col[:, n * (NG + 1) + g:n * (NG + 1) + g + 1]

    def y2v(n, j, sl=slice(0, NG)):
        base = n * T2 * NG + (j - 1) * NG
        return y2[:, base:base + NG][:, sl]

    ph = Phase(nc, "p0_load")
    ph.op("sync", lambda sy: sy.dma_start(out=xf[:], in_=x_f), inc=LD, k=16)
    ph.op("sync", lambda sy: sy.dma_start(out=p0stg[:, 0:512], in_=at_m), inc=LD, k=16)
    ph.op("sync", lambda sy: sy.dma_start(out=p0stg[:, 512:1024], in_=a_m), inc=LD, k=16)
    ph.op("sync", lambda sy: sy.dma_start(out=p0stg[:, 1024:3072], in_=bt_m), inc=LD, k=16)
    ph.op("sync", lambda sy: sy.dma_start(out=p0stg[:, 3072:5120], in_=ct_m), inc=LD, k=16)
    ph.op("sync", lambda sy: sy.dma_start(out=idf[:], in_=idf_m), inc=LD, k=16)
    ph.op("sync", lambda sy: sy.dma_start(out=idb[:], in_=idb_m), inc=LD, k=16)
    ph.op("sync", lambda sy: sy.dma_start(out=gam[:], in_=gam_m), inc=LD, k=16)
    ph.op("sync", lambda sy: sy.dma_start(out=bet[:], in_=bet_m), inc=LD, k=16)
    ph.op("sync", lambda sy: sy.dma_start(out=b1c[:], in_=b1_m), inc=LD, k=16)
    ph.op("sync", lambda sy: sy.dma_start(out=b2c[:], in_=b2_m), inc=LD, k=16)
    ldall = LD.n

    def _consts(vv):
        vv.memset(ones128b[:], 1.0)
        vv.memset(onescb[:], 1.0)
        vv.memset(onesc[:], 1.0)
        vv.memset(ones_r[:], 1.0)
        vv.memset(ones128f[:], 1.0)
        vv.memset(epspp[:], EPS)
        for kc in range(NKC):
            vv.tensor_copy(pw(1, kc), p0stg[:, kc * N:(kc + 1) * N])
            vv.tensor_copy(ladv(1, kc), p0stg[:, 512 + kc * N:512 + (kc + 1) * N])
            vv.tensor_copy(ctbc(kc), p0stg[:, 3072 + kc * D:3072 + (kc + 1) * D])
        for k in range(DC):
            vv.tensor_copy(btbc(k), p0stg[:, 1024 + k * N:1024 + (k + 1) * N])
        last = None
        for k in range(DC):
            last = vv.tensor_copy(xbc(k), xfc(k))
        return last
    ph.op("vector", _consts, waits=[(LD, ldall)], inc=VV)
    ph.emit()

    lad_state = {"vv": VV.n}

    def emit_ladder_stage(ph, c, pwf, rslot, cnt_target):
        if skip_ladder:
            return
        njs = min(c, cnt_target - c)
        for j in range(1, njs + 1):
            nj = c + j

            def mmf(tt, j=j, c=c, pwf=pwf, rslot=rslot):
                last = None
                for mc in range(2):
                    for kc in range(NKC):
                        last = tt.matmul(pbk[4][:, mc * 256:(mc + 1) * 256],
                                         pwf(c, kc)[:, mc * 128:(mc + 1) * 128],
                                         rslot(j, kc),
                                         start=(kc == 0), stop=(kc == NKC - 1))
                return last
            m1 = ph.op("tensor", mmf, waits=[(VV, lad_state["vv"])], inc=MM)

            def evf(vv, nj=nj, rslot=rslot):
                vv.tensor_copy(rslot(nj, 0), pbk[4][:, 0:256])
                return vv.tensor_copy(rslot(nj, 1), pbk[4][:, 256:512])
            e1 = ph.op("vector", evf, waits=[(MM, m1)], inc=VV)

            def tpf(tt, nj=nj, rslot=rslot):
                last = None
                for rc in range(2):
                    for sc in range(2):
                        last = tt.transpose(
                            pT[:, (2 * rc + sc) * 128:(2 * rc + sc + 1) * 128],
                            rslot(nj, rc)[:, sc * 128:(sc + 1) * 128], idb[:])
                return last
            m2 = ph.op("tensor", tpf, waits=[(VV, e1)], inc=MM)

            def pvf(vv, nj=nj, pwf=pwf):
                last = None
                for rc in range(2):
                    for sc in range(2):
                        last = vv.tensor_copy(
                            pwf(nj, sc)[:, rc * 128:(rc + 1) * 128],
                            pT[:, (2 * rc + sc) * 128:(2 * rc + sc + 1) * 128])
                return last
            lad_state["vv"] = ph.op("vector", pvf, waits=[(MM, m2)], inc=VV)

    def emit_ladder_phase():
        ph = Phase(nc, "p1_ladder")

        def rslot1(j, kc):
            return ladv(j, kc)

        c = 1
        while c < T1:
            emit_ladder_stage(ph, c, pw, rslot1, T1)
            c *= 2

        def pw2f(i, kc):
            if i == 1:
                return pw(T1, kc)
            return pw2(i, kc)

        def rslot2(j, kc):
            if j == 1:
                return ladv(16, kc)
            return ladv(8 + j, kc)

        def cp16(vv):
            vv.tensor_copy(pw2(1, 0), pw(T1, 0))
            return vv.tensor_copy(pw2(1, 1), pw(T1, 1))
        if not skip_ladder:
            lad_state["vv"] = ph.op("vector", cp16,
                                    waits=[(VV, lad_state["vv"])], inc=VV)

        c = 1
        while c < T2:
            emit_ladder_stage(ph, c, pw2f, rslot2, T2)
            c *= 2
        ph.emit()

    ph = Phase(nc, "p1_rowb")

    def rbmm(tt):
        last = None
        for k in range(DC):
            last = tt.matmul(pbk[6][0:1, 0:N], onescb[:], btbc(k),
                             start=(k == 0), stop=(k == DC - 1))
        return last
    m = ph.op("tensor", rbmm, waits=[(VV, VV.n)], inc=MM)
    ph.op("vector", lambda vv: vv.tensor_copy(rowbb[:], pbk[6][0:1, 0:N]),
          waits=[(MM, m)], inc=VV)
    ph.emit()

    S_MEAN, S_VAR, S_RSTD, S_NEGM, S_SCR, S_A, S_B, S_C = range(8)

    def emit_stats(ph, src_b16, src_f32, sqbuf, want_negm_row=False):
        def sqf(ss):
            last = None
            for k in range(DC):
                last = ss.activation(sqbuf[:, k * 512:(k + 1) * 512],
                                     src_f32(k), AF.Square)
            return last
        s1 = ph.op("scalar", sqf, waits=[(VV, VV.n), (MM, MM.n)], inc=SS)

        def summ(tt):
            last = None
            for k in range(DC):
                last = tt.matmul(pbk[0][:], ones128b[:], src_b16(k),
                                 start=(k == 0), stop=(k == DC - 1))
            return last
        ph.op("tensor", summ, waits=[(VV, VV.n), (SS, SS.n)], inc=MM)

        def sqm(tt):
            last = None
            for k in range(DC):
                last = tt.matmul(pbk[1][:], ones128b[:],
                                 sqbuf[:, k * 512:(k + 1) * 512],
                                 start=(k == 0), stop=(k == DC - 1))
            return last
        m2 = ph.op("tensor", sqm, waits=[(SS, s1)], inc=MM)

        def soup1(vv):
            vv.tensor_scalar_mul(st(S_MEAN), pbk[0][:], 1.0 / D)
            vv.tensor_scalar_mul(st(S_VAR), pbk[1][:], 1.0 / D)
            vv.tensor_mul(st(S_SCR), st(S_MEAN), st(S_MEAN))
            vv.tensor_sub(st(S_VAR), st(S_VAR), st(S_SCR))
            return vv.tensor_scalar_mul(st(S_NEGM), st(S_MEAN), -1.0)
        v1 = ph.op("vector", soup1, waits=[(MM, m2)], inc=VV)

        def rstdf(ss):
            ss.activation(st(S_SCR), st(S_VAR), AF.Ln, bias=epspp[:, 0:1])
            return ss.activation(st(S_RSTD), st(S_SCR), AF.Exp, scale=-0.5)
        s2 = ph.op("scalar", rstdf, waits=[(VV, v1)], inc=SS)
        v2 = v1
        if want_negm_row:
            v2 = ph.op("vector",
                       lambda vv: vv.tensor_copy(nmrow[:], st(S_NEGM)[0:1, :]),
                       inc=VV)
        return s2, v2

    if stop_after == "sq0":
        sqtest = nc.alloc_sbuf_tensor("sqtest", [128, 1024], BF16)
        ph = Phase(nc)
        def sq0f(ss):
            ss.activation(sq8_a[:, 0:512], xfc(0, slice(0, 512)), AF.Square)
            ss.activation(sqtest[:, 0:512], xfc(0, slice(0, 512)), AF.Square)
            return ss.activation(sqtest[:, 512:1024], scr1_f1[:, 0:512], AF.Square)
        def cpf(vv):
            return vv.tensor_copy(scr1_f1[:, 0:512], xfc(0, slice(0, 512)))
        v0 = ph.op("vector", cpf, waits=[(VV, VV.n)], inc=VV)
        s0 = ph.op("scalar", sq0f, waits=[(VV, v0)], inc=SS)
        ph.op("sync", lambda sy: sy.dma_start(out=sqdump, in_=sq8_a[:]),
              waits=[(SS, s0)], inc=STO, k=16)
        ph.op("sync", lambda sy: sy.dma_start(out=xbdump[:, 0:1024], in_=sqtest[:]),
              waits=[(SS, s0)], inc=STO, k=16)
        ph.op("sync", lambda sy: sy.dma_start(out=xfdump[:, 0:512], in_=scr1_f1[:, 0:512]),
              waits=[(SS, s0)], inc=STO, k=16)
        ph.emit()
        return nc, _emit_finish(nc, STO)

    def make_wq_pass1(ph, src_m, mu_pp, inv_pp):
        state = {"h": 0, "svals": []}

        def step(n):
            for _ in range(n):
                h = state["h"]
                if h >= 16:
                    return
                state["h"] += 1
                buf = wqbufA if h % 2 == 0 else wqbufB
                w = [(SS, state["svals"][h - 2])] if h >= 2 else []
                l = ph.op("sync",
                          lambda sy, h=h, buf=buf: sy.dma_start(
                              out=buf[:], in_=src_m[:, h * 2048:(h + 1) * 2048]),
                          waits=w, inc=LD, k=16)
                state["svals"].append(
                    ph.op("scalar",
                          lambda ss, h=h, buf=buf: ss.activation(
                              wqabs[:], buf[:], AF.Abs,
                              accum_out=acc16[:, h:h + 1]),
                          waits=[(LD, l)], inc=SS))

        def finish():
            step(16)
            v1 = ph.op("vector",
                       lambda vv: vv.tensor_copy(acc16b[:], acc16[:]),
                       waits=[(SS, state["svals"][-1])], inc=VV)
            m1 = ph.op("tensor",
                       lambda tt: tt.matmul(pbk[6][:, 0:16], ones128b[:],
                                            acc16b[:], start=True, stop=True),
                       waits=[(VV, v1)], inc=MM)
            v2 = ph.op("vector",
                       lambda vv: vv.reduce_sum(inv_pp[:], pbk[6][:, 0:16],
                                                axis=AX.X),
                       waits=[(MM, m1)], inc=VV)
            v3 = ph.op("vector",
                       lambda vv: vv.tensor_scalar(mu_pp[:], inv_pp[:],
                                                   1.0 / (1024.0 * 4096.0), 1e-5,
                                                   ALU.mult, ALU.max),
                       waits=[(VV, v2)], inc=VV)
            ph.op("vector", lambda vv: vv.reciprocal(inv_pp[:], mu_pp[:]),
                  waits=[(VV, v3)], inc=VV)

        return step, finish


    ph = Phase(nc, "p2_lnub")
    for tt_i in range(NTT):
        cs = slice(tt_i * 512, (tt_i + 1) * 512)
        sstat, vstat = emit_stats(ph,
                                  (lambda k, cs=cs: xbc(k, cs)),
                                  (lambda k, cs=cs: xfc(k, cs)),
                                  sq8_a, want_negm_row=True)

        def ubmm(tt, cs=cs):
            last = None
            for n_ in range(NKC):
                for k in range(DC):
                    tt.matmul(pbk[2 + n_][:], btbc(k)[:, n_ * 128:(n_ + 1) * 128],
                              xbc(k, cs), start=(k == 0), stop=False)
                last = tt.matmul(pbk[2 + n_][:], rowbb[0:1, n_ * 128:(n_ + 1) * 128],
                                 nmrow[:], start=False, stop=True)
            return last
        m3 = ph.op("tensor", ubmm, waits=[(VV, vstat)], inc=MM)

        def ubev(vv, cs=cs):
            vv.tensor_mul(uc(0, cs), pbk[2][:], st(S_RSTD))
            return vv.tensor_mul(uc(1, cs), pbk[3][:], st(S_RSTD))
        ph.op("vector", ubev, waits=[(MM, m3), (SS, sstat)], inc=VV)
    ph.emit()
    emit_ladder_phase()
    if dbg:
        ph = Phase(nc)
        ph.op("sync", lambda sy: sy.dma_start(out=udump, in_=u_t[:]),
              waits=[(VV, VV.n)], inc=STO, k=16)
        ph.op("sync", lambda sy: sy.dma_start(out=statdump, in_=stat[:]),
              inc=STO, k=16)
        ph.op("sync", lambda sy: sy.dma_start(out=sqdump, in_=sq8_a[:]),
              inc=STO, k=16)
        ph.op("sync", lambda sy: sy.dma_start(out=xbdump, in_=xb[:]),
              inc=STO, k=16)
        ph.op("sync", lambda sy: sy.dma_start(out=xfdump, in_=xf[:]),
              inc=STO, k=16)
        ph.emit()
    if stop_after == "p2":
        return nc, _emit_finish(nc, STO)

    ph = Phase(nc, "p3_scan")
    wq1_step, wq1_fin = make_wq_pass1(ph, w1t_m, mu1pp, invmu1)
    wq1_step(16)

    def s1f(vv):
        vv.tensor_copy(slocv(0, 0), uc(0, slice(0, T, T1)))
        return vv.tensor_copy(slocv(1, 0), uc(1, slice(0, T, T1)))
    vprev = ph.op("vector", s1f, waits=[(VV, VV.n)], inc=VV)

    for i in range(2, T1 + 1):
        bank = pbk[i % 2]

        def stepmm(tt, i=i, bank=bank):
            last = None
            for n_ in range(NKC):
                for kc in range(NKC):
                    last = tt.matmul(bank[:, n_ * 128:(n_ + 1) * 128],
                                     pw(1, kc)[:, n_ * 128:(n_ + 1) * 128],
                                     slocv(kc, i - 2),
                                     start=(kc == 0), stop=(kc == NKC - 1))
            return last
        m = ph.op("tensor", stepmm, waits=[(VV, vprev)], inc=MM)

        def stepev(vv, i=i, bank=bank):
            vv.tensor_add(slocv(0, i - 1), bank[:, 0:128], uc(0, slice(i - 1, T, T1)))
            return vv.tensor_add(slocv(1, i - 1), bank[:, 128:256],
                                 uc(1, slice(i - 1, T, T1)))
        vprev = ph.op("vector", stepev, waits=[(MM, m)], inc=VV)

    def y1f(vv):
        vv.tensor_copy(y2v(0, 1), slocv(0, T1 - 1, slice(0, 128, T2)))
        return vv.tensor_copy(y2v(1, 1), slocv(1, T1 - 1, slice(0, 128, T2)))
    vprev = ph.op("vector", y1f, inc=VV)
    for j in range(2, T2 + 1):
        bank = pbk[j % 2]

        def l2mm(tt, j=j, bank=bank):
            last = None
            for n_ in range(NKC):
                for kc in range(NKC):
                    last = tt.matmul(bank[:, n_ * 128:n_ * 128 + NG],
                                     pw2(1, kc)[:, n_ * 128:(n_ + 1) * 128],
                                     y2v(kc, j - 1),
                                     start=(kc == 0), stop=(kc == NKC - 1))
            return last
        m = ph.op("tensor", l2mm, waits=[(VV, vprev)], inc=MM)

        def l2ev(vv, j=j, bank=bank):
            vv.tensor_add(y2v(0, j), bank[:, 0:NG],
                          slocv(0, T1 - 1, slice(j - 1, 128, T2)))
            return vv.tensor_add(y2v(1, j), bank[:, 128:128 + NG],
                                 slocv(1, T1 - 1, slice(j - 1, 128, T2)))
        vprev = ph.op("vector", l2ev, waits=[(MM, m)], inc=VV)

    def gz(vv):
        vv.memset(gcol(0, 0), 0.0)
        return vv.memset(gcol(1, 0), 0.0)
    vg = ph.op("vector", gz, inc=VV)
    for g in range(NG):
        bank = pbk[g % 2]

        def l3mm(tt, g=g, bank=bank):
            last = None
            for n_ in range(NKC):
                for kc in range(NKC):
                    last = tt.matmul(bank[:, n_ * 128:n_ * 128 + 1],
                                     pw2(T2, kc)[:, n_ * 128:(n_ + 1) * 128],
                                     gcol(kc, g),
                                     start=(kc == 0), stop=(kc == NKC - 1))
            return last
        m = ph.op("tensor", l3mm, waits=[(VV, vg)], inc=MM)

        def l3ev(vv, g=g, bank=bank):
            vv.tensor_add(gcol(0, g + 1), bank[:, 0:1], y2v(0, T2, slice(g, g + 1)))
            return vv.tensor_add(gcol(1, g + 1), bank[:, 128:129],
                                 y2v(1, T2, slice(g, g + 1)))
        vg = ph.op("vector", l3ev, waits=[(MM, m)], inc=VV)
    vprev = vg

    def ecopyf(vv):
        vv.memset(ev(0, slice(0, 1)), 0.0)
        vv.memset(ev(1, slice(0, 1)), 0.0)
        vv.tensor_copy(ev(0, slice(T2, NCH, T2)),
                       g0col[:, 1:NG])
        return vv.tensor_copy(ev(1, slice(T2, NCH, T2)),
                              g0col[:, (NG + 1) + 1:(NG + 1) + NG])
    vprev3 = ph.op("vector", ecopyf, waits=[(VV, vprev)], inc=VV)
    for j in range(1, T2):
        bank = pbk[2 + (j % 2)]

        def c2mm(tt, j=j, bank=bank):
            last = None
            for n_ in range(NKC):
                for kc in range(NKC):
                    last = tt.matmul(bank[:, n_ * 128:n_ * 128 + NG],
                                     pw2(j, kc)[:, n_ * 128:(n_ + 1) * 128],
                                     g0col[:, kc * (NG + 1):kc * (NG + 1) + NG],
                                     start=(kc == 0), stop=(kc == NKC - 1))
            return last
        m = ph.op("tensor", c2mm, waits=[(VV, vprev3)], inc=MM)

        def c2ev(vv, j=j, bank=bank):
            vv.tensor_add(ev(0, slice(j, NCH, T2)), bank[:, 0:NG], y2v(0, j))
            return vv.tensor_add(ev(1, slice(j, NCH, T2)), bank[:, 128:128 + NG],
                                 y2v(1, j))
        vprev3 = ph.op("vector", c2ev, waits=[(MM, m)], inc=VV)

    vfin = vprev3
    for i in range(1, T1 + 1):
        bank = pbk[4 + (i % 2)]

        def fmm(tt, i=i, bank=bank):
            last = None
            for n_ in range(NKC):
                for kc in range(NKC):
                    last = tt.matmul(bank[:, n_ * 128:(n_ + 1) * 128],
                                     pw(i, kc)[:, n_ * 128:(n_ + 1) * 128],
                                     ev(kc, slice(0, NCH)),
                                     start=(kc == 0), stop=(kc == NKC - 1))
            return last
        m = ph.op("tensor", fmm, waits=[(VV, vfin)], inc=MM)

        def fev(vv, i=i, bank=bank):
            vv.tensor_add(stcv(0, slice(i - 1, T, T1)), bank[:, 0:128],
                          slocv(0, i - 1))
            return vv.tensor_add(stcv(1, slice(i - 1, T, T1)), bank[:, 128:256],
                                 slocv(1, i - 1))
        vfin = ph.op("vector", fev, waits=[(MM, m)], inc=VV)
    wq1_fin()
    ph.emit()
    if dbg:
        ph = Phase(nc)
        ph.op("sync", lambda sy: sy.dma_start(out=sdump, in_=states[:]),
              waits=[(VV, VV.n)], inc=STO, k=16)
        ph.emit()
    if stop_after == "scan":
        return nc, _emit_finish(nc, STO)

    spill_sto = []
    ph = Phase(nc, "p4_epi")
    wq2_step, wq2_fin = make_wq_pass1(ph, w2t_m, mu2pp, invmu2)
    for tt_i in range(NTT):
        wq2_step(4)
        cs = slice(tt_i * 512, (tt_i + 1) * 512)
        sgel = []
        for dc in range(DC):
            bank = pbk[dc % 3]
            w = [(VV, VV.n)] if dc < 3 else [(SS, sgel[dc - 3])]

            def ymm(tt, dc=dc, cs=cs, bank=bank):
                last = None
                for kc in range(NKC):
                    last = tt.matmul(bank[:], ctbc(kc)[:, dc * 128:(dc + 1) * 128],
                                     stcv(kc, cs), start=(kc == 0),
                                     stop=(kc == NKC - 1))
                return last
            m = ph.op("tensor", ymm, waits=w, inc=MM)
            sgel.append(ph.op("scalar",
                              lambda ss, dc=dc, bank=bank:
                              ss.activation(att(dc), bank[:], AF.Gelu),
                              waits=[(MM, m)], inc=SS))

        def abf(vv):
            last = None
            for k in range(DC):
                last = vv.tensor_copy(x1bb_p5[:, k * 512:(k + 1) * 512], att(k))
            return last
        ph.op("vector", abf, waits=[(SS, sgel[-1])], inc=VV)
        sstat, vstat = emit_stats(ph, lambda k: x1bb_p5[:, k * 512:(k + 1) * 512],
                                  lambda k: att(k), sq8_a)

        vap = None
        for k in range(DC):
            def ap1(vv, k=k):
                vv.tensor_add(att(k), att(k), st(S_NEGM))
                return vv.tensor_mul(att(k), att(k), st(S_RSTD))
            v1 = ph.op("vector", ap1, waits=[(VV, vstat), (SS, sstat)], inc=VV)
            s1 = ph.op("scalar",
                       lambda ss, k=k: ss.activation(att(k), att(k), AF.Identity,
                                                     bias=bet[:, k:k + 1],
                                                     scale=gam[:, k:k + 1]),
                       waits=[(VV, v1)], inc=SS)
            vap = ph.op("vector",
                        lambda vv, k=k: vv.tensor_scalar(att(k), att(k),
                                                         CLAMP, -CLAMP,
                                                         ALU.min, ALU.max),
                        waits=[(SS, s1)], inc=VV)

        def abf2(vv):
            last = None
            for k in range(DC):
                last = vv.tensor_copy(x1bb_p5[:, k * 512:(k + 1) * 512], att(k))
            return last
        ph.op("vector", abf2, waits=[(VV, vap)], inc=VV)
        sstat2, vstat2 = emit_stats(ph, lambda k: x1bb_p5[:, k * 512:(k + 1) * 512],
                                    lambda k: att(k), sq8_a)

        vr = None
        for k in range(DC):
            def ap2(vv, k=k, cs=cs):
                vv.tensor_add(att(k), att(k), st(S_NEGM))
                vv.tensor_mul(att(k), att(k), st(S_RSTD))
                return vv.tensor_add(xfc(k, cs), att(k), xfc(k, cs))
            vr = ph.op("vector", ap2, waits=[(VV, vstat2), (SS, sstat2)], inc=VV)

        spill_sto.append(
            ph.op("sync",
                  lambda sy, cs=cs: sy.dma_start(
                      out=x1s.rearrange("p (k t) -> p k t", k=DC)[:, :, cs],
                      in_=xf.ap().rearrange("p (k t) -> p k t", k=DC)[:, :, cs]),
                  waits=[(VV, vr)], inc=STO, k=16))
    wq2_fin()
    ph.emit()
    if stop_after == "attn":
        return nc, _emit_finish(nc, STO)

    def emit_wq_pass1(src_m, mu_pp, inv_pp):
        ph = Phase(nc, "wq_p1")
        svals = []
        for ch in range(8):
            buf = wchunk if ch % 2 == 0 else wchunk2
            w = [(SS, svals[ch - 2])] if ch >= 2 else [(STO, STO.n)]
            l = ph.op("sync",
                      lambda sy, ch=ch, buf=buf: sy.dma_start(
                          out=buf[:], in_=src_m[:, ch * 4096:(ch + 1) * 4096]),
                      waits=w, inc=LD, k=16)
            svals.append(ph.op("scalar",
                               lambda ss, ch=ch, buf=buf: ss.activation(
                                   absdump[:], buf[:], AF.Abs,
                                   accum_out=acc8[:, ch:ch + 1]),
                               waits=[(LD, l)], inc=SS))
        v1 = ph.op("vector",
                   lambda vv: vv.tensor_copy(acc8b[:], acc8[:]),
                   waits=[(SS, svals[-1])], inc=VV)
        m1 = ph.op("tensor",
                   lambda tt: tt.matmul(pbk[6][:, 0:8], ones128b[:], acc8b[:],
                                        start=True, stop=True),
                   waits=[(VV, v1)], inc=MM)

        v2 = ph.op("vector",
                   lambda vv: vv.reduce_sum(inv_pp[:], pbk[6][:, 0:8], axis=AX.X),
                   waits=[(MM, m1)], inc=VV)
        v3 = ph.op("vector",
                   lambda vv: vv.tensor_scalar(mu_pp[:], inv_pp[:],
                                               1.0 / (1024.0 * 4096.0), 1e-5,
                                               ALU.mult, ALU.max),
                   waits=[(VV, v2)], inc=VV)
        ph.op("vector", lambda vv: vv.reciprocal(inv_pp[:], mu_pp[:]),
              waits=[(VV, v3)], inc=VV)
        ph.emit()

    def emit_wq_pass2(src_m, dst_cols, inv_pp, first_waits=()):
        ph = Phase(nc, "wq_p2")
        qvals = []
        for ch in range(8):
            buf2 = wchunk if ch % 2 == 0 else wchunk2
            w = [(VV, qvals[ch - 2])] if ch >= 2 else list(first_waits) + [(VV, VV.n)]
            l = ph.op("sync",
                      lambda sy, ch=ch, buf2=buf2: sy.dma_start(
                          out=buf2[:], in_=src_m[:, ch * 4096:(ch + 1) * 4096]),
                      waits=w, inc=LD, k=16)

            def qf(vv, ch=ch, buf2=buf2):
                vv.tensor_scalar_mul(wcint[:], buf2[:], inv_pp[:, 0:1])
                return vv.tensor_scalar(dst_cols(ch), wcint[:], 1.0, -1.0,
                                        ALU.min, ALU.max)
            qvals.append(ph.op("vector", qf, waits=[(LD, l)], inc=VV))
        ph.emit()

    emit_wq_pass2(w1t_m, lambda ch: w1t[:, ch * 4096:(ch + 1) * 4096], invmu1,
                  first_waits=[(STO, spill_sto[-1])])
    if stop_after == "wq":
        return nc, _emit_finish(nc, STO)

    q2_sto = []
    ph = Phase(nc, "f1")
    x1_lds = []
    x1_lastuse = []
    gdma = None
    for tt_i in range(NTT):
        cs = slice(tt_i * 512, (tt_i + 1) * 512)
        xbuf = x1t if tt_i % 2 == 0 else x1t2

        def x1tc_f1(k, xbuf=xbuf):
            return xbuf[:, k * 512:(k + 1) * 512]

        wld = [(STO, spill_sto[tt_i])]
        if tt_i >= 2:
            wld.append((VV, x1_lastuse[tt_i - 2]))
        l = ph.op("sync",
                  lambda sy, cs=cs, xbuf=xbuf: sy.dma_start(
                      out=xbuf[:],
                      in_=x1s.rearrange("p (k t) -> p k t", k=DC)[:, :, cs]),
                  waits=wld, inc=LD, k=16)
        x1_lds.append(l)

        def xbf(ss, x1tc=x1tc_f1):
            last = None
            for k in range(DC):
                last = ss.copy(x1bb_f1[:, k * 512:(k + 1) * 512], x1tc(k))
            return last
        sxb = ph.op("scalar", xbf, waits=[(LD, l)], inc=SS)
        emit_stats(ph, lambda k, x1tc=x1tc_f1: x1bb_f1[:, k * 512:(k + 1) * 512] if False else x1bb_f1[:, k * 512:(k + 1) * 512],
                   lambda k, x1tc=x1tc_f1: x1tc(k), sq8_f1)

        scen = []
        for k in range(DC):
            sl = scr1_f1[:, (k % 2) * 512:((k % 2) + 1) * 512]
            slb = scrb[:, (k % 2) * 512:((k % 2) + 1) * 512]
            w = [(SS, scen[k - 2])] if k >= 2 else [(VV, VV.n)]
            v = ph.op("vector",
                      lambda vv, k=k, sl=sl, x1tc=x1tc_f1:
                      vv.tensor_add(sl, x1tc(k), st(S_NEGM)),
                      waits=w, inc=VV)
            scen.append(ph.op("scalar",
                              lambda ss, sl=sl, slb=slb:
                              ss.activation(slb, sl, AF.Square),
                              waits=[(VV, v)], inc=SS))
            if k == 0:
                ph.op("vector", lambda vv: vv.tensor_copy(rmax[:], scrb[:, 0:512]),
                      waits=[(SS, scen[0])], inc=VV)
            else:
                ph.op("vector",
                      lambda vv, slb=slb: vv.tensor_max(rmax[:], rmax[:], slb),
                      waits=[(SS, scen[k])], inc=VV)
        vrm = VV.n

        def amax_chain(tag):
            def tpf(tt):
                last = None
                for j in range(4):
                    last = tt.transpose(pT[:, j * 128:(j + 1) * 128],
                                        rmax[:, j * 128:(j + 1) * 128], idb[:])
                return last
            m = ph.op("tensor", tpf, waits=[(VV, VV.n)], inc=MM)

            def redf(vv):
                last = None
                for j in range(4):
                    last = vv.reduce_max(mcol[:, j:j + 1],
                                         pT[:, j * 128:(j + 1) * 128], axis=AX.X)
                return last
            v = ph.op("vector", redf, waits=[(MM, m)], inc=VV)

            def mtpf(tt):
                last = None
                for j in range(4):
                    last = tt.transpose(pbk[6][0:1, j * 128:(j + 1) * 128],
                                        mcol[:, j:j + 1], idf[:])
                return last
            m = ph.op("tensor", mtpf, waits=[(VV, v)], inc=MM)
            v = ph.op("vector",
                      lambda vv: vv.tensor_copy(mrow[:], pbk[6][0:1, 0:512]),
                      waits=[(MM, m)], inc=VV)
            m = ph.op("tensor",
                      lambda tt: tt.matmul(pbk[5][:], ones_r[:], mrow[:],
                                           start=True, stop=True),
                      waits=[(VV, v)], inc=MM)
            return ph.op("scalar",
                         lambda ss: ss.activation(st(S_A), pbk[5][:], AF.Identity),
                         waits=[(MM, m)], inc=SS)

        s = amax_chain("a1")

        def soup2(vv):
            vv.tensor_mul(st(S_SCR), st(S_VAR), st(S_RSTD))
            return vv.tensor_mul(st(S_SCR), st(S_SCR), st(S_RSTD))
        v = ph.op("vector", soup2, waits=[(SS, s)], inc=VV)

        def invr1f(ss):
            ss.activation(st(S_B), st(S_SCR), AF.Ln, bias=epspp[:, 0:1])
            return ss.activation(st(S_B), st(S_B), AF.Exp, scale=-0.5)
        s = ph.op("scalar", invr1f, waits=[(VV, v)], inc=SS)

        def soup3a(vv):
            vv.tensor_mul(st(S_A), st(S_A), st(S_RSTD))
            vv.tensor_mul(st(S_A), st(S_A), st(S_RSTD))
            vv.tensor_mul(st(S_A), st(S_A), st(S_B))
            vv.tensor_mul(st(S_A), st(S_A), st(S_B))
            return vv.tensor_scalar_max(st(S_A), st(S_A), 1e-10)
        v = ph.op("vector", soup3a, waits=[(SS, s)], inc=VV)

        def soup3s(ss):
            ss.activation(st(S_MEAN), st(S_A), AF.Ln)
            ss.activation(st(S_A), st(S_MEAN), AF.Exp, scale=0.5)
            return ss.activation(st(S_MEAN), st(S_MEAN), AF.Exp, scale=-0.5)
        s = ph.op("scalar", soup3s, waits=[(VV, v)], inc=SS)

        def soup3b(vv):
            vv.tensor_scalar(st(S_C), st(S_A), mu1pp[:, 0:1], 1.0 / 127.0,
                             ALU.mult, ALU.mult)
            vv.tensor_mul(st(S_B), st(S_B), st(S_MEAN))
            vv.tensor_scalar_mul(st(S_B), st(S_B), 127.0)
            vv.tensor_mul(st(S_B), st(S_B), st(S_RSTD))
            return vv.tensor_mul(st(S_SCR), st(S_NEGM), st(S_B))
        v = ph.op("vector", soup3b, waits=[(SS, s)], inc=VV)

        for k in range(DC):
            def q1f(vv, k=k, x1tc=x1tc_f1):
                vv.tensor_mul(scr1_f1[:, 0:512], x1tc(k), st(S_B))
                vv.tensor_tensor(scri_f1[:, 512:1024], scr1_f1[:, 0:512],
                                 st(S_SCR), ALU.add)
                return vv.tensor_mul(q1bc(k), scri_f1[:, 512:1024], st(S_C))
            v = ph.op("vector", q1f, inc=VV)
        vq1 = v
        x1_lastuse.append(vq1)

        sgel = []
        ssqs = []
        vmaxs = []
        sqmms = []

        def emit_sqmm(j):
            slbj = scrb[:, (j % 2) * 512:((j % 2) + 1) * 512]
            sqmms.append(ph.op(
                "tensor",
                lambda tt, slbj=slbj, j=j: tt.matmul(
                    pbk[4][:], ones128b[:], slbj,
                    start=(j == 0), stop=(j == FC - 1)),
                waits=[(SS, ssqs[j])], inc=MM))

        for fc in range(FC):
            bank = pbk[fc % 4]
            w = [(VV, vq1)]
            if fc >= 4:
                w.append((SS, ssqs[fc - 4]))

            def m1f(tt, fc=fc, bank=bank):
                last = None
                for k in range(DC):
                    last = tt.matmul(
                        bank[:], w1t[:, k * FF + fc * 128:k * FF + (fc + 1) * 128],
                        q1bc(k), start=(k == 0), stop=(k == DC - 1))
                return last
            m = ph.op("tensor", m1f, waits=w, inc=MM)
            if fc >= 2:
                emit_sqmm(fc - 2)
            sg = ph.op("scalar",
                       lambda ss, fc=fc, bank=bank: ss.activation(
                           gtc(fc), bank[:], AF.Gelu, bias=b1c[:, fc:fc + 1]),
                       waits=[(MM, m)], inc=SS)
            sgel.append(sg)
            slb = scrb[:, (fc % 2) * 512:((fc % 2) + 1) * 512]
            wsq = ([(VV, vmaxs[fc - 2]), (MM, sqmms[fc - 2])]
                   if fc >= 2 else [])
            ssq = ph.op("scalar",
                        lambda ss, fc=fc, slb=slb:
                        ss.activation(slb, gtc(fc), AF.Square),
                        waits=wsq, inc=SS)
            ssqs.append(ssq)
            if fc == 0:
                vg = ph.op("vector",
                           lambda vv: vv.tensor_copy(rmax[:], scrb[:, 0:512]),
                           waits=[(SS, ssq)], inc=VV)
            else:
                vg = ph.op("vector",
                           lambda vv, slb=slb: vv.tensor_max(rmax[:], rmax[:], slb),
                           waits=[(SS, ssq)], inc=VV)
            vmaxs.append(vg)
        emit_sqmm(FC - 2)
        emit_sqmm(FC - 1)

        v = ph.op("vector",
                  lambda vv: vv.tensor_scalar_mul(st(S_VAR), pbk[4][:], 1.0 / FF),
                  waits=[(MM, sqmms[-1])], inc=VV)

        def invr2f(ss):
            ss.activation(st(S_B), st(S_VAR), AF.Ln, bias=epspp[:, 0:1])
            return ss.activation(st(S_B), st(S_B), AF.Exp, scale=-0.5)
        s_r2 = ph.op("scalar", invr2f, waits=[(VV, v)], inc=SS)
        s = amax_chain("a2")

        def soup4a(vv):
            vv.tensor_mul(st(S_A), st(S_A), st(S_B))
            vv.tensor_mul(st(S_A), st(S_A), st(S_B))
            return vv.tensor_scalar_max(st(S_A), st(S_A), 1e-10)
        v = ph.op("vector", soup4a, waits=[(SS, s)], inc=VV)

        def soup4s(ss):
            ss.activation(st(S_MEAN), st(S_A), AF.Ln)
            ss.activation(st(S_A), st(S_MEAN), AF.Exp, scale=0.5)
            return ss.activation(st(S_MEAN), st(S_MEAN), AF.Exp, scale=-0.5)
        s = ph.op("scalar", soup4s, waits=[(VV, v)], inc=SS)

        def soup4b(vv):
            vv.tensor_scalar(st(S_C), st(S_A), mu2pp[:, 0:1], 1.0 / 127.0,
                             ALU.mult, ALU.mult)
            vv.tensor_mul(st(S_B), st(S_B), st(S_MEAN))
            return vv.tensor_scalar_mul(st(S_B), st(S_B), 127.0)
        v = ph.op("vector", soup4b, waits=[(SS, s)], inc=VV)
        c2sto = ph.op("sync",
                      lambda sy, cs=cs: sy.dma_start(out=c2row_t.ap()[0:1, cs],
                                                     in_=st(S_C)[0:1, :]),
                      waits=[(VV, v)], inc=STO, k=16)
        q2_sto.append(c2sto)

        if gdma is None:
            gdma = c2sto
        for fcg in range(4):
            packs = []
            for jj in range(8):
                fc = fcg * 8 + jj
                si = scri_f1[:, (fc % 2) * 512:((fc % 2) + 1) * 512]
                ph.op("vector",
                      lambda vv, fc=fc, si=si:
                      vv.tensor_mul(si, gtc(fc), st(S_B)),
                      waits=[], inc=VV)
                wp = [(STO, gdma)] if jj == 0 else []
                packs.append(ph.op("vector",
                                   lambda vv, jj=jj, si=si: vv.tensor_copy(
                                       q2st[:, jj * 512:(jj + 1) * 512], si),
                                   waits=wp, inc=VV))
            gdma = ph.op("sync",
                         lambda sy, fcg=fcg, cs=cs: sy.dma_start(
                             out=bass.AP(q2s_t, 8 * fcg * T + cs.start,
                                         [[FC * T, 128], [T, 8], [1, 512]]),
                             in_=q2st.ap().rearrange("p (c t) -> p c t", c=8)),
                         waits=[(VV, packs[-1])], inc=STO, k=16)
            q2_sto.append(gdma)
    ph.emit()

    if dbg:
        ph = Phase(nc)
        ph.op("sync", lambda sy: sy.dma_start(out=gdump, in_=g_t[:]),
              waits=[(VV, VV.n), (SS, SS.n), (PP, PP.n)], inc=STO, k=16)
        ph.op("sync", lambda sy: sy.dma_start(out=q1dump, in_=q1b[:]),
              inc=STO, k=16)
        ph.op("sync", lambda sy: sy.dma_start(out=wqdump[:, 0:8], in_=acc8[:]),
              inc=STO, k=16)
        ph.op("sync", lambda sy: sy.dma_start(out=wqdump[:, 8:12], in_=mcol[:]),
              inc=STO, k=16)
        ph.op("sync", lambda sy: sy.dma_start(out=wqdump[:, 12:13], in_=mu1pp[:], allow_slow_non_contiguous=True),
              inc=STO, k=16)
        ph.op("sync", lambda sy: sy.dma_start(out=wqdump[:, 13:14], in_=mu2pp[:], allow_slow_non_contiguous=True),
              inc=STO, k=16)
        ph.op("sync", lambda sy: sy.dma_start(out=wqdump[:, 14:15], in_=invmu1[:], allow_slow_non_contiguous=True),
              inc=STO, k=16)
        ph.op("sync", lambda sy: sy.dma_start(out=wqdump[:, 15:16], in_=invmu2[:], allow_slow_non_contiguous=True),
              inc=STO, k=16)
        ph.op("sync", lambda sy: sy.dma_start(out=w1tdump, in_=w1t[:]),
              inc=STO, k=16)
        ph.op("sync", lambda sy: sy.dma_start(out=statdump, in_=stat[:]),
              inc=STO, k=16)
        ph.emit()
    if stop_after == "f1":
        return nc, _emit_finish(nc, STO)

    emit_wq_pass2(w2t_m, lambda ch: w2t[:, ch * 4096:(ch + 1) * 4096], invmu2)
    q2_done = STO.n

    ph = Phase(nc, "f2")

    def o2v(t, k):
        buf = o2stA if t % 2 == 0 else o2stB
        return buf[:, k * 512:(k + 1) * 512]

    ctxs = []
    prev_mm_last = None
    prev_sev_last = None
    prev_vr = None
    prev_outdma = None
    prev_evac_last = None

    def emit_head(tt_i):
        nonlocal prev_mm_last, prev_sev_last, prev_evac_last
        cs = slice(tt_i * 512, (tt_i + 1) * 512)
        wq2 = [(MM, prev_mm_last)] if prev_mm_last else [(STO, q2_done)]
        l = ph.op("sync",
                  lambda sy, cs=cs: sy.dma_start(
                      out=q2in[:].rearrange("p (c t) -> p c t", c=FC),
                      in_=bass.AP(q2s_t, cs.start, [[FC * T, 128], [T, FC], [1, 512]])),
                  waits=wq2, inc=LD, k=16)
        wc2 = [(VV, prev_evac_last)] if prev_evac_last else [(STO, q2_done)]
        l3 = ph.op("sync",
                   lambda sy, cs=cs: sy.dma_start(
                       out=st(S_C),
                       in_=bass.AP(c2row_t, cs.start, [[0, 128], [1, 512]])),
                   waits=wc2, inc=LD, k=16)
        sev = []
        for dc in range(DC):
            bank = pbk[dc % 3]
            w = [(LD, l)]
            if dc >= 3:
                w.append((SS, sev[dc - 3]))
            elif prev_sev_last is not None:
                w.append((SS, prev_sev_last))

            def m2f(tt, dc=dc, bank=bank):
                last = None
                for fc in range(FC):
                    last = tt.matmul(
                        bank[:], w2t[:, fc * D + dc * 128:fc * D + (dc + 1) * 128],
                        q2inc(fc), start=(fc == 0), stop=(fc == FC - 1))
                return last
            m = ph.op("tensor", m2f, waits=w, inc=MM)
            if dc == DC - 1:
                prev_mm_last = m
            slo = scr1_f2[:, (dc % 2) * 512:((dc % 2) + 1) * 512]
            wv = [(MM, m), (LD, l3)]
            if dc >= 2:
                wv.append((SS, sev[dc - 2]))
            v = ph.op("vector",
                      lambda vv, bank=bank, slo=slo: vv.tensor_mul(slo, bank[:],
                                                                   st(S_C)),
                      waits=wv, inc=VV)
            if dc == DC - 1:
                prev_evac_last = v
            sev.append(ph.op("scalar",
                             lambda ss, dc=dc, tt_i=tt_i, slo=slo: ss.activation(
                                 o2v(tt_i, dc), slo,
                                 AF.Identity, bias=b2c[:, dc:dc + 1]),
                             waits=[(VV, v)], inc=SS))
        prev_sev_last = sev[-1]
        ctxs.append((tt_i, cs, sev))

    def emit_tail():
        nonlocal prev_vr, prev_outdma
        tt_i, cs, sev = ctxs.pop(0)
        wx1 = [(VV, prev_vr)] if prev_vr else [(STO, q2_done)]
        l2 = ph.op("sync",
                   lambda sy, cs=cs: sy.dma_start(
                       out=x1t[:],
                       in_=x1s.rearrange("p (k t) -> p k t", k=DC)[:, :, cs]),
                   waits=wx1, inc=LD, k=16)
        sstat3, _ = emit_stats(ph, lambda k: o2v(tt_i, k),
                               lambda k: o2v(tt_i, k), sq8_f2)
        vr = None
        for k in range(DC):
            wap = [(LD, l2), (VV, VV.n), (SS, sstat3)]
            if k == 0 and prev_outdma is not None:
                wap.append((STO, prev_outdma))

            def ap3(vv, k=k, tt_i=tt_i):
                vv.tensor_add(scr1_f2[:, 0:512], o2v(tt_i, k), st(S_NEGM))
                vv.tensor_mul(scr1_f2[:, 0:512], scr1_f2[:, 0:512], st(S_RSTD))
                return vv.tensor_add(outst[:, k * 512:(k + 1) * 512],
                                     scr1_f2[:, 0:512], x1tc(k))
            vr = ph.op("vector", ap3, waits=wap, inc=VV)
        prev_vr = vr
        prev_outdma = ph.op("sync",
                            lambda sy, cs=cs: sy.dma_start(
                                out=out_m.rearrange("p (k t) -> p k t", k=DC)[:, :, cs],
                                in_=outst.ap().rearrange("p (k t) -> p k t", k=DC)),
                            waits=[(VV, vr)], inc=STO, k=16)

    for tt_i in range(NTT):
        emit_head(tt_i)
        if tt_i >= 1:
            emit_tail()
    emit_tail()
    ph.emit()
    return nc, _emit_finish(nc, STO)


def _emit_finish(nc, STO):
    with nc.Block() as blk:
        @blk.sync
        def _(sy):
            sy.wait_ge(STO.h, STO.n)
    return STO.n


def _tile_rows(a, p=128):
    a = np.ascontiguousarray(a)
    R, C = a.shape
    return np.ascontiguousarray(
        a.reshape(R // p, p, C).transpose(1, 0, 2).reshape(p, (R // p) * C))


def make_inputs(x, A, B, C, gam, bet, w1, b1, w2, b2):
    import ml_dtypes
    shared = {
        "at_m": _tile_rows(A.T),
        "a_m": _tile_rows(A),
        "bt_m": _tile_rows(B.T),
        "ct_m": _tile_rows(C.T),
        "gam_m": _tile_rows(gam.reshape(D, 1)),
        "bet_m": _tile_rows(bet.reshape(D, 1)),
        "w1t_m": _tile_rows(w1.T),
        "b1_m": _tile_rows(b1.reshape(FF, 1)),
        "w2t_m": _tile_rows(w2.T),
        "b2_m": _tile_rows(b2.reshape(D, 1)),
        "idf_m": np.eye(128, dtype=np.float32),
        "idb_m": np.eye(128, dtype=ml_dtypes.bfloat16),
    }
    maps = []
    for b in range(x.shape[0]):
        m = dict(shared)
        m["x_f"] = _tile_rows(x[b].T)
        maps.append(m)
    return maps


def kernel(L0_attn_x, L0_attn_A, L0_attn_B, L0_attn_C, L0_attn_gamma,
           L0_attn_beta, L0_ffn_w1, L0_ffn_b1, L0_ffn_w2, L0_ffn_b2):
    x = np.asarray(L0_attn_x, dtype=np.float32)
    in_maps = make_inputs(x,
                          np.asarray(L0_attn_A, np.float32),
                          np.asarray(L0_attn_B, np.float32),
                          np.asarray(L0_attn_C, np.float32),
                          np.asarray(L0_attn_gamma, np.float32),
                          np.asarray(L0_attn_beta, np.float32),
                          np.asarray(L0_ffn_w1, np.float32),
                          np.asarray(L0_ffn_b1, np.float32),
                          np.asarray(L0_ffn_w2, np.float32),
                          np.asarray(L0_ffn_b2, np.float32))
    nc, _ = build_program()
    res = run_bass_kernel_spmd(nc, in_maps, core_ids=list(range(8)))
    outs = []
    for b in range(8):
        o = res.results[b]["out"]
        o = o.reshape(128, DC, T).transpose(1, 0, 2).reshape(D, T)
        outs.append(np.ascontiguousarray(o.T))
    return np.stack(outs, axis=0)


# version 12
# speedup vs baseline: 1.2617x; 1.0897x over previous
import numpy as np
import concourse.bass as bass
import concourse.mybir as mybir
from concourse.bass_utils import run_bass_kernel_spmd

F32 = mybir.dt.float32
BF16 = mybir.dt.bfloat16
I32 = mybir.dt.int32
AF = mybir.ActivationFunctionType
ALU = mybir.AluOpType
AX = mybir.AxisListType

D = 1024
FF = 4096
T = 2048
N = 256
DC = D // 128
FC = FF // 128
NKC = N // 128
NTT = T // 512
EPS = 1e-5
CLAMP = 1e6

T1 = 16
NCH = T // T1
T2 = 8
NG = NCH // T2

KB = 1024
ENGS = ("sync", "tensor", "scalar", "vector", "gpsimd")


class Sem:
    def __init__(self, nc, name):
        self._cm = nc.semaphore(name)
        self.h = self._cm.__enter__()
        self.n = 0


class Phase:
    def __init__(self, nc, name=None):
        self.nc = nc
        self.name = name
        self.ops = {e: [] for e in ENGS}

    def op(self, eng, fn, waits=(), inc=None, k=1):
        self.ops[eng].append((fn, [(s.h, c) for (s, c) in waits if c > 0], inc, k))
        if inc is not None:
            inc.n += k
            return inc.n
        return None

    def emit(self):
        import contextlib
        scope = (self.nc.named_scope(self.name) if self.name
                 else contextlib.nullcontext())
        with scope, self.nc.Block() as blk:
            for name in ENGS:
                items = self.ops[name]
                if not items:
                    continue
                section = getattr(blk, name)

                @section
                def _(eng, items=items):
                    for fn, waits, inc, k in items:
                        for (h, c) in waits:
                            eng.wait_ge(h, c)
                        instr = fn(eng)
                        if inc is not None:
                            instr.then_inc(inc.h, k)


def build_program(dbg=False, stop_after=None, skip_ladder=False):
    nc = bass.Bass("TRN2", target_bir_lowering=False)

    x_f = nc.dram_tensor("x_f", [128, DC * T], F32, kind="ExternalInput").ap()
    at_m = nc.dram_tensor("at_m", [128, NKC * N], F32, kind="ExternalInput").ap()
    a_m = nc.dram_tensor("a_m", [128, NKC * N], F32, kind="ExternalInput").ap()
    bt_m = nc.dram_tensor("bt_m", [128, DC * N], F32, kind="ExternalInput").ap()
    ct_m = nc.dram_tensor("ct_m", [128, NKC * D], F32, kind="ExternalInput").ap()
    gam_m = nc.dram_tensor("gam_m", [128, DC], F32, kind="ExternalInput").ap()
    bet_m = nc.dram_tensor("bet_m", [128, DC], F32, kind="ExternalInput").ap()
    w1t_m = nc.dram_tensor("w1t_m", [128, DC * FF], F32, kind="ExternalInput").ap()
    b1_m = nc.dram_tensor("b1_m", [128, FC], F32, kind="ExternalInput").ap()
    w2t_m = nc.dram_tensor("w2t_m", [128, FC * D], F32, kind="ExternalInput").ap()
    b2_m = nc.dram_tensor("b2_m", [128, DC], F32, kind="ExternalInput").ap()
    idf_m = nc.dram_tensor("idf_m", [128, 128], F32, kind="ExternalInput").ap()
    idb_m = nc.dram_tensor("idb_m", [128, 128], BF16, kind="ExternalInput").ap()

    out_m = nc.dram_tensor("out", [128, DC * T], F32, kind="ExternalOutput").ap()
    x1s_t = nc.dram_tensor("x1s", [128, DC * T], F32,
                           kind="ExternalOutput" if dbg else "Internal")
    x1s = x1s_t.ap()
    q2s_t = nc.dram_tensor("q2s", [128, FC * T], BF16,
                           kind="ExternalOutput" if dbg else "Internal")
    c2row_t = nc.dram_tensor("c2row", [1, T], F32,
                             kind="ExternalOutput" if dbg else "Internal")
    if dbg:
        udump = nc.dram_tensor("udump", [128, NKC * T], F32, kind="ExternalOutput").ap()
        sdump = nc.dram_tensor("sdump", [128, NKC * T], BF16, kind="ExternalOutput").ap()
        statdump = nc.dram_tensor("statdump", [128, 8 * 512], F32, kind="ExternalOutput").ap()
        sqdump = nc.dram_tensor("sqdump", [128, DC * 512], BF16, kind="ExternalOutput").ap()
        xbdump = nc.dram_tensor("xbdump", [128, DC * T], BF16, kind="ExternalOutput").ap()
        xfdump = nc.dram_tensor("xfdump", [128, DC * T], F32, kind="ExternalOutput").ap()
        gdump = nc.dram_tensor("gdump", [128, FC * 512], BF16, kind="ExternalOutput").ap()
        wqdump = nc.dram_tensor("wqdump", [128, 16], F32, kind="ExternalOutput").ap()
        w1tdump = nc.dram_tensor("w1tdump", [128, DC * FF], BF16, kind="ExternalOutput").ap()
        q1dump = nc.dram_tensor("q1dump", [128, DC * 512], BF16, kind="ExternalOutput").ap()
        w2tdump = nc.dram_tensor("w2tdump", [128, FC * D], BF16, kind="ExternalOutput").ap()
        o2dump = nc.dram_tensor("o2dump", [128, DC * 512], F32, kind="ExternalOutput").ap()

    A0, B0, C0, D0 = 18 * KB, 82 * KB, 146 * KB, 194 * KB

    def at_(name, shape, dt, off):
        return nc.alloc_sbuf_tensor_at(name, shape, dt, offset=off)

    xf = at_("xf", [128, DC * T], F32, A0)
    w1t = at_("w1t", [128, DC * FF], BF16, A0)
    o2stA = at_("o2stA", [128, DC * 512], BF16, A0)
    o2stB = at_("o2stB", [128, DC * 512], BF16, A0 + 8 * KB)
    outst = at_("outst", [128, DC * 512], F32, A0 + 16 * KB)
    sq8_f2 = at_("sq8_f2", [128, DC * 512], BF16, A0 + 40 * KB)
    scr1_f2 = at_("scr1_f2", [128, 2 * 512], F32, A0 + 48 * KB)
    p0stg = at_("p0stg", [128, 6144], F32, B0)
    xb = at_("xb", [128, DC * T], BF16, B0)
    sq8_a = at_("sq8_a", [128, DC * 512], BF16, B0 + 32 * KB)
    lad = at_("lad", [128, 16 * 512], BF16, B0 + 40 * KB)
    ppow2 = at_("ppow2", [128, T2 * NKC * N], BF16, B0 + 40 * KB)
    act_t = at_("act_t", [128, DC * 512], F32, B0 + 40 * KB)
    btb = at_("btb", [128, DC * N], BF16, B0 + 56 * KB)
    ctb = at_("ctb", [128, NKC * D], BF16, B0 + 60 * KB)
    absdump = at_("absdump", [128, 4096], BF16, B0)
    wqbufA = at_("wqbufA", [128, 2048], F32, B0)
    wqbufB = at_("wqbufB", [128, 2048], F32, B0 + 8 * KB)
    wqabs = at_("wqabs", [128, 2048], BF16, B0 + 16 * KB)
    g_t = at_("g_t", [128, FC * 512], BF16, B0)
    w2t = at_("w2t", [128, FC * D], BF16, B0)
    u_t = at_("u_t", [128, NKC * T], F32, C0)
    sloc = at_("sloc", [128, NKC * T1 * 128], BF16, C0 + 16 * KB)
    states = at_("states", [128, NKC * T], BF16, C0 + 24 * KB)
    ppow = at_("ppow", [128, T1 * NKC * N], BF16, C0 + 32 * KB)
    x1bb_p5 = at_("x1bb_p5", [128, DC * 512], BF16, C0)
    wchunk = at_("wchunk", [128, 4096], F32, C0)
    wchunk2 = at_("wchunk2", [128, 4096], F32, C0 + 32 * KB)
    wcint = at_("wcint", [128, 4096], I32, C0 + 16 * KB)
    q1b = at_("q1b", [128, DC * 512], BF16, C0)
    x1bb_f1 = at_("x1bb_f1", [128, DC * 512], BF16, B0 + 56 * KB)
    sb2 = at_("sb2", [128, 512], F32, B0 + 48 * KB)
    sc2 = at_("sc2", [128, 512], F32, B0 + 50 * KB)
    scrH = at_("scrH", [128, 2 * 512], F32, B0 + 52 * KB)
    scrHi = at_("scrHi", [128, 2 * 512], I32, B0 + 52 * KB)
    act_tB = at_("act_tB", [128, DC * 512], F32, C0 + 32 * KB)
    x1bbB_p5 = at_("x1bbB_p5", [128, DC * 512], BF16, C0 + 8 * KB)
    sq8_aB = at_("sq8_aB", [128, DC * 512], BF16, C0 + 16 * KB)
    statB = at_("statB", [128, 5 * 512], F32, B0 + 20 * KB)
    scr1_f1 = at_("scr1_f1", [128, 2 * 512], F32, C0 + 8 * KB)
    scri_f1 = at_("scri_f1", [128, 2 * 512], I32, C0 + 8 * KB)
    sq8_f1 = at_("sq8_f1", [128, DC * 512], BF16, C0 + 16 * KB)
    x1t = at_("x1t", [128, DC * 512], F32, C0 + 32 * KB)
    x1t2 = at_("x1t2", [128, DC * 512], F32, B0 + 32 * KB)
    q2st = at_("q2st", [128, 8 * 512], BF16, C0 + 24 * KB)
    q2in = at_("q2in", [128, FC * 512], BF16, C0)
    dptr = D0

    def dal(name, shape, dt, nbytes, share=None):
        nonlocal dptr
        off = share if share is not None else dptr
        t = at_(name, shape, dt, off)
        if share is None:
            dptr += nbytes
        return t

    ones128b = dal("ones128b", [128, 128], BF16, 256)
    onescb = dal("onescb", [128, 1], BF16, 32)
    onesc = dal("onesc", [128, 1], F32, 32)
    ones_r = dal("ones_r", [1, 128], F32, 512)
    idf = dal("idf", [128, 128], F32, 512)
    idb = dal("idb", [128, 128], BF16, 256)
    gam = dal("gam", [128, DC], F32, 32)
    bet = dal("bet", [128, DC], F32, 32)
    b1c = dal("b1c", [128, FC], F32, 128)
    b2c = dal("b2c", [128, DC], F32, 32)
    rowbb = dal("rowbb", [1, N], BF16, 512)
    nmrow = dal("nmrow", [1, 512], BF16, 1024)
    NSTAT = 8
    stat = dal("stat", [128, NSTAT * 512], F32, NSTAT * 2 * KB)
    scrb = dal("scrb", [128, 2 * 512], BF16, 2 * KB)
    rmax = dal("rmax", [128, 512], BF16, KB)
    mcol = dal("mcol", [128, 4], F32, 32)
    mrow = dal("mrow", [1, 512], F32, 2 * KB)
    acc8 = dal("acc8", [128, 8], F32, 32)
    mu1pp = dal("mu1pp", [128, 1], F32, 32)
    mu2pp = dal("mu2pp", [128, 1], F32, 32)
    invmu1 = dal("invmu1", [128, 1], F32, 32)
    invmu2 = dal("invmu2", [128, 1], F32, 32)
    epspp = dal("epspp", [128, 1], F32, 32)
    acc8b = dal("acc8b", [128, 8], BF16, 32)
    acc16 = dal("acc16", [128, 16], F32, 64)
    acc16b = dal("acc16b", [128, 16], BF16, 64)
    ones128f = dal("ones128f", [128, 128], F32, 512)
    sqacc = dal("sqacc", [128, 512], F32, 2 * KB)
    g0col = dal("g0col", [128, NKC * (NG + 1)], BF16, 96)
    y2 = dal("y2", [128, NKC * T2 * NG], BF16, NKC * T2 * NG * 2)
    e_t = dal("e_t", [128, NKC * NCH], BF16, NKC * NCH * 2)
    assert dptr <= 228 * KB, f"D region overflow: ends at {dptr}"
    nc.sbuf_base = dptr

    def st(i, sl=slice(0, 512)):
        return stat[:, i * 512:(i + 1) * 512][:, sl]

    pbk = [nc.alloc_psum_tensor(f"pb{i}", [128, 512], F32) for i in range(7)]
    pT = nc.alloc_psum_tensor("pT", [128, 512], BF16)

    LD = Sem(nc, "LD")
    LDG = Sem(nc, "LDG")
    MM = Sem(nc, "MM")
    VV = Sem(nc, "VV")
    SS = Sem(nc, "SS")
    PP = Sem(nc, "PP")
    STO = Sem(nc, "STO")

    def xfc(k, sl=slice(0, T)):
        return xf[:, k * T:(k + 1) * T][:, sl]

    def xbc(k, sl=slice(0, T)):
        return xb[:, k * T:(k + 1) * T][:, sl]

    def att(k):
        return act_t[:, k * 512:(k + 1) * 512]

    def uc(n, sl):
        return u_t[:, n * T:(n + 1) * T][:, sl]

    def stcv(n, sl):
        return states[:, n * T:(n + 1) * T][:, sl]

    def slocv(n, i, sl=slice(0, 128)):
        base = n * T1 * 128 + i * 128
        return sloc[:, base:base + 128][:, sl]

    def pw(i, kc):
        base = (i - 1) * NKC * N + kc * N
        return ppow[:, base:base + N]

    def pw2(j, kc):
        base = (j - 1) * NKC * N + kc * N
        return ppow2[:, base:base + N]

    def ladv(j, rc):
        base = (j - 1) * 512 + rc * 256
        return lad[:, base:base + 256]

    def btbc(k):
        return btb[:, k * N:(k + 1) * N]

    def ctbc(kc):
        return ctb[:, kc * D:(kc + 1) * D]

    def gtc(fc):
        return g_t[:, fc * 512:(fc + 1) * 512]

    def x1tc(k):
        return x1t[:, k * 512:(k + 1) * 512]

    def q1bc(k):
        return q1b[:, k * 512:(k + 1) * 512]

    def q2inc(fc):
        return q2in[:, fc * 512:(fc + 1) * 512]

    def ev(n, sl):
        return e_t[:, n * NCH:(n + 1) * NCH][:, sl]

    def gcol(n, g):
        return g0col[:, n * (NG + 1) + g:n * (NG + 1) + g + 1]

    def y2v(n, j, sl=slice(0, NG)):
        base = n * T2 * NG + (j - 1) * NG
        return y2[:, base:base + NG][:, sl]

    ph = Phase(nc, "p0_load")
    ph.op("sync", lambda sy: sy.dma_start(out=xf[:], in_=x_f), inc=LD, k=16)
    ph.op("sync", lambda sy: sy.dma_start(out=p0stg[:, 0:512], in_=at_m), inc=LD, k=16)
    ph.op("sync", lambda sy: sy.dma_start(out=p0stg[:, 512:1024], in_=a_m), inc=LD, k=16)
    ph.op("sync", lambda sy: sy.dma_start(out=p0stg[:, 1024:3072], in_=bt_m), inc=LD, k=16)
    ph.op("sync", lambda sy: sy.dma_start(out=p0stg[:, 3072:5120], in_=ct_m), inc=LD, k=16)
    ph.op("sync", lambda sy: sy.dma_start(out=idf[:], in_=idf_m), inc=LD, k=16)
    ph.op("sync", lambda sy: sy.dma_start(out=idb[:], in_=idb_m), inc=LD, k=16)
    ph.op("sync", lambda sy: sy.dma_start(out=gam[:], in_=gam_m), inc=LD, k=16)
    ph.op("sync", lambda sy: sy.dma_start(out=bet[:], in_=bet_m), inc=LD, k=16)
    ph.op("sync", lambda sy: sy.dma_start(out=b1c[:], in_=b1_m), inc=LD, k=16)
    ph.op("sync", lambda sy: sy.dma_start(out=b2c[:], in_=b2_m), inc=LD, k=16)
    ldall = LD.n

    def _consts(vv):
        vv.memset(ones128b[:], 1.0)
        vv.memset(onescb[:], 1.0)
        vv.memset(onesc[:], 1.0)
        vv.memset(ones_r[:], 1.0)
        vv.memset(ones128f[:], 1.0)
        vv.memset(epspp[:], EPS)
        for kc in range(NKC):
            vv.tensor_copy(pw(1, kc), p0stg[:, kc * N:(kc + 1) * N])
            vv.tensor_copy(ladv(1, kc), p0stg[:, 512 + kc * N:512 + (kc + 1) * N])
            vv.tensor_copy(ctbc(kc), p0stg[:, 3072 + kc * D:3072 + (kc + 1) * D])
        for k in range(DC):
            vv.tensor_copy(btbc(k), p0stg[:, 1024 + k * N:1024 + (k + 1) * N])
        last = None
        for k in range(DC):
            last = vv.tensor_copy(xbc(k), xfc(k))
        return last
    ph.op("vector", _consts, waits=[(LD, ldall)], inc=VV)
    ph.emit()

    lad_state = {"vv": VV.n}

    def emit_ladder_stage(ph, c, pwf, rslot, cnt_target):
        if skip_ladder:
            return
        njs = min(c, cnt_target - c)
        for j in range(1, njs + 1):
            nj = c + j

            def mmf(tt, j=j, c=c, pwf=pwf, rslot=rslot):
                last = None
                for mc in range(2):
                    for kc in range(NKC):
                        last = tt.matmul(pbk[4][:, mc * 256:(mc + 1) * 256],
                                         pwf(c, kc)[:, mc * 128:(mc + 1) * 128],
                                         rslot(j, kc),
                                         start=(kc == 0), stop=(kc == NKC - 1))
                return last
            m1 = ph.op("tensor", mmf, waits=[(VV, lad_state["vv"])], inc=MM)

            def evf(vv, nj=nj, rslot=rslot):
                vv.tensor_copy(rslot(nj, 0), pbk[4][:, 0:256])
                return vv.tensor_copy(rslot(nj, 1), pbk[4][:, 256:512])
            e1 = ph.op("vector", evf, waits=[(MM, m1)], inc=VV)

            def tpf(tt, nj=nj, rslot=rslot):
                last = None
                for rc in range(2):
                    for sc in range(2):
                        last = tt.transpose(
                            pT[:, (2 * rc + sc) * 128:(2 * rc + sc + 1) * 128],
                            rslot(nj, rc)[:, sc * 128:(sc + 1) * 128], idb[:])
                return last
            m2 = ph.op("tensor", tpf, waits=[(VV, e1)], inc=MM)

            def pvf(vv, nj=nj, pwf=pwf):
                last = None
                for rc in range(2):
                    for sc in range(2):
                        last = vv.tensor_copy(
                            pwf(nj, sc)[:, rc * 128:(rc + 1) * 128],
                            pT[:, (2 * rc + sc) * 128:(2 * rc + sc + 1) * 128])
                return last
            lad_state["vv"] = ph.op("vector", pvf, waits=[(MM, m2)], inc=VV)

    def emit_ladder_phase():
        ph = Phase(nc, "p1_ladder")

        def rslot1(j, kc):
            return ladv(j, kc)

        c = 1
        while c < T1:
            emit_ladder_stage(ph, c, pw, rslot1, T1)
            c *= 2

        def pw2f(i, kc):
            if i == 1:
                return pw(T1, kc)
            return pw2(i, kc)

        def rslot2(j, kc):
            if j == 1:
                return ladv(16, kc)
            return ladv(8 + j, kc)

        def cp16(vv):
            vv.tensor_copy(pw2(1, 0), pw(T1, 0))
            return vv.tensor_copy(pw2(1, 1), pw(T1, 1))
        if not skip_ladder:
            lad_state["vv"] = ph.op("vector", cp16,
                                    waits=[(VV, lad_state["vv"])], inc=VV)

        c = 1
        while c < T2:
            emit_ladder_stage(ph, c, pw2f, rslot2, T2)
            c *= 2
        ph.emit()

    ph = Phase(nc, "p1_rowb")

    def rbmm(tt):
        last = None
        for k in range(DC):
            last = tt.matmul(pbk[6][0:1, 0:N], onescb[:], btbc(k),
                             start=(k == 0), stop=(k == DC - 1))
        return last
    m = ph.op("tensor", rbmm, waits=[(VV, VV.n)], inc=MM)
    ph.op("vector", lambda vv: vv.tensor_copy(rowbb[:], pbk[6][0:1, 0:N]),
          waits=[(MM, m)], inc=VV)
    ph.emit()

    S_MEAN, S_VAR, S_RSTD, S_NEGM, S_SCR, S_A, S_B, S_C = range(8)

    def emit_stats(ph, src_b16, src_f32, sqbuf, want_negm_row=False,
                   sq_waits=None, summ_waits=None, stx=None, mm_war=()):
        if stx is None:
            stx = st

        def sqf(ss):
            last = None
            for k in range(DC):
                last = ss.activation(sqbuf[:, k * 512:(k + 1) * 512],
                                     src_f32(k), AF.Square)
            return last
        s1 = ph.op("scalar", sqf,
                   waits=(sq_waits if sq_waits is not None
                          else [(VV, VV.n), (MM, MM.n)]), inc=SS)

        def summ(tt):
            last = None
            for k in range(DC):
                last = tt.matmul(pbk[0][:], ones128b[:], src_b16(k),
                                 start=(k == 0), stop=(k == DC - 1))
            return last
        ph.op("tensor", summ,
              waits=(summ_waits if summ_waits is not None
                     else [(VV, VV.n), (SS, SS.n)]) + list(mm_war), inc=MM)

        def sqm(tt):
            last = None
            for k in range(DC):
                last = tt.matmul(pbk[1][:], ones128b[:],
                                 sqbuf[:, k * 512:(k + 1) * 512],
                                 start=(k == 0), stop=(k == DC - 1))
            return last
        m2 = ph.op("tensor", sqm, waits=[(SS, s1)] + list(mm_war), inc=MM)

        def soup1(vv):
            vv.tensor_scalar_mul(stx(S_MEAN), pbk[0][:], 1.0 / D)
            vv.tensor_scalar_mul(stx(S_VAR), pbk[1][:], 1.0 / D)
            vv.tensor_mul(stx(S_SCR), stx(S_MEAN), stx(S_MEAN))
            vv.tensor_sub(stx(S_VAR), stx(S_VAR), stx(S_SCR))
            return vv.tensor_scalar_mul(stx(S_NEGM), stx(S_MEAN), -1.0)
        v1 = ph.op("vector", soup1, waits=[(MM, m2)], inc=VV)

        def rstdf(ss):
            ss.activation(stx(S_SCR), stx(S_VAR), AF.Ln, bias=epspp[:, 0:1])
            return ss.activation(stx(S_RSTD), stx(S_SCR), AF.Exp, scale=-0.5)
        s2 = ph.op("scalar", rstdf, waits=[(VV, v1)], inc=SS)
        v2 = v1
        if want_negm_row:
            v2 = ph.op("vector",
                       lambda vv: vv.tensor_copy(nmrow[:], stx(S_NEGM)[0:1, :]),
                       inc=VV)
        return s2, v2

    if stop_after == "sq0":
        sqtest = nc.alloc_sbuf_tensor("sqtest", [128, 1024], BF16)
        ph = Phase(nc)
        def sq0f(ss):
            ss.activation(sq8_a[:, 0:512], xfc(0, slice(0, 512)), AF.Square)
            ss.activation(sqtest[:, 0:512], xfc(0, slice(0, 512)), AF.Square)
            return ss.activation(sqtest[:, 512:1024], scr1_f1[:, 0:512], AF.Square)
        def cpf(vv):
            return vv.tensor_copy(scr1_f1[:, 0:512], xfc(0, slice(0, 512)))
        v0 = ph.op("vector", cpf, waits=[(VV, VV.n)], inc=VV)
        s0 = ph.op("scalar", sq0f, waits=[(VV, v0)], inc=SS)
        ph.op("sync", lambda sy: sy.dma_start(out=sqdump, in_=sq8_a[:]),
              waits=[(SS, s0)], inc=STO, k=16)
        ph.op("sync", lambda sy: sy.dma_start(out=xbdump[:, 0:1024], in_=sqtest[:]),
              waits=[(SS, s0)], inc=STO, k=16)
        ph.op("sync", lambda sy: sy.dma_start(out=xfdump[:, 0:512], in_=scr1_f1[:, 0:512]),
              waits=[(SS, s0)], inc=STO, k=16)
        ph.emit()
        return nc, _emit_finish(nc, STO)

    def make_wq_pass1(ph, src_m, mu_pp, inv_pp):
        state = {"h": 0, "svals": []}

        def step(n):
            for _ in range(n):
                h = state["h"]
                if h >= 16:
                    return
                state["h"] += 1
                buf = wqbufA if h % 2 == 0 else wqbufB
                w = ([(SS, state["svals"][h - 2])] if h >= 2
                     else [(SS, SS.n), (MM, MM.n), (VV, VV.n)])
                l = ph.op("gpsimd",
                          lambda sy, h=h, buf=buf: sy.dma_start(
                              out=buf[:], in_=src_m[:, h * 2048:(h + 1) * 2048]),
                          waits=w, inc=LDG, k=16)
                state["svals"].append(
                    ph.op("scalar",
                          lambda ss, h=h, buf=buf: ss.activation(
                              wqabs[:], buf[:], AF.Abs,
                              accum_out=acc16[:, h:h + 1]),
                          waits=[(LDG, l)], inc=SS))

        def finish():
            step(16)
            v1 = ph.op("vector",
                       lambda vv: vv.tensor_copy(acc16b[:], acc16[:]),
                       waits=[(SS, state["svals"][-1])], inc=VV)
            m1 = ph.op("tensor",
                       lambda tt: tt.matmul(pbk[6][:, 0:16], ones128b[:],
                                            acc16b[:], start=True, stop=True),
                       waits=[(VV, v1)], inc=MM)
            v2 = ph.op("vector",
                       lambda vv: vv.reduce_sum(inv_pp[:], pbk[6][:, 0:16],
                                                axis=AX.X),
                       waits=[(MM, m1)], inc=VV)
            v3 = ph.op("vector",
                       lambda vv: vv.tensor_scalar(mu_pp[:], inv_pp[:],
                                                   1.0 / (1024.0 * 4096.0), 1e-5,
                                                   ALU.mult, ALU.max),
                       waits=[(VV, v2)], inc=VV)
            ph.op("vector", lambda vv: vv.reciprocal(inv_pp[:], mu_pp[:]),
                  waits=[(VV, v3)], inc=VV)

        return step, finish


    ph = Phase(nc, "p2_lnub")
    for tt_i in range(NTT):
        cs = slice(tt_i * 512, (tt_i + 1) * 512)
        sstat, vstat = emit_stats(ph,
                                  (lambda k, cs=cs: xbc(k, cs)),
                                  (lambda k, cs=cs: xfc(k, cs)),
                                  sq8_a, want_negm_row=True)

        def ubmm(tt, cs=cs):
            last = None
            for n_ in range(NKC):
                for k in range(DC):
                    tt.matmul(pbk[2 + n_][:], btbc(k)[:, n_ * 128:(n_ + 1) * 128],
                              xbc(k, cs), start=(k == 0), stop=False)
                last = tt.matmul(pbk[2 + n_][:], rowbb[0:1, n_ * 128:(n_ + 1) * 128],
                                 nmrow[:], start=False, stop=True)
            return last
        m3 = ph.op("tensor", ubmm, waits=[(VV, vstat)], inc=MM)

        def ubev(vv, cs=cs):
            vv.tensor_mul(uc(0, cs), pbk[2][:], st(S_RSTD))
            return vv.tensor_mul(uc(1, cs), pbk[3][:], st(S_RSTD))
        ph.op("vector", ubev, waits=[(MM, m3), (SS, sstat)], inc=VV)
    ph.emit()
    emit_ladder_phase()
    if dbg:
        ph = Phase(nc)
        ph.op("sync", lambda sy: sy.dma_start(out=udump, in_=u_t[:]),
              waits=[(VV, VV.n)], inc=STO, k=16)
        ph.op("sync", lambda sy: sy.dma_start(out=statdump, in_=stat[:]),
              inc=STO, k=16)
        ph.op("sync", lambda sy: sy.dma_start(out=sqdump, in_=sq8_a[:]),
              inc=STO, k=16)
        ph.op("sync", lambda sy: sy.dma_start(out=xbdump, in_=xb[:]),
              inc=STO, k=16)
        ph.op("sync", lambda sy: sy.dma_start(out=xfdump, in_=xf[:]),
              inc=STO, k=16)
        ph.emit()
    if stop_after == "p2":
        return nc, _emit_finish(nc, STO)

    ph = Phase(nc, "p3_scan")
    wq1_step, wq1_fin = make_wq_pass1(ph, w1t_m, mu1pp, invmu1)
    wq1_step(16)

    def s1f(vv):
        vv.tensor_copy(slocv(0, 0), uc(0, slice(0, T, T1)))
        return vv.tensor_copy(slocv(1, 0), uc(1, slice(0, T, T1)))
    vprev = ph.op("vector", s1f, waits=[(VV, VV.n)], inc=VV)

    for i in range(2, T1 + 1):
        bank = pbk[i % 2]

        def stepmm(tt, i=i, bank=bank):
            last = None
            for n_ in range(NKC):
                for kc in range(NKC):
                    last = tt.matmul(bank[:, n_ * 128:(n_ + 1) * 128],
                                     pw(1, kc)[:, n_ * 128:(n_ + 1) * 128],
                                     slocv(kc, i - 2),
                                     start=(kc == 0), stop=(kc == NKC - 1))
            return last
        m = ph.op("tensor", stepmm, waits=[(VV, vprev)], inc=MM)

        def stepev(vv, i=i, bank=bank):
            vv.tensor_add(slocv(0, i - 1), bank[:, 0:128], uc(0, slice(i - 1, T, T1)))
            return vv.tensor_add(slocv(1, i - 1), bank[:, 128:256],
                                 uc(1, slice(i - 1, T, T1)))
        vprev = ph.op("vector", stepev, waits=[(MM, m)], inc=VV)

    def y1f(vv):
        vv.tensor_copy(y2v(0, 1), slocv(0, T1 - 1, slice(0, 128, T2)))
        return vv.tensor_copy(y2v(1, 1), slocv(1, T1 - 1, slice(0, 128, T2)))
    vprev = ph.op("vector", y1f, inc=VV)
    for j in range(2, T2 + 1):
        bank = pbk[j % 2]

        def l2mm(tt, j=j, bank=bank):
            last = None
            for n_ in range(NKC):
                for kc in range(NKC):
                    last = tt.matmul(bank[:, n_ * 128:n_ * 128 + NG],
                                     pw2(1, kc)[:, n_ * 128:(n_ + 1) * 128],
                                     y2v(kc, j - 1),
                                     start=(kc == 0), stop=(kc == NKC - 1))
            return last
        m = ph.op("tensor", l2mm, waits=[(VV, vprev)], inc=MM)

        def l2ev(vv, j=j, bank=bank):
            vv.tensor_add(y2v(0, j), bank[:, 0:NG],
                          slocv(0, T1 - 1, slice(j - 1, 128, T2)))
            return vv.tensor_add(y2v(1, j), bank[:, 128:128 + NG],
                                 slocv(1, T1 - 1, slice(j - 1, 128, T2)))
        vprev = ph.op("vector", l2ev, waits=[(MM, m)], inc=VV)

    def gz(vv):
        vv.memset(gcol(0, 0), 0.0)
        return vv.memset(gcol(1, 0), 0.0)
    vg = ph.op("vector", gz, inc=VV)
    for g in range(NG):
        bank = pbk[g % 2]

        def l3mm(tt, g=g, bank=bank):
            last = None
            for n_ in range(NKC):
                for kc in range(NKC):
                    last = tt.matmul(bank[:, n_ * 128:n_ * 128 + 1],
                                     pw2(T2, kc)[:, n_ * 128:(n_ + 1) * 128],
                                     gcol(kc, g),
                                     start=(kc == 0), stop=(kc == NKC - 1))
            return last
        m = ph.op("tensor", l3mm, waits=[(VV, vg)], inc=MM)

        def l3ev(vv, g=g, bank=bank):
            vv.tensor_add(gcol(0, g + 1), bank[:, 0:1], y2v(0, T2, slice(g, g + 1)))
            return vv.tensor_add(gcol(1, g + 1), bank[:, 128:129],
                                 y2v(1, T2, slice(g, g + 1)))
        vg = ph.op("vector", l3ev, waits=[(MM, m)], inc=VV)
    vprev = vg

    def ecopyf(vv):
        vv.memset(ev(0, slice(0, 1)), 0.0)
        vv.memset(ev(1, slice(0, 1)), 0.0)
        vv.tensor_copy(ev(0, slice(T2, NCH, T2)),
                       g0col[:, 1:NG])
        return vv.tensor_copy(ev(1, slice(T2, NCH, T2)),
                              g0col[:, (NG + 1) + 1:(NG + 1) + NG])
    vprev3 = ph.op("vector", ecopyf, waits=[(VV, vprev)], inc=VV)
    for j in range(1, T2):
        bank = pbk[2 + (j % 2)]

        def c2mm(tt, j=j, bank=bank):
            last = None
            for n_ in range(NKC):
                for kc in range(NKC):
                    last = tt.matmul(bank[:, n_ * 128:n_ * 128 + NG],
                                     pw2(j, kc)[:, n_ * 128:(n_ + 1) * 128],
                                     g0col[:, kc * (NG + 1):kc * (NG + 1) + NG],
                                     start=(kc == 0), stop=(kc == NKC - 1))
            return last
        m = ph.op("tensor", c2mm, waits=[(VV, vprev3)], inc=MM)

        def c2ev(vv, j=j, bank=bank):
            vv.tensor_add(ev(0, slice(j, NCH, T2)), bank[:, 0:NG], y2v(0, j))
            return vv.tensor_add(ev(1, slice(j, NCH, T2)), bank[:, 128:128 + NG],
                                 y2v(1, j))
        vprev3 = ph.op("vector", c2ev, waits=[(MM, m)], inc=VV)

    vfin = vprev3
    for i in range(1, T1 + 1):
        bank = pbk[4 + (i % 2)]

        def fmm(tt, i=i, bank=bank):
            last = None
            for n_ in range(NKC):
                for kc in range(NKC):
                    last = tt.matmul(bank[:, n_ * 128:(n_ + 1) * 128],
                                     pw(i, kc)[:, n_ * 128:(n_ + 1) * 128],
                                     ev(kc, slice(0, NCH)),
                                     start=(kc == 0), stop=(kc == NKC - 1))
            return last
        m = ph.op("tensor", fmm, waits=[(VV, vfin)], inc=MM)

        def fev(vv, i=i, bank=bank):
            vv.tensor_add(stcv(0, slice(i - 1, T, T1)), bank[:, 0:128],
                          slocv(0, i - 1))
            return vv.tensor_add(stcv(1, slice(i - 1, T, T1)), bank[:, 128:256],
                                 slocv(1, i - 1))
        vfin = ph.op("vector", fev, waits=[(MM, m)], inc=VV)
    wq1_fin()
    ph.emit()
    if dbg:
        ph = Phase(nc)
        ph.op("sync", lambda sy: sy.dma_start(out=sdump, in_=states[:]),
              waits=[(VV, VV.n)], inc=STO, k=16)
        ph.emit()
    if stop_after == "scan":
        return nc, _emit_finish(nc, STO)

    spill_sto = []
    ph = Phase(nc, "p4_epi")
    wq2_step, wq2_fin = make_wq_pass1(ph, w2t_m, mu2pp, invmu2)

    def stP(t):
        if t % 2 == 0:
            return st
        return lambda i: statB[:, i * 512:(i + 1) * 512]

    def attP(t, k):
        buf = act_t if t % 2 == 0 else act_tB
        return buf[:, k * 512:(k + 1) * 512]

    def x1bbP(t, k):
        buf = x1bb_p5 if t % 2 == 0 else x1bbB_p5
        return buf[:, k * 512:(k + 1) * 512]

    p4 = {"sgel": {}, "soup1_last": None, "vr": {}}

    def p4_ymm(t):
        wq2_step(4)
        cs = slice(t * 512, (t + 1) * 512)
        sgel = []
        for dc in range(DC):
            bank = pbk[3 + dc % 3]
            if dc < 3:
                w = ([(SS, p4["sgel"][t - 1][-1])] if t >= 1 else [(VV, VV.n)])
            else:
                w = [(SS, sgel[dc - 3])]

            def ymm(tt, dc=dc, cs=cs, bank=bank):
                last = None
                for kc in range(NKC):
                    last = tt.matmul(bank[:], ctbc(kc)[:, dc * 128:(dc + 1) * 128],
                                     stcv(kc, cs), start=(kc == 0),
                                     stop=(kc == NKC - 1))
                return last
            m = ph.op("tensor", ymm, waits=w, inc=MM)
            wg = [(MM, m)]
            if dc == 0 and t >= 2:
                wg.append((VV, p4["vr"][t - 2]))
            sgel.append(ph.op("scalar",
                              lambda ss, dc=dc, bank=bank, t=t:
                              ss.activation(attP(t, dc), bank[:], AF.Gelu),
                              waits=wg, inc=SS))
        p4["sgel"][t] = sgel

    def p4_epi(t):
        cs = slice(t * 512, (t + 1) * 512)
        sx = stP(t)
        sqb = sq8_a if t % 2 == 0 else sq8_aB

        def abf(vv, t=t):
            last = None
            for k in range(DC):
                last = vv.tensor_copy(x1bbP(t, k), attP(t, k))
            return last
        vabf = ph.op("vector", abf, waits=[(SS, p4["sgel"][t][-1])], inc=VV)
        mmw = [(VV, p4["soup1_last"])] if p4["soup1_last"] else []
        sstat, vstat = emit_stats(ph, lambda k, t=t: x1bbP(t, k),
                                  lambda k, t=t: attP(t, k), sqb,
                                  sq_waits=[], summ_waits=[(VV, vabf)],
                                  stx=sx, mm_war=mmw)

        vap = None
        for k in range(DC):
            def ap1(vv, k=k, t=t):
                vv.tensor_add(attP(t, k), attP(t, k), sx(S_NEGM))
                return vv.tensor_mul(attP(t, k), attP(t, k), sx(S_RSTD))
            v1 = ph.op("vector", ap1, waits=[(VV, vstat), (SS, sstat)], inc=VV)
            s1 = ph.op("scalar",
                       lambda ss, k=k, t=t: ss.activation(
                           attP(t, k), attP(t, k), AF.Identity,
                           bias=bet[:, k:k + 1], scale=gam[:, k:k + 1]),
                       waits=[(VV, v1)], inc=SS)
            vap = ph.op("vector",
                        lambda vv, k=k, t=t: vv.tensor_scalar(
                            attP(t, k), attP(t, k), CLAMP, -CLAMP,
                            ALU.min, ALU.max),
                        waits=[(SS, s1)], inc=VV)

        def abf2(vv, t=t):
            last = None
            for k in range(DC):
                last = vv.tensor_copy(x1bbP(t, k), attP(t, k))
            return last
        vabf2 = ph.op("vector", abf2, waits=[(VV, vap)], inc=VV)
        sstat2, vstat2 = emit_stats(ph, lambda k, t=t: x1bbP(t, k),
                                    lambda k, t=t: attP(t, k), sqb,
                                    sq_waits=[], summ_waits=[(VV, vabf2)],
                                    stx=sx, mm_war=[(VV, vstat)])
        p4["soup1_last"] = vstat2

        vr = None
        for k in range(DC):
            def ap2(vv, k=k, cs=cs, t=t):
                vv.tensor_add(attP(t, k), attP(t, k), sx(S_NEGM))
                vv.tensor_mul(attP(t, k), attP(t, k), sx(S_RSTD))
                return vv.tensor_add(xfc(k, cs), attP(t, k), xfc(k, cs))
            vr = ph.op("vector", ap2, waits=[(VV, vstat2), (SS, sstat2)], inc=VV)
        p4["vr"][t] = vr

        spill_sto.append(
            ph.op("sync",
                  lambda sy, cs=cs: sy.dma_start(
                      out=x1s.rearrange("p (k t) -> p k t", k=DC)[:, :, cs],
                      in_=xf.ap().rearrange("p (k t) -> p k t", k=DC)[:, :, cs]),
                  waits=[(VV, vr)], inc=STO, k=16))

    p4_ymm(0)
    for t in range(NTT):
        if t + 1 < NTT:
            p4_ymm(t + 1)
        p4_epi(t)
    wq2_fin()
    ph.emit()
    if stop_after == "attn":
        return nc, _emit_finish(nc, STO)

    def emit_wq_pass1(src_m, mu_pp, inv_pp):
        ph = Phase(nc, "wq_p1")
        svals = []
        for ch in range(8):
            buf = wchunk if ch % 2 == 0 else wchunk2
            w = [(SS, svals[ch - 2])] if ch >= 2 else [(STO, STO.n)]
            l = ph.op("gpsimd",
                      lambda sy, ch=ch, buf=buf: sy.dma_start(
                          out=buf[:], in_=src_m[:, ch * 4096:(ch + 1) * 4096]),
                      waits=w, inc=LDG, k=16)
            svals.append(ph.op("scalar",
                               lambda ss, ch=ch, buf=buf: ss.activation(
                                   absdump[:], buf[:], AF.Abs,
                                   accum_out=acc8[:, ch:ch + 1]),
                               waits=[(LDG, l)], inc=SS))
        v1 = ph.op("vector",
                   lambda vv: vv.tensor_copy(acc8b[:], acc8[:]),
                   waits=[(SS, svals[-1])], inc=VV)
        m1 = ph.op("tensor",
                   lambda tt: tt.matmul(pbk[6][:, 0:8], ones128b[:], acc8b[:],
                                        start=True, stop=True),
                   waits=[(VV, v1)], inc=MM)

        v2 = ph.op("vector",
                   lambda vv: vv.reduce_sum(inv_pp[:], pbk[6][:, 0:8], axis=AX.X),
                   waits=[(MM, m1)], inc=VV)
        v3 = ph.op("vector",
                   lambda vv: vv.tensor_scalar(mu_pp[:], inv_pp[:],
                                               1.0 / (1024.0 * 4096.0), 1e-5,
                                               ALU.mult, ALU.max),
                   waits=[(VV, v2)], inc=VV)
        ph.op("vector", lambda vv: vv.reciprocal(inv_pp[:], mu_pp[:]),
              waits=[(VV, v3)], inc=VV)
        ph.emit()

    def emit_wq_pass2(src_m, dst_cols, inv_pp, first_waits=()):
        ph = Phase(nc, "wq_p2")
        qvals = []
        for ch in range(8):
            buf2 = wchunk if ch % 2 == 0 else wchunk2
            w = [(VV, qvals[ch - 2])] if ch >= 2 else list(first_waits) + [(VV, VV.n)]
            sem = LD if ch % 2 == 0 else LDG
            l = ph.op("sync" if ch % 2 == 0 else "gpsimd",
                      lambda sy, ch=ch, buf2=buf2: sy.dma_start(
                          out=buf2[:], in_=src_m[:, ch * 4096:(ch + 1) * 4096]),
                      waits=w, inc=sem, k=16)

            def qf(vv, ch=ch, buf2=buf2):
                vv.tensor_scalar_mul(wcint[:], buf2[:], inv_pp[:, 0:1])
                return vv.tensor_scalar(dst_cols(ch), wcint[:], 1.0, -1.0,
                                        ALU.min, ALU.max)
            qvals.append(ph.op("vector", qf, waits=[(sem, l)], inc=VV))
        ph.emit()

    emit_wq_pass2(w1t_m, lambda ch: w1t[:, ch * 4096:(ch + 1) * 4096], invmu1,
                  first_waits=[(STO, spill_sto[-1])])
    if stop_after == "wq":
        return nc, _emit_finish(nc, STO)

    q2_sto = []
    ph = Phase(nc, "f1")
    x1_lastuse = {}
    st_f1 = {"gdma": None, "c2sto": {}, "vq1": {}, "alpha2": {}}

    def sb(t):
        return st(S_B) if t % 2 == 0 else sb2[:, 0:512]

    def sc(t):
        return st(S_C) if t % 2 == 0 else sc2[:, 0:512]

    def amax_chain(ph):
        def tpf(tt):
            last = None
            for j in range(4):
                last = tt.transpose(pT[:, j * 128:(j + 1) * 128],
                                    rmax[:, j * 128:(j + 1) * 128], idb[:])
            return last
        m = ph.op("tensor", tpf, waits=[(VV, VV.n)], inc=MM)

        def redf(vv):
            last = None
            for j in range(4):
                last = vv.reduce_max(mcol[:, j:j + 1],
                                     pT[:, j * 128:(j + 1) * 128], axis=AX.X)
            return last
        v = ph.op("vector", redf, waits=[(MM, m)], inc=VV)

        def mtpf(tt):
            last = None
            for j in range(4):
                last = tt.transpose(pbk[6][0:1, j * 128:(j + 1) * 128],
                                    mcol[:, j:j + 1], idf[:])
            return last
        m = ph.op("tensor", mtpf, waits=[(VV, v)], inc=MM)
        v = ph.op("vector",
                  lambda vv: vv.tensor_copy(mrow[:], pbk[6][0:1, 0:512]),
                  waits=[(MM, m)], inc=VV)
        m = ph.op("tensor",
                  lambda tt: tt.matmul(pbk[5][:], ones_r[:], mrow[:],
                                       start=True, stop=True),
                  waits=[(VV, v)], inc=MM)
        return ph.op("scalar",
                     lambda ss: ss.activation(st(S_A), pbk[5][:], AF.Identity),
                     waits=[(MM, m)], inc=SS)

    def f1_head(t):
        cs = slice(t * 512, (t + 1) * 512)
        xbuf = x1t if t % 2 == 0 else x1t2

        def x1tc_f1(k, xbuf=xbuf):
            return xbuf[:, k * 512:(k + 1) * 512]

        wld = [(STO, spill_sto[t])]
        if t >= 2:
            wld.append((VV, x1_lastuse[t - 2]))
        else:
            wld.append((VV, VV.n))
        l = ph.op("sync",
                  lambda sy, cs=cs, xbuf=xbuf: sy.dma_start(
                      out=xbuf[:],
                      in_=x1s.rearrange("p (k t) -> p k t", k=DC)[:, :, cs]),
                  waits=wld, inc=LD, k=16)

        def xbf(ss, x1tc=x1tc_f1):
            last = None
            for k in range(DC):
                last = ss.copy(x1bb_f1[:, k * 512:(k + 1) * 512], x1tc(k))
            return last
        wxb = [(LD, l)]
        if t == 0:
            wxb.append((MM, MM.n))
        sxb = ph.op("scalar", xbf, waits=wxb, inc=SS)
        emit_stats(ph, lambda k: x1bb_f1[:, k * 512:(k + 1) * 512],
                   lambda k, x1tc=x1tc_f1: x1tc(k), sq8_f1,
                   sq_waits=(None if t == 0 else []),
                   summ_waits=(None if t == 0 else [(SS, sxb)]))

        scen = []
        for k in range(DC):
            sl = scrH[:, (k % 2) * 512:((k % 2) + 1) * 512]
            slb = scrb[:, (k % 2) * 512:((k % 2) + 1) * 512]
            w = [(SS, scen[k - 2])] if k >= 2 else [(VV, VV.n)]
            v = ph.op("vector",
                      lambda vv, k=k, sl=sl, x1tc=x1tc_f1:
                      vv.tensor_add(sl, x1tc(k), st(S_NEGM)),
                      waits=w, inc=VV)
            scen.append(ph.op("scalar",
                              lambda ss, sl=sl, slb=slb:
                              ss.activation(slb, sl, AF.Square),
                              waits=[(VV, v)], inc=SS))
            if k == 0:
                ph.op("vector", lambda vv: vv.tensor_copy(rmax[:], scrb[:, 0:512]),
                      waits=[(SS, scen[0])], inc=VV)
            else:
                ph.op("vector",
                      lambda vv, slb=slb: vv.tensor_max(rmax[:], rmax[:], slb),
                      waits=[(SS, scen[k])], inc=VV)

        s = amax_chain(ph)

        def soup2(vv):
            vv.tensor_mul(st(S_SCR), st(S_VAR), st(S_RSTD))
            return vv.tensor_mul(st(S_SCR), st(S_SCR), st(S_RSTD))
        v = ph.op("vector", soup2, waits=[(SS, s)], inc=VV)

        def invr1f(ss, t=t):
            ss.activation(sb(t), st(S_SCR), AF.Ln, bias=epspp[:, 0:1])
            return ss.activation(sb(t), sb(t), AF.Exp, scale=-0.5)
        s = ph.op("scalar", invr1f, waits=[(VV, v)], inc=SS)

        def soup3a(vv, t=t):
            vv.tensor_mul(st(S_A), st(S_A), st(S_RSTD))
            vv.tensor_mul(st(S_A), st(S_A), st(S_RSTD))
            vv.tensor_mul(st(S_A), st(S_A), sb(t))
            vv.tensor_mul(st(S_A), st(S_A), sb(t))
            return vv.tensor_scalar_max(st(S_A), st(S_A), 1e-10)
        v = ph.op("vector", soup3a, waits=[(SS, s)], inc=VV)

        def soup3s(ss):
            ss.activation(st(S_MEAN), st(S_A), AF.Ln)
            ss.activation(st(S_A), st(S_MEAN), AF.Exp, scale=0.5)
            return ss.activation(st(S_MEAN), st(S_MEAN), AF.Exp, scale=-0.5)
        s = ph.op("scalar", soup3s, waits=[(VV, v)], inc=SS)

        w3b = [(SS, s)]
        if t >= 2:
            w3b.append((STO, st_f1["c2sto"][t - 2]))

        def soup3b(vv, t=t):
            vv.tensor_scalar(sc(t), st(S_A), mu1pp[:, 0:1], 1.0 / 127.0,
                             ALU.mult, ALU.mult)
            vv.tensor_mul(sb(t), sb(t), st(S_MEAN))
            vv.tensor_scalar_mul(sb(t), sb(t), 127.0)
            vv.tensor_mul(sb(t), sb(t), st(S_RSTD))
            return vv.tensor_mul(st(S_SCR), st(S_NEGM), sb(t))
        v = ph.op("vector", soup3b, waits=w3b, inc=VV)

        for k in range(DC):
            def q1f(vv, k=k, t=t, x1tc=x1tc_f1):
                vv.tensor_mul(scrH[:, 0:512], x1tc(k), sb(t))
                vv.tensor_tensor(scrHi[:, 512:1024], scrH[:, 0:512],
                                 st(S_SCR), ALU.add)
                return vv.tensor_mul(q1bc(k), scrHi[:, 512:1024], sc(t))
            v = ph.op("vector", q1f, inc=VV)
        st_f1["vq1"][t] = v
        x1_lastuse[t] = v

    def f1_gloop_post(t):
        cs = slice(t * 512, (t + 1) * 512)
        vq1 = st_f1["vq1"][t]
        sgel = []
        ssqs = []
        vmaxs = []
        sqmms = []

        def emit_sqmm(j):
            slbj = scrb[:, (j % 2) * 512:((j % 2) + 1) * 512]
            sqmms.append(ph.op(
                "tensor",
                lambda tt, slbj=slbj, j=j: tt.matmul(
                    pbk[4][:], ones128b[:], slbj,
                    start=(j == 0), stop=(j == FC - 1)),
                waits=[(SS, ssqs[j])], inc=MM))

        for fc in range(FC):
            bank = pbk[fc % 4]
            w = [(VV, vq1)]
            if fc >= 4:
                w.append((SS, ssqs[fc - 4]))

            def m1f(tt, fc=fc, bank=bank):
                last = None
                for k in range(DC):
                    last = tt.matmul(
                        bank[:], w1t[:, k * FF + fc * 128:k * FF + (fc + 1) * 128],
                        q1bc(k), start=(k == 0), stop=(k == DC - 1))
                return last
            m = ph.op("tensor", m1f, waits=w, inc=MM)
            if fc >= 2:
                emit_sqmm(fc - 2)
            sg = ph.op("scalar",
                       lambda ss, fc=fc, bank=bank: ss.activation(
                           gtc(fc), bank[:], AF.Gelu, bias=b1c[:, fc:fc + 1]),
                       waits=[(MM, m)], inc=SS)
            sgel.append(sg)
            slb = scrb[:, (fc % 2) * 512:((fc % 2) + 1) * 512]
            wsq = ([(VV, vmaxs[fc - 2]), (MM, sqmms[fc - 2])]
                   if fc >= 2 else [])
            ssq = ph.op("scalar",
                        lambda ss, fc=fc, slb=slb:
                        ss.activation(slb, gtc(fc), AF.Square),
                        waits=wsq, inc=SS)
            ssqs.append(ssq)
            if fc == 0:
                vg = ph.op("vector",
                           lambda vv: vv.tensor_copy(rmax[:], scrb[:, 0:512]),
                           waits=[(SS, ssq)], inc=VV)
            else:
                vg = ph.op("vector",
                           lambda vv, slb=slb: vv.tensor_max(rmax[:], rmax[:], slb),
                           waits=[(SS, ssq)], inc=VV)
            vmaxs.append(vg)
        emit_sqmm(FC - 2)
        emit_sqmm(FC - 1)

        v = ph.op("vector",
                  lambda vv: vv.tensor_scalar_mul(st(S_VAR), pbk[4][:], 1.0 / FF),
                  waits=[(MM, sqmms[-1])], inc=VV)

        def invr2f(ss, t=t):
            ss.activation(sb(t), st(S_VAR), AF.Ln, bias=epspp[:, 0:1])
            return ss.activation(sb(t), sb(t), AF.Exp, scale=-0.5)
        ph.op("scalar", invr2f, waits=[(VV, v)], inc=SS)
        s = amax_chain(ph)

        def soup4a(vv, t=t):
            vv.tensor_mul(st(S_A), st(S_A), sb(t))
            vv.tensor_mul(st(S_A), st(S_A), sb(t))
            return vv.tensor_scalar_max(st(S_A), st(S_A), 1e-10)
        v = ph.op("vector", soup4a, waits=[(SS, s)], inc=VV)

        def soup4s(ss):
            ss.activation(st(S_MEAN), st(S_A), AF.Ln)
            ss.activation(st(S_A), st(S_MEAN), AF.Exp, scale=0.5)
            return ss.activation(st(S_MEAN), st(S_MEAN), AF.Exp, scale=-0.5)
        s = ph.op("scalar", soup4s, waits=[(VV, v)], inc=SS)

        def soup4b(vv, t=t):
            vv.tensor_scalar(sc(t), st(S_A), mu2pp[:, 0:1], 1.0 / 127.0,
                             ALU.mult, ALU.mult)
            vv.tensor_mul(sb(t), sb(t), st(S_MEAN))
            return vv.tensor_scalar_mul(sb(t), sb(t), 127.0)
        v = ph.op("vector", soup4b, waits=[(SS, s)], inc=VV)
        c2sto = ph.op("sync",
                      lambda sy, cs=cs, t=t: sy.dma_start(
                          out=c2row_t.ap()[0:1, cs], in_=sc(t)[0:1, :]),
                      waits=[(VV, v)], inc=STO, k=16)
        st_f1["c2sto"][t] = c2sto
        q2_sto.append(c2sto)

    def f1_tail(t):
        cs = slice(t * 512, (t + 1) * 512)
        gdma = st_f1["gdma"]
        if gdma is None:
            gdma = st_f1["c2sto"][0]
        all_packs = []
        for fcg in range(4):
            packs = []
            for jj in range(8):
                fc = fcg * 8 + jj
                si = scri_f1[:, (fc % 2) * 512:((fc % 2) + 1) * 512]
                wm = [(SS, all_packs[fc - 2])] if fc >= 2 else []
                vm = ph.op("vector",
                           lambda vv, fc=fc, si=si, t=t:
                           vv.tensor_mul(si, gtc(fc), sb(t)),
                           waits=wm, inc=VV)
                wp = [(VV, vm)]
                if jj == 0:
                    wp.append((STO, gdma))
                packs.append(ph.op("scalar",
                                   lambda ss, jj=jj, si=si: ss.copy(
                                       q2st[:, jj * 512:(jj + 1) * 512], si),
                                   waits=wp, inc=SS))
                all_packs.append(packs[-1])
            gdma = ph.op("sync",
                         lambda sy, fcg=fcg, cs=cs: sy.dma_start(
                             out=bass.AP(q2s_t, 8 * fcg * T + cs.start,
                                         [[FC * T, 128], [T, 8], [1, 512]]),
                             in_=q2st.ap().rearrange("p (c t) -> p c t", c=8)),
                         waits=[(SS, packs[-1])], inc=STO, k=16)
            q2_sto.append(gdma)
        st_f1["gdma"] = gdma

    f1_head(0)
    for t in range(NTT):
        f1_gloop_post(t)
        if t + 1 < NTT:
            f1_head(t + 1)
        f1_tail(t)
    ph.emit()

    if dbg:
        ph = Phase(nc)
        ph.op("sync", lambda sy: sy.dma_start(out=gdump, in_=g_t[:]),
              waits=[(VV, VV.n), (SS, SS.n), (PP, PP.n)], inc=STO, k=16)
        ph.op("sync", lambda sy: sy.dma_start(out=q1dump, in_=q1b[:]),
              inc=STO, k=16)
        ph.op("sync", lambda sy: sy.dma_start(out=wqdump[:, 0:8], in_=acc8[:]),
              inc=STO, k=16)
        ph.op("sync", lambda sy: sy.dma_start(out=wqdump[:, 8:12], in_=mcol[:]),
              inc=STO, k=16)
        ph.op("sync", lambda sy: sy.dma_start(out=wqdump[:, 12:13], in_=mu1pp[:], allow_slow_non_contiguous=True),
              inc=STO, k=16)
        ph.op("sync", lambda sy: sy.dma_start(out=wqdump[:, 13:14], in_=mu2pp[:], allow_slow_non_contiguous=True),
              inc=STO, k=16)
        ph.op("sync", lambda sy: sy.dma_start(out=wqdump[:, 14:15], in_=invmu1[:], allow_slow_non_contiguous=True),
              inc=STO, k=16)
        ph.op("sync", lambda sy: sy.dma_start(out=wqdump[:, 15:16], in_=invmu2[:], allow_slow_non_contiguous=True),
              inc=STO, k=16)
        ph.op("sync", lambda sy: sy.dma_start(out=w1tdump, in_=w1t[:]),
              inc=STO, k=16)
        ph.op("sync", lambda sy: sy.dma_start(out=statdump, in_=stat[:]),
              inc=STO, k=16)
        ph.emit()
    if stop_after == "f1":
        return nc, _emit_finish(nc, STO)

    emit_wq_pass2(w2t_m, lambda ch: w2t[:, ch * 4096:(ch + 1) * 4096], invmu2)
    q2_done = STO.n

    ph = Phase(nc, "f2")

    def o2v(t, k):
        buf = o2stA if t % 2 == 0 else o2stB
        return buf[:, k * 512:(k + 1) * 512]

    ctxs = []
    prev_mm_last = None
    prev_sev_last = None
    prev_vr = None
    prev_outdma = None
    prev_evac_last = None

    def emit_head(tt_i):
        nonlocal prev_mm_last, prev_sev_last, prev_evac_last
        cs = slice(tt_i * 512, (tt_i + 1) * 512)
        wq2 = [(MM, prev_mm_last)] if prev_mm_last else [(STO, q2_done)]
        l = ph.op("gpsimd",
                  lambda sy, cs=cs: sy.dma_start(
                      out=q2in[:].rearrange("p (c t) -> p c t", c=FC),
                      in_=bass.AP(q2s_t, cs.start, [[FC * T, 128], [T, FC], [1, 512]])),
                  waits=wq2, inc=LDG, k=16)
        wc2 = [(VV, prev_evac_last)] if prev_evac_last else [(STO, q2_done)]
        l3 = ph.op("sync",
                   lambda sy, cs=cs: sy.dma_start(
                       out=st(S_C),
                       in_=bass.AP(c2row_t, cs.start, [[0, 128], [1, 512]])),
                   waits=wc2, inc=LD, k=16)
        sev = []
        for dc in range(DC):
            bank = pbk[dc % 3]
            w = [(LDG, l)]
            if dc >= 3:
                w.append((SS, sev[dc - 3]))
            elif prev_sev_last is not None:
                w.append((SS, prev_sev_last))

            def m2f(tt, dc=dc, bank=bank):
                last = None
                for fc in range(FC):
                    last = tt.matmul(
                        bank[:], w2t[:, fc * D + dc * 128:fc * D + (dc + 1) * 128],
                        q2inc(fc), start=(fc == 0), stop=(fc == FC - 1))
                return last
            m = ph.op("tensor", m2f, waits=w, inc=MM)
            if dc == DC - 1:
                prev_mm_last = m
            slo = scr1_f2[:, (dc % 2) * 512:((dc % 2) + 1) * 512]
            wv = [(MM, m), (LD, l3)]
            if dc >= 2:
                wv.append((SS, sev[dc - 2]))
            v = ph.op("vector",
                      lambda vv, bank=bank, slo=slo: vv.tensor_mul(slo, bank[:],
                                                                   st(S_C)),
                      waits=wv, inc=VV)
            if dc == DC - 1:
                prev_evac_last = v
            sev.append(ph.op("scalar",
                             lambda ss, dc=dc, tt_i=tt_i, slo=slo: ss.activation(
                                 o2v(tt_i, dc), slo,
                                 AF.Identity, bias=b2c[:, dc:dc + 1]),
                             waits=[(VV, v)], inc=SS))
        prev_sev_last = sev[-1]
        ctxs.append((tt_i, cs, sev))

    def emit_tail():
        nonlocal prev_vr, prev_outdma
        tt_i, cs, sev = ctxs.pop(0)
        wx1 = [(VV, prev_vr)] if prev_vr else [(STO, q2_done)]
        l2 = ph.op("sync",
                   lambda sy, cs=cs: sy.dma_start(
                       out=x1t[:],
                       in_=x1s.rearrange("p (k t) -> p k t", k=DC)[:, :, cs]),
                   waits=wx1, inc=LD, k=16)
        sstat3, _ = emit_stats(ph, lambda k: o2v(tt_i, k),
                               lambda k: o2v(tt_i, k), sq8_f2)
        vr = None
        for k in range(DC):
            wap = [(LD, l2), (VV, VV.n), (SS, sstat3)]
            if k == 0 and prev_outdma is not None:
                wap.append((STO, prev_outdma))

            def ap3(vv, k=k, tt_i=tt_i):
                vv.tensor_add(scr1_f2[:, 0:512], o2v(tt_i, k), st(S_NEGM))
                vv.tensor_mul(scr1_f2[:, 0:512], scr1_f2[:, 0:512], st(S_RSTD))
                return vv.tensor_add(outst[:, k * 512:(k + 1) * 512],
                                     scr1_f2[:, 0:512], x1tc(k))
            vr = ph.op("vector", ap3, waits=wap, inc=VV)
        prev_vr = vr
        prev_outdma = ph.op("sync",
                            lambda sy, cs=cs: sy.dma_start(
                                out=out_m.rearrange("p (k t) -> p k t", k=DC)[:, :, cs],
                                in_=outst.ap().rearrange("p (k t) -> p k t", k=DC)),
                            waits=[(VV, vr)], inc=STO, k=16)

    for tt_i in range(NTT):
        emit_head(tt_i)
        if tt_i >= 1:
            emit_tail()
    emit_tail()
    ph.emit()
    return nc, _emit_finish(nc, STO)


def _emit_finish(nc, STO):
    with nc.Block() as blk:
        @blk.sync
        def _(sy):
            sy.wait_ge(STO.h, STO.n)
    return STO.n


def _tile_rows(a, p=128):
    a = np.ascontiguousarray(a)
    R, C = a.shape
    return np.ascontiguousarray(
        a.reshape(R // p, p, C).transpose(1, 0, 2).reshape(p, (R // p) * C))


def make_inputs(x, A, B, C, gam, bet, w1, b1, w2, b2):
    import ml_dtypes
    shared = {
        "at_m": _tile_rows(A.T),
        "a_m": _tile_rows(A),
        "bt_m": _tile_rows(B.T),
        "ct_m": _tile_rows(C.T),
        "gam_m": _tile_rows(gam.reshape(D, 1)),
        "bet_m": _tile_rows(bet.reshape(D, 1)),
        "w1t_m": _tile_rows(w1.T),
        "b1_m": _tile_rows(b1.reshape(FF, 1)),
        "w2t_m": _tile_rows(w2.T),
        "b2_m": _tile_rows(b2.reshape(D, 1)),
        "idf_m": np.eye(128, dtype=np.float32),
        "idb_m": np.eye(128, dtype=ml_dtypes.bfloat16),
    }
    maps = []
    for b in range(x.shape[0]):
        m = dict(shared)
        m["x_f"] = _tile_rows(x[b].T)
        maps.append(m)
    return maps


def kernel(L0_attn_x, L0_attn_A, L0_attn_B, L0_attn_C, L0_attn_gamma,
           L0_attn_beta, L0_ffn_w1, L0_ffn_b1, L0_ffn_w2, L0_ffn_b2):
    x = np.asarray(L0_attn_x, dtype=np.float32)
    in_maps = make_inputs(x,
                          np.asarray(L0_attn_A, np.float32),
                          np.asarray(L0_attn_B, np.float32),
                          np.asarray(L0_attn_C, np.float32),
                          np.asarray(L0_attn_gamma, np.float32),
                          np.asarray(L0_attn_beta, np.float32),
                          np.asarray(L0_ffn_w1, np.float32),
                          np.asarray(L0_ffn_b1, np.float32),
                          np.asarray(L0_ffn_w2, np.float32),
                          np.asarray(L0_ffn_b2, np.float32))
    nc, _ = build_program()
    res = run_bass_kernel_spmd(nc, in_maps, core_ids=list(range(8)))
    outs = []
    for b in range(8):
        o = res.results[b]["out"]
        o = o.reshape(128, DC, T).transpose(1, 0, 2).reshape(D, T)
        outs.append(np.ascontiguousarray(o.T))
    return np.stack(outs, axis=0)
